# version 08
# baseline (speedup 1.0000x reference)
import contextlib
import numpy as np
import ml_dtypes
import concourse.bass as bass
import concourse.mybir as mybir
from concourse.bass_utils import run_bass_kernel_spmd

F32 = mybir.dt.float32
BF16 = mybir.dt.bfloat16
I32 = mybir.dt.int32
U8 = mybir.dt.uint8
AX = mybir.AxisListType
ALU = mybir.AluOpType
ACTF = mybir.ActivationFunctionType

T = 4096
D = 2048
NT = T // 128
DC = D // 128
H = 128
NQH, NKH = 16, 8
ATTN_SCALE = 0.08838834764831845
WINDOW = 1024
HID0 = 5632
HID1 = 8192
SEG = 512
NSEG = T // SEG
NH1 = 16
EPS_LN = 1e-5
EPS_RMS = 1e-6
NEG = -1.0e30
SERIAL = False

ENG_NAME = {}


def _engname(eng):
    return ENG_NAME[id(eng)]


class Bld:

    def __init__(self, nc, es):
        self.nc = nc
        self.es = es
        self.sem = {}
        self.cnt = {}
        self.lastwait = {}
        for ch in ("pe", "dv", "ac", "gp"):
            self._mk(ch)
        ENG_NAME[id(nc.tensor)] = "pe"
        ENG_NAME[id(nc.vector)] = "dv"
        ENG_NAME[id(nc.scalar)] = "ac"
        ENG_NAME[id(nc.gpsimd)] = "gp"
        ENG_NAME[id(nc.sync)] = "sp"

    def _mk(self, ch):
        self.sem[ch] = self.es.enter_context(self.nc.semaphore(f"ch_{ch}"))
        self.cnt[ch] = 0

    def inc(self, ch, instr, k=1):
        if ch not in self.sem:
            self._mk(ch)
        instr.then_inc(self.sem[ch], k)
        self.cnt[ch] += k
        return self.cnt[ch]

    def wait(self, eng, ch, val=None):
        if ch not in self.cnt:
            return
        if val is None:
            val = self.cnt[ch]
        if val <= 0:
            return
        key = (_engname(eng), ch)
        if self.lastwait.get(key, -1) >= val:
            return
        eng.wait_ge(self.sem[ch], val)
        self.lastwait[key] = val

    def hard(self, eng):
        if SERIAL:
            for ch in ("pe", "dv", "ac"):
                self.wait(eng, ch)

    def barrier(self, block):
        for sec in (block.tensor, block.vector, block.scalar, block.gpsimd, block.sync):
            @sec
            def _(eng):
                for ch in sorted(self.cnt):
                    self.wait(eng, ch)


class Arena:

    def __init__(self, nc, nbytes):
        self.t_u8 = nc.alloc_sbuf_tensor("arena", [128, nbytes], U8)
        self.views = {
            F32: self.t_u8.bitcast(F32),
            BF16: self.t_u8.bitcast(BF16),
            I32: self.t_u8.bitcast(I32),
        }
        self.nbytes = nbytes
        self.off = 0
        self.marks = []

    def alloc(self, shape, dtype):
        parts, free = shape
        esz = {F32: 4, BF16: 2, I32: 4}[dtype]
        off = (self.off + 31) // 32 * 32
        self.off = off + free * esz
        assert self.off <= self.nbytes, f"SBUF arena overflow: {self.off} > {self.nbytes}"
        v = self.views[dtype]
        ap = v[0:parts, off // esz: off // esz + free]
        return ap

    def push(self):
        self.marks.append(self.off)

    def pop(self):
        self.off = self.marks.pop()

    def union(self, *shapes):
        base = (self.off + 31) // 32 * 32
        aps = []
        end = base
        for (parts, free), dtype in shapes:
            esz = {F32: 4, BF16: 2, I32: 4}[dtype]
            v = self.views[dtype]
            aps.append(v[0:parts, base // esz: base // esz + free])
            end = max(end, base + free * esz)
        self.off = end
        assert self.off <= self.nbytes, "SBUF arena overflow (union)"
        return aps



def host_consts():
    c = {}
    c["cid_f"] = np.eye(128, dtype=np.float32)
    c["cid_b"] = np.eye(128, dtype=ml_dtypes.bfloat16)
    pr = np.zeros((128, 128), np.float32)
    for k in range(128):
        pr[k, (k - 64) % 128] = 1.0
    c["cprot"] = pr
    masks = np.full((12, 128, 512), NEG, np.float32)
    tt = np.arange(128)[:, None]
    ss = np.arange(512)[None, :]
    for idx in range(12):
        dd = tt - ss + 128 * idx
        masks[idx][(dd >= 0) & (dd < WINDOW)] = 0.0
    c["cmask"] = masks.astype(ml_dtypes.bfloat16)
    frac = np.arange(0, H, 2, dtype=np.float64) / H
    timescale = 10000.0 ** frac
    pos = np.arange(T, dtype=np.float64)
    sinusoid = pos[None, :] / timescale[:, None]
    cosF = np.cos(sinusoid)
    sinF = np.sin(sinusoid)
    cos128 = np.concatenate([cosF, cosF], 0)
    sin128 = np.concatenate([-sinF, sinF], 0)
    cos4 = np.empty((128, NT * 512), np.float32)
    sin4 = np.empty((128, NT * 512), np.float32)
    for i in range(NT):
        ct = cos128[:, i * 128:(i + 1) * 128]
        st = sin128[:, i * 128:(i + 1) * 128]
        cos4[:, i * 512:(i + 1) * 512] = np.tile(ct, (1, 4))
        sin4[:, i * 512:(i + 1) * 512] = np.tile(st, (1, 4))
    c["ccos4"] = cos4
    c["csin4"] = sin4
    memz0 = np.zeros((128, NH1 * 129), np.float32)
    for n in range(NH1):
        memz0[:, n * 129 + 128] = 1.0 / 128.0
    c["cmemz0"] = memz0
    cpool = np.zeros((128, 8), np.float32)
    cpool[:, 0] = EPS_LN
    cpool[:, 1] = EPS_RMS
    cpool[:, 2] = 1.0
    c["ccpool"] = cpool
    return c


def host_prepare(inputs, core):
    f32 = np.float32
    d = {}
    d["x0"] = np.ascontiguousarray(inputs["L0_attn_x"][core]).astype(f32, copy=False)
    d["w_q0"] = np.ascontiguousarray(
        inputs["L0_attn_q_kernel"].transpose(1, 0, 2).reshape(D, NQH * H))
    d["w_k0"] = np.ascontiguousarray(
        inputs["L0_attn_kv_kernel"][0].transpose(1, 0, 2).reshape(D, NKH * H))
    d["w_v0"] = np.ascontiguousarray(
        inputs["L0_attn_kv_kernel"][1].transpose(1, 0, 2).reshape(D, NKH * H))
    d["w_o0"] = np.ascontiguousarray(inputs["L0_attn_out_kernel"].reshape(NQH * H, D))
    d["qns"] = np.ascontiguousarray(inputs["L0_attn_q_norm_scale"].reshape(H, 1))
    d["kns"] = np.ascontiguousarray(inputs["L0_attn_k_norm_scale"].reshape(H, 1))
    d["g0w"] = np.ascontiguousarray(inputs["L0_ffn_wq_gate"])
    d["g0s"] = np.ascontiguousarray(inputs["L0_ffn_s_gate"].reshape(HID0 // 128, 128).T)
    d["u0w"] = np.ascontiguousarray(inputs["L0_ffn_wq_up"])
    d["u0s"] = np.ascontiguousarray(inputs["L0_ffn_s_up"].reshape(HID0 // 128, 128).T)
    d["d0w"] = np.ascontiguousarray(inputs["L0_ffn_wq_down"])
    d["d0s"] = np.ascontiguousarray(inputs["L0_ffn_s_down"].reshape(DC, 128).T)
    d["w_q1"] = np.ascontiguousarray(inputs["L1_attn_Wq"])
    d["w_k1"] = np.ascontiguousarray(inputs["L1_attn_Wk"])
    d["w_v1"] = np.ascontiguousarray(inputs["L1_attn_Wv"])
    d["w_o1"] = np.ascontiguousarray(inputs["L1_attn_Wo"])
    d["betasT"] = np.ascontiguousarray(inputs["L1_attn_betas"][0, :, 0, :].T)
    d["g1w"] = np.ascontiguousarray(inputs["L1_ffn_w_gate"])
    d["u1w"] = np.ascontiguousarray(inputs["L1_ffn_w_up"])
    d["d1w"] = np.ascontiguousarray(inputs["L1_ffn_w_down"])
    return d



class Prog:
    def __init__(self, nc, es, taps):
        self.nc = nc
        self.es = es
        self.taps = taps or {}
        self.b = Bld(nc, es)
        dt = nc.dram_tensor
        self.x0 = dt("x0", [T, D], F32, kind="ExternalInput").ap()
        self.w_q0 = dt("w_q0", [D, NQH * H], F32, kind="ExternalInput").ap()
        self.w_k0 = dt("w_k0", [D, NKH * H], F32, kind="ExternalInput").ap()
        self.w_v0 = dt("w_v0", [D, NKH * H], F32, kind="ExternalInput").ap()
        self.w_o0 = dt("w_o0", [NQH * H, D], F32, kind="ExternalInput").ap()
        self.qns = dt("qns", [H, 1], F32, kind="ExternalInput").ap()
        self.kns = dt("kns", [H, 1], F32, kind="ExternalInput").ap()
        self.g0w = dt("g0w", [HID0, D], I32, kind="ExternalInput").ap()
        self.g0s = dt("g0s", [128, HID0 // 128], F32, kind="ExternalInput").ap()
        self.u0w = dt("u0w", [HID0, D], I32, kind="ExternalInput").ap()
        self.u0s = dt("u0s", [128, HID0 // 128], F32, kind="ExternalInput").ap()
        self.d0w = dt("d0w", [D, HID0], I32, kind="ExternalInput").ap()
        self.d0s = dt("d0s", [128, DC], F32, kind="ExternalInput").ap()
        self.w_q1 = dt("w_q1", [D, D], F32, kind="ExternalInput").ap()
        self.w_k1 = dt("w_k1", [D, D], F32, kind="ExternalInput").ap()
        self.w_v1 = dt("w_v1", [D, D], F32, kind="ExternalInput").ap()
        self.w_o1 = dt("w_o1", [D, D], F32, kind="ExternalInput").ap()
        self.betasT = dt("betasT", [H, NH1], F32, kind="ExternalInput").ap()
        self.g1w = dt("g1w", [D, HID1], F32, kind="ExternalInput").ap()
        self.u1w = dt("u1w", [D, HID1], F32, kind="ExternalInput").ap()
        self.d1w = dt("d1w", [HID1, D], F32, kind="ExternalInput").ap()
        self.cid_f = dt("cid_f", [128, 128], F32, kind="ExternalInput").ap()
        self.cid_b = dt("cid_b", [128, 128], BF16, kind="ExternalInput").ap()
        self.cprot = dt("cprot", [128, 128], F32, kind="ExternalInput").ap()
        self.cmask = dt("cmask", [12, 128, 512], BF16, kind="ExternalInput").ap()
        self.ccos4 = dt("ccos4", [128, NT * 512], F32, kind="ExternalInput").ap()
        self.csin4 = dt("csin4", [128, NT * 512], F32, kind="ExternalInput").ap()
        self.cmemz0 = dt("cmemz0", [128, NH1 * 129], F32, kind="ExternalInput").ap()
        self.ccpool = dt("ccpool", [128, 8], F32, kind="ExternalInput").ap()
        self.y = dt("y", [T, D], F32, kind="ExternalOutput").ap()
        self.s_xA = dt("s_xA", [T, D], F32, kind="Internal").ap()
        self.s_xB = dt("s_xB", [T, D], F32, kind="Internal").ap()
        self.s_xhatT = dt("s_xhatT", [DC, 128, T], BF16, kind="Internal").ap()
        self.s_qT0 = dt("s_qT0", [NQH, 128, T], BF16, kind="Internal").ap()
        self.s_kT0 = dt("s_kT0", [NKH, 128, T], BF16, kind="Internal").ap()
        self.s_v0 = dt("s_v0", [T, NKH * H], BF16, kind="Internal").ap()
        self.s_qT1 = dt("s_qT1", [NH1, 128, T], F32, kind="Internal").ap()
        self.s_kT1 = dt("s_kT1", [NH1, 128, T], F32, kind="Internal").ap()
        self.s_v1 = dt("s_v1", [T, D], F32, kind="Internal").ap()
        self.arena = Arena(nc, 188 * 1024)
        self.id_f = self.arena.alloc([128, 128], F32)
        self.id_b = self.arena.alloc([128, 128], BF16)
        self.prot = self.arena.alloc([128, 128], F32)
        self.cpool_s = self.arena.alloc([128, 8], F32)
        for ci, cv in enumerate([EPS_LN, EPS_RMS, 1.0]):
            nc.const_aps.register(F32, cv, self.cpool_s[:, ci:ci + 1])
        self.pb = [nc.alloc_psum_tensor(f"pb{i}", [128, 512], F32) for i in range(8)]
        self.pbb = [p.bitcast(BF16) for p in self.pb]

    def tap(self, name, shape, dtype):
        if name in self.taps:
            return self.nc.dram_tensor("tap_" + name, shape, dtype,
                                       kind="ExternalOutput").ap()
        return None


def emit_prologue(p, block):
    b = p.b

    @block.sync
    def _(sp):
        b.inc("LC", sp.dma_start(out=p.id_f, in_=p.cid_f), 16)
        b.inc("LC", sp.dma_start(out=p.id_b, in_=p.cid_b), 16)
        b.inc("LC", sp.dma_start(out=p.prot, in_=p.cprot), 16)
        b.inc("LC", sp.dma_start(out=p.cpool_s, in_=p.ccpool), 16)


def emit_ln_xhatT(p, block, x_ap, xhat_b, xhatT_b, scr, tr_banks, out_f32=None,
                  expln=False):
    nc, b = p.nc, p.b

    @block.vector
    def _(dv):
        dv.tensor_reduce(out=scr["sum"], in_=x_ap, axis=AX.X, op=ALU.add)
        dv.scalar_tensor_tensor(out=scr["big"], in0=x_ap, scalar=1.0, in1=x_ap,
                                op0=ALU.bypass, op1=ALU.mult, accum_out=scr["ssq"])
        dv.drain()
        dv.tensor_scalar(out=scr["negmean"], in0=scr["sum"], scalar1=-1.0 / D,
                         scalar2=None, op0=ALU.mult)
        dv.drain()
        dv.tensor_tensor(out=scr["m2"], in0=scr["negmean"], in1=scr["negmean"],
                         op=ALU.mult)
        dv.tensor_scalar(out=scr["var"], in0=scr["ssq"], scalar1=1.0 / D,
                         scalar2=None, op0=ALU.mult)
        dv.drain()
        b.inc("dv", dv.tensor_tensor(out=scr["var"], in0=scr["var"], in1=scr["m2"],
                                     op=ALU.subtract))

    if expln:
        @block.scalar
        def _(ac):
            b.wait(ac, "dv")
            ac.activation(out=scr["std"], in_=scr["var"], func=ACTF.Ln,
                          bias=EPS_LN, scale=1.0)
            ac.drain()
            b.inc("ac", ac.activation(out=scr["rstd"], in_=scr["std"],
                                      func=ACTF.Exp, scale=-0.5))

        @block.vector
        def _(dv):
            b.wait(dv, "ac")
            b.inc("dv", dv.tensor_tensor(out=scr["lnb"], in0=scr["negmean"],
                                         in1=scr["rstd"], op=ALU.mult))
    else:
        @block.scalar
        def _(ac):
            b.wait(ac, "dv")
            b.inc("ac", ac.activation(out=scr["std"], in_=scr["var"], func=ACTF.Sqrt,
                                      bias=EPS_LN, scale=1.0))

        @block.vector
        def _(dv):
            b.wait(dv, "ac")
            b.inc("dv", dv.reciprocal(out=scr["rstd"], in_=scr["std"]))
            dv.drain()
            b.inc("dv", dv.tensor_tensor(out=scr["lnb"], in0=scr["negmean"],
                                         in1=scr["rstd"], op=ALU.mult))

    @block.scalar
    def _(ac):
        b.wait(ac, "dv")
        b.inc("ac", ac.activation(out=xhat_b, in_=x_ap, func=ACTF.Identity,
                                  bias=scr["lnb"], scale=scr["rstd"]))
        if out_f32 is not None:
            b.inc("ac", ac.activation(out=out_f32, in_=x_ap, func=ACTF.Identity,
                                      bias=scr["lnb"], scale=scr["rstd"]))
    tok_xhat = b.cnt["ac"]

    for g in range(4):
        bank = tr_banks[g % 2]
        pbank = p.pbb[bank]

        @block.tensor
        def _(pe, g=g, bank=bank, pbank=pbank):
            b.wait(pe, "ac", tok_xhat)
            if g >= 2:
                b.wait(pe, "ac")
            for j in range(4):
                c = 4 * g + j
                instr = pe.transpose(out=pbank[:, j * 128:(j + 1) * 128],
                                     in_=xhat_b[:, c * 128:(c + 1) * 128],
                                     identity=p.id_b)
            b.inc("pe", instr)

        @block.scalar
        def _(ac, g=g, pbank=pbank):
            b.wait(ac, "pe")
            b.inc("ac", ac.activation(out=xhatT_b[:, g * 512:(g + 1) * 512],
                                      in_=pbank[:, 0:512], func=ACTF.Copy))


def ln_scratch(p):
    scr = {}
    scr["big"] = p.arena.alloc([128, D], F32)
    for k in ("sum", "ssq", "negmean", "m2", "var", "std", "rstd", "lnb"):
        scr[k] = p.arena.alloc([128, 1], F32)
    return scr



def emit_p0a(p, block):
    nc, b, ar = p.nc, p.b, p.arena
    ar.push()
    FQ = NQH * H
    FK = NKH * H
    wq_s = ar.alloc([128, DC * FQ], BF16)
    wk_s = ar.alloc([128, DC * FK], BF16)
    wv_s = ar.alloc([128, DC * FK], BF16)
    qns_s = ar.alloc([128, 1], F32)
    kns_s = ar.alloc([128, 1], F32)
    qsc_s = ar.alloc([128, 1], F32)
    xf = [ar.alloc([128, D], F32) for _ in range(2)]
    rc_s = [ar.alloc([128, 512], F32) for _ in range(2)]
    rs_s = [ar.alloc([128, 512], F32) for _ in range(2)]
    xhat_b = ar.alloc([128, D], BF16)
    xhatT_b = ar.alloc([128, D], BF16)
    scr = ln_scratch(p)
    qn_f = scr["big"]
    kn_f = ar.alloc([128, FK], F32)
    v_b = ar.alloc([128, FK], BF16)
    ssq_q = ar.alloc([128, NQH], F32)
    ssq_k = ar.alloc([128, NKH], F32)
    rq_s = ar.alloc([128, NQH], F32)
    rk_s = ar.alloc([128, NKH], F32)
    qTf = ar.alloc([128, 512], F32)
    qTrot = ar.alloc([128, 512], F32)
    r1 = ar.alloc([128, 512], F32)
    r2 = ar.alloc([128, 512], F32)
    qT_out = [ar.alloc([128, 512], BF16) for _ in range(2)]

    tap_qT = p.tap("qT0", [NQH, 128, T], BF16)
    tap_v = p.tap("v0", [T, NKH * H], BF16)
    tap_xhatT = p.tap("xhatT_p0", [DC, 128, T], BF16)

    @block.sync
    def _(sp):
        b.inc("LC", sp.dma_start(out=qns_s, in_=p.qns), 16)
        b.inc("LC", sp.dma_start(out=kns_s, in_=p.kns), 16)
    tok_ns = b.cnt["LC"]

    @block.gpsimd
    def _(gp):
        for dk in range(DC):
            b.inc("LW", gp.dma_start(out=wq_s[:, dk * FQ:(dk + 1) * FQ],
                                     in_=p.w_q0[dk * 128:(dk + 1) * 128, :]), 16)
            b.inc("LW", gp.dma_start(out=wk_s[:, dk * FK:(dk + 1) * FK],
                                     in_=p.w_k0[dk * 128:(dk + 1) * 128, :]), 16)
            b.inc("LW", gp.dma_start(out=wv_s[:, dk * FK:(dk + 1) * FK],
                                     in_=p.w_v0[dk * 128:(dk + 1) * 128, :]), 16)
    tok_w = b.cnt["LW"]

    @block.vector
    def _(dv):
        b.wait(dv, "LC", tok_ns)
        b.inc("dv", dv.tensor_scalar(out=qsc_s, in0=qns_s, scalar1=ATTN_SCALE,
                                     scalar2=None, op0=ALU.mult))

    @block.sync
    def _(sp):
        b.inc("LT0", sp.dma_start(out=xf[0], in_=p.x0[0:128, :]), 16)
        b.inc("LT0", sp.dma_start(out=rc_s[0], in_=p.ccos4[:, 0:512]), 16)
        b.inc("LT0", sp.dma_start(out=rs_s[0], in_=p.csin4[:, 0:512]), 16)

    lt_tok = [b.cnt["LT0"], 0]
    cons_tok = [0, 0]

    for i in range(NT):
        cur, nxt = i % 2, (i + 1) % 2
        if i + 1 < NT:
            @block.sync
            def _(sp, i=i, nxt=nxt):
                ch = f"LT{nxt}"
                if i >= 1:
                    b.wait(sp, "dv", cons_tok[nxt])
                b.inc(ch, sp.dma_start(out=xf[nxt],
                                       in_=p.x0[(i + 1) * 128:(i + 2) * 128, :]), 16)
                b.inc(ch, sp.dma_start(out=rc_s[nxt],
                                       in_=p.ccos4[:, (i + 1) * 512:(i + 2) * 512]), 16)
                b.inc(ch, sp.dma_start(out=rs_s[nxt],
                                       in_=p.csin4[:, (i + 1) * 512:(i + 2) * 512]), 16)
                lt_tok[nxt] = b.cnt[ch]

        @block.vector
        def _(dv, cur=cur):
            b.wait(dv, f"LT{cur}", lt_tok[cur])
            b.wait(dv, "pe")

        emit_ln_xhatT(p, block, xf[cur], xhat_b, xhatT_b, scr, (6, 7))
        tok_xhatT = b.cnt["ac"]

        @block.tensor
        def _(pe, i=i):
            b.wait(pe, "ac", tok_xhatT)
            if i == 0:
                b.wait(pe, "LW", tok_w)
            for j in range(4):
                for dk in range(DC):
                    instr = pe.matmul(out=p.pb[j][:],
                                      lhsT=xhatT_b[:, dk * 128:(dk + 1) * 128],
                                      rhs=wq_s[:, dk * FQ + j * 512: dk * FQ + (j + 1) * 512],
                                      start=(dk == 0), stop=(dk == DC - 1))
                b.inc("pe", instr)
            for j in range(2):
                for dk in range(DC):
                    instr = pe.matmul(out=p.pb[4 + j][:],
                                      lhsT=xhatT_b[:, dk * 128:(dk + 1) * 128],
                                      rhs=wk_s[:, dk * FK + j * 512: dk * FK + (j + 1) * 512],
                                      start=(dk == 0), stop=(dk == DC - 1))
                b.inc("pe", instr)
            for j in range(2):
                for dk in range(DC):
                    instr = pe.matmul(out=p.pb[6 + j][:],
                                      lhsT=xhatT_b[:, dk * 128:(dk + 1) * 128],
                                      rhs=wv_s[:, dk * FK + j * 512: dk * FK + (j + 1) * 512],
                                      start=(dk == 0), stop=(dk == DC - 1))
                b.inc("pe", instr)
        tok_projq = b.cnt["pe"] - 4
        tok_projk = b.cnt["pe"] - 2
        tok_projv = b.cnt["pe"]

        @block.scalar
        def _(ac):
            b.wait(ac, "pe", tok_projq)
            ins = None
            for hh in range(NQH):
                bank, col = hh // 4, (hh % 4) * 128
                ins = ac.activation(
                    out=scr["big"][:, hh * 128:(hh + 1) * 128],
                    in_=p.pb[bank][:, col:col + 128], func=ACTF.Square,
                    accum_out=ssq_q[:, hh:hh + 1])
            b.inc("ac", ins)
            b.wait(ac, "pe", tok_projk)
            for hh in range(NKH):
                bank, col = 4 + hh // 4, (hh % 4) * 128
                ins = ac.activation(
                    out=kn_f[:, hh * 128:(hh + 1) * 128],
                    in_=p.pb[bank][:, col:col + 128], func=ACTF.Square,
                    accum_out=ssq_k[:, hh:hh + 1])
            b.inc("ac", ins)
        tok_ssq = b.cnt["ac"]

        @block.scalar
        def _(ac):
            b.wait(ac, "pe", tok_projv)
            b.wait(ac, "SV")
            ac.activation(out=v_b[:, 0:512], in_=p.pb[6][:], func=ACTF.Copy)
            b.inc("ac", ac.activation(out=v_b[:, 512:1024], in_=p.pb[7][:],
                                      func=ACTF.Copy))
        tok_vevac = b.cnt["ac"]

        @block.gpsimd
        def _(gp, i=i):
            b.wait(gp, "ac", tok_vevac)
            b.inc("SV", gp.dma_start(out=p.s_v0[i * 128:(i + 1) * 128, :], in_=v_b), 16)
            if tap_v is not None:
                b.inc("SV", gp.dma_start(out=tap_v[i * 128:(i + 1) * 128, :], in_=v_b), 16)

        @block.scalar
        def _(ac):
            b.wait(ac, "ac", tok_ssq)
            ac.drain()
            ac.activation(out=rq_s, in_=ssq_q, func=ACTF.Sqrt,
                          bias=EPS_RMS, scale=1.0 / H)
            b.inc("ac", ac.activation(out=rk_s, in_=ssq_k, func=ACTF.Sqrt,
                                      bias=EPS_RMS, scale=1.0 / H))

        @block.vector
        def _(dv):
            b.wait(dv, "ac")
            dv.reciprocal(out=rq_s, in_=rq_s)
            b.inc("dv", dv.reciprocal(out=rk_s, in_=rk_s))
        tok_rms = b.cnt["dv"]

        @block.scalar
        def _(ac):
            b.wait(ac, "dv", tok_rms)
            for hh in range(NQH):
                bank, col = hh // 4, (hh % 4) * 128
                instr = ac.activation(out=qn_f[:, hh * 128:(hh + 1) * 128],
                                      in_=p.pb[bank][:, col:col + 128],
                                      func=ACTF.Identity, scale=rq_s[:, hh:hh + 1])
            for hh in range(NKH):
                bank, col = 4 + hh // 4, (hh % 4) * 128
                instr = ac.activation(out=kn_f[:, hh * 128:(hh + 1) * 128],
                                      in_=p.pb[bank][:, col:col + 128],
                                      func=ACTF.Identity, scale=rk_s[:, hh:hh + 1])
            b.inc("ac", instr)
        tok_rmsevac = b.cnt["ac"]

        for g in range(6):
            is_q = g < 4
            src = qn_f if is_q else kn_f
            goff = g * 512 if is_q else (g - 4) * 512
            sc = qsc_s if is_q else kns_s
            bank_t, bank_r = (0, 1) if g % 2 == 0 else (2, 3)

            @block.tensor
            def _(pe, src=src, goff=goff, bank_t=bank_t):
                b.wait(pe, "ac", tok_rmsevac)
                b.wait(pe, "dv")
                for j in range(4):
                    instr = pe.transpose(out=p.pb[bank_t][:, j * 128:(j + 1) * 128],
                                         in_=src[:, goff + j * 128: goff + (j + 1) * 128],
                                         identity=p.id_f)
                b.inc("pe", instr)
            tok_tr = b.cnt["pe"]

            @block.scalar
            def _(ac, bank_t=bank_t, sc=sc):
                b.wait(ac, "pe", tok_tr)
                b.inc("ac", ac.activation(out=qTf, in_=p.pb[bank_t][:],
                                          func=ACTF.Identity, scale=sc))
            tok_qTf = b.cnt["ac"]

            @block.tensor
            def _(pe, bank_r=bank_r):
                b.wait(pe, "ac", tok_qTf)
                b.inc("pe", pe.matmul(out=p.pb[bank_r][:], lhsT=p.prot, rhs=qTf,
                                      start=True, stop=True))
            tok_rot = b.cnt["pe"]

            @block.scalar
            def _(ac, bank_r=bank_r):
                b.wait(ac, "pe", tok_rot)
                b.inc("ac", ac.activation(out=qTrot, in_=p.pb[bank_r][:],
                                          func=ACTF.Copy))
            tok_rot_e = b.cnt["ac"]

            gpar = g % 2

            @block.vector
            def _(dv, cur=cur, gpar=gpar):
                b.wait(dv, "ac", tok_rot_e)
                b.wait(dv, f"LT{cur}", lt_tok[cur])
                b.wait(dv, f"GQ{gpar}")
                dv.tensor_tensor(out=r1, in0=qTf, in1=rc_s[cur], op=ALU.mult)
                dv.tensor_tensor(out=r2, in0=qTrot, in1=rs_s[cur], op=ALU.mult)
                dv.drain()
                b.inc("dv", dv.tensor_tensor(out=qT_out[gpar], in0=r1, in1=r2,
                                             op=ALU.add))
            tok_rope = b.cnt["dv"]

            @block.gpsimd
            def _(gp, i=i, g=g, is_q=is_q, gpar=gpar):
                b.wait(gp, "dv", tok_rope)
                dst = p.s_qT0 if is_q else p.s_kT0
                h0 = 4 * g if is_q else 4 * (g - 4)
                b.inc(f"GQ{gpar}", gp.dma_start(
                    out=dst[h0:h0 + 4, :, i * 128:(i + 1) * 128].rearrange("h p t -> p h t"),
                    in_=qT_out[gpar].rearrange("p (h t) -> p h t", h=4)), 16)
                if is_q and tap_qT is not None:
                    b.inc(f"GQ{gpar}", gp.dma_start(
                        out=tap_qT[h0:h0 + 4, :, i * 128:(i + 1) * 128].rearrange("h p t -> p h t"),
                        in_=qT_out[gpar].rearrange("p (h t) -> p h t", h=4)), 16)
            @block.scalar
            def _(ac):
                b.wait(ac, "dv", tok_rope)

        if tap_xhatT is not None:
            @block.sync
            def _(sp, i=i):
                b.wait(sp, "ac", tok_xhatT)
                b.inc("ST", sp.dma_start(
                    out=tap_xhatT[:, :, i * 128:(i + 1) * 128].rearrange("c p t -> p c t"),
                    in_=xhatT_b.rearrange("p (c t) -> p c t", c=DC)), 16)
            @block.scalar
            def _(ac):
                b.wait(ac, "ST")

        cons_tok[cur] = b.cnt["dv"]

    ar.pop()



def emit_p0b(p, block):
    nc, b, ar = p.nc, p.b, p.arena
    ar.push()
    wo_s = ar.alloc([128, DC * D], BF16)
    mask_s = ar.alloc([128, 8 * 512], BF16)
    kr = ar.alloc([128, 3 * NKH * 512], BF16)
    vr = ar.alloc([128, 3 * 4 * 1024], BF16)
    qT4 = [ar.alloc([128, NQH * 128], BF16) for _ in range(4)]
    probs_A, x1_t = ar.union(
        ([128, 4 * 1536], BF16),
        ([128, D], F32))
    probs_B, x0_t = ar.union(
        ([128, 4 * 1536], BF16),
        ([128, D], F32))
    probs_ph = [[probs_A[:, j * 1536:(j + 1) * 1536] for j in range(4)],
                [probs_B[:, j * 1536:(j + 1) * 1536] for j in range(4)]]
    pslot = [ar.alloc([128, 512], BF16) for _ in range(2)]
    attT_s = ar.alloc([128, NQH * 512], BF16)
    xhat_b2, xhatT_b2 = ar.union(([128, 2 * D], BF16), ([128, 2 * D], BF16))
    xhat_b = xhat_b2[:, 0:D]
    xhatT_b = xhatT_b2[:, D:2 * D]
    scr = {}
    scr["big"] = xhat_b2.bitcast(F32)[:, 0:D]
    for k in ("sum", "ssq", "negmean", "m2", "var", "std", "rstd", "lnb"):
        scr[k] = ar.alloc([128, 1], F32)
    nm3 = [ar.alloc([128, 4], F32) for _ in range(2)]
    nm = [ar.alloc([128, 1], F32) for _ in range(2)]
    lb = [ar.alloc([128, 4], F32) for _ in range(2)]
    lcol = [ar.alloc([128, 1], F32) for _ in range(2)]
    lnl = [ar.alloc([128, 1], F32) for _ in range(2)]
    bias2 = [ar.alloc([128, 1], F32) for _ in range(2)]

    tap_x1 = p.tap("x1", [T, D], F32)
    tap_dbg = p.tap("dbg0", [NSEG, 4, 128, 8], F32)
    tap_pr = p.tap("probs2", [4, 128, 1536], BF16)

    @block.gpsimd
    def _(gp):
        for c in range(DC):
            b.inc("LW", gp.dma_start(out=wo_s[:, c * D:(c + 1) * D],
                                     in_=p.w_o0[c * 128:(c + 1) * 128, :]), 16)
    tok_w = b.cnt["LW"]

    @block.sync
    def _(sp):
        for si, mi in enumerate((0, 1, 2, 3, 8, 9, 10, 11)):
            b.inc("LM", sp.dma_start(
                out=mask_s[:, si * 512:(si + 1) * 512], in_=p.cmask[mi]), 16)
    tok_mask = b.cnt["LM"]

    SB = ((0, 1, 2), (3, 4, 2))
    BTS, BA = (6, 5), 7
    relq_tok = [0, 0]
    ev_tok = [0, 0]
    t_ptr = [0, 0]

    for B in range(NSEG):
        chunks = [c for c in (B - 2, B - 1, B) if c >= 0]
        nch = len(chunks)

        @block.sync
        def _(sp, B=B):
            b.wait(sp, "pe")
            slot = B % 3
            b.inc(f"KR{slot}", sp.dma_start(
                out=kr[:, slot * NKH * 512:(slot + 1) * NKH * 512]
                    .rearrange("p (h t) -> p h t", h=NKH),
                in_=p.s_kT0[:, :, B * 512:(B + 1) * 512].rearrange("h p t -> p h t")), 16)
            b.inc(f"VR{slot}", sp.dma_start(
                out=vr[:, slot * 4096:(slot + 1) * 4096]
                    .rearrange("p (a n) -> p a n", a=4),
                in_=p.s_v0[B * 512:(B + 1) * 512, :]
                    .rearrange("(a p) n -> p a n", p=128)), 16)
            for j in range(4):
                b.inc("LQ", sp.dma_start(
                    out=qT4[j].rearrange("p (h t) -> p h t", h=NQH),
                    in_=p.s_qT0[:, :, (4 * B + j) * 128:(4 * B + j + 1) * 128]
                        .rearrange("h p t -> p h t")), 16)
        tok_ring = {c: b.cnt[f"KR{c % 3}"] for c in chunks}
        tok_vring = {c: b.cnt[f"VR{c % 3}"] for c in chunks}
        tok_lq = b.cnt["LQ"]

        for n in range(NQH):
            kvn = n // 2
            hpar = n % 2
            probs_all = probs_ph[hpar]
            tok_pass2 = [0, 0, 0, 0]
            for j in range(4):
                S = SB[j % 2]
                jp = j % 2

                @block.tensor
                def _(pe, n=n, j=j, S=S, kvn=kvn, B=B, chunks=chunks):
                    if n == 0 and j < 2:
                        b.wait(pe, "ac")
                        b.wait(pe, "dv")
                        b.wait(pe, "LW", tok_w)
                    b.wait(pe, "ac", relq_tok[j % 2])
                    if nch == 3:
                        b.wait(pe, "ac", relq_tok[(j + 1) % 2])
                    b.wait(pe, "LQ", tok_lq)
                    for ci, c in enumerate(chunks):
                        b.wait(pe, f"KR{c % 3}", tok_ring[c])
                        ins = pe.matmul(
                            out=p.pb[S[ci]][:],
                            lhsT=qT4[j][:, n * 128:(n + 1) * 128],
                            rhs=kr[:, (c % 3) * NKH * 512 + kvn * 512:
                                   (c % 3) * NKH * 512 + (kvn + 1) * 512],
                            start=True, stop=True)
                    b.inc("pe", ins)
                tok_sc = b.cnt["pe"]

                @block.vector
                def _(dv, j=j, jp=jp, S=S, B=B, chunks=chunks, nch=nch):
                    b.wait(dv, "pe", tok_sc)
                    b.wait(dv, "LM", tok_mask)
                    ins = None
                    for ci, c in enumerate(chunks):
                        idx = j + 4 * (B - c)
                        if 4 <= idx <= 7:
                            continue
                        slot = idx if idx < 4 else idx - 4
                        ins = dv.tensor_tensor(
                            out=p.pb[S[ci]][:], in0=p.pb[S[ci]][:],
                            in1=mask_s[:, slot * 512:(slot + 1) * 512], op=ALU.add)
                    if ins is not None:
                        dv.drain()
                    for ci in range(nch):
                        dv.reduce_max(out=nm3[jp][:, ci:ci + 1], in_=p.pb[S[ci]][:],
                                      axis=AX.X, negate=True)
                    dv.drain()
                    b.inc("dv", dv.tensor_reduce(out=nm[jp], in_=nm3[jp][:, 0:nch],
                                                 axis=AX.X, op=ALU.min))
                tok_nm = b.cnt["dv"]

                @block.scalar
                def _(ac, n=n, j=j, jp=jp, S=S, nch=nch, hpar=hpar):
                    b.wait(ac, "dv", tok_nm)
                    b.wait(ac, "pe", t_ptr[hpar])
                    if hpar == 0:
                        b.wait(ac, "GT")
                        b.wait(ac, "GX")
                    for ci in range(nch):
                        ins = ac.activation(
                            out=probs_all[j][:, ci * 512:(ci + 1) * 512],
                            in_=p.pb[S[ci]][:], func=ACTF.Exp, bias=nm[jp],
                            scale=1.0, accum_out=lb[jp][:, ci:ci + 1])
                    b.inc("ac", ins)
                tok_l = b.cnt["ac"]

                @block.vector
                def _(dv, jp=jp, nch=nch):
                    b.wait(dv, "ac", tok_l)
                    b.inc("dv", dv.tensor_reduce(out=lcol[jp], in_=lb[jp][:, 0:nch],
                                                 axis=AX.X, op=ALU.add))

                @block.scalar
                def _(ac, jp=jp):
                    b.wait(ac, "dv")
                    b.inc("ac", ac.activation(out=lnl[jp], in_=lcol[jp],
                                              func=ACTF.Ln))

                @block.vector
                def _(dv, jp=jp):
                    b.wait(dv, "ac")
                    b.inc("dv", dv.tensor_tensor(out=bias2[jp], in0=nm[jp],
                                                 in1=lnl[jp], op=ALU.subtract))

                @block.scalar
                def _(ac, j=j, jp=jp, S=S, nch=nch):
                    b.wait(ac, "dv")
                    for ci in range(nch):
                        ins = ac.activation(
                            out=probs_all[j][:, ci * 512:(ci + 1) * 512],
                            in_=p.pb[S[ci]][:], func=ACTF.Exp, bias=bias2[jp],
                            scale=1.0)
                    b.inc("ac", ins)
                tok_pass2[j] = b.cnt["ac"]
                relq_tok[j % 2] = tok_pass2[j]

                if tap_dbg is not None and n == 0:
                    @block.sync
                    def _(sp, B=B, j=j, jp=jp, nch=nch):
                        b.wait(sp, "ac", tok_pass2[j])
                        b.wait(sp, "dv")
                        for col, buf in ((0, nm[jp]), (1, lcol[jp]),
                                         (2, lnl[jp]), (3, bias2[jp])):
                            b.inc("ST", sp.dma_start(
                                out=tap_dbg[B, j, :, col:col + 1], in_=buf,
                                allow_slow_non_contiguous=True), 16)
                        if tap_pr is not None and B == 2:
                            b.inc("ST", sp.dma_start(
                                out=tap_pr[j], in_=probs_all[j]), 16)

                    @block.vector
                    def _(dv):
                        b.wait(dv, "ST")

            nsb = 4 * nch
            for sb in range(nsb):
                ci, sub = sb // 4, sb % 4
                c = chunks[ci]

                bt = BTS[sb % 2]

                @block.tensor
                def _(pe, ci=ci, sub=sub, sb=sb, bt=bt, probs_all=probs_all):
                    b.wait(pe, "ac", max(tok_pass2))
                    b.wait(pe, "ac", ev_tok[sb % 2])
                    for j in range(4):
                        ins = pe.transpose(
                            out=p.pbb[bt][:, j * 128:(j + 1) * 128],
                            in_=probs_all[j][:, ci * 512 + sub * 128:
                                             ci * 512 + (sub + 1) * 128],
                            identity=p.id_b)
                    b.inc("pe", ins)
                tok_tr = b.cnt["pe"]

                @block.scalar
                def _(ac, sb=sb, bt=bt):
                    b.wait(ac, "pe", tok_tr)
                    b.inc("ac", ac.activation(out=pslot[sb % 2],
                                              in_=p.pbb[bt][:, 0:512],
                                              func=ACTF.Copy))
                tok_ev = b.cnt["ac"]
                ev_tok[sb % 2] = tok_ev

                @block.tensor
                def _(pe, sb=sb, ci=ci, sub=sub, c=c, kvn=kvn, nsb=nsb):
                    b.wait(pe, "ac", tok_ev)
                    b.wait(pe, f"VR{c % 3}", tok_vring[c])
                    b.inc("pe", pe.matmul(
                        out=p.pb[BA][:],
                        lhsT=vr[:, (c % 3) * 4096 + sub * 1024 + kvn * 128:
                                (c % 3) * 4096 + sub * 1024 + (kvn + 1) * 128],
                        rhs=pslot[sb % 2], start=(sb == 0), stop=(sb == nsb - 1)))
            t_ptr[hpar] = tok_tr
            tok_pv = b.cnt["pe"]

            @block.scalar
            def _(ac, n=n):
                b.wait(ac, "pe", tok_pv)
                b.inc("ac", ac.activation(out=attT_s[:, n * 512:(n + 1) * 512],
                                          in_=p.pb[BA][:], func=ACTF.Copy))

        tok_attT = b.cnt["ac"]

        for j in range(4):
            i = 4 * B + j

            @block.sync
            def _(sp, i=i):
                b.wait(sp, "dv")
                b.wait(sp, "pe")
                b.inc("LX", sp.dma_start(out=x0_t, in_=p.x0[i * 128:(i + 1) * 128, :]), 16)
            tok_x0 = b.cnt["LX"]

            @block.tensor
            def _(pe, j=j):
                b.wait(pe, "ac", tok_attT)
                b.wait(pe, "dv")
                for dcol in range(4):
                    for c in range(DC):
                        ins = pe.matmul(
                            out=p.pb[dcol][:],
                            lhsT=attT_s[:, c * 512 + j * 128: c * 512 + j * 128 + 128],
                            rhs=wo_s[:, c * D + dcol * 512: c * D + (dcol + 1) * 512],
                            start=(c == 0), stop=(c == DC - 1))
                    b.inc("pe", ins)
            tok_op = b.cnt["pe"]

            @block.vector
            def _(dv, j=j):
                b.wait(dv, "pe", tok_op)
                b.wait(dv, "LX", tok_x0)
                b.wait(dv, "GX")
                b.wait(dv, "ac")
                for dcol in range(4):
                    ins = dv.tensor_tensor(out=x1_t[:, dcol * 512:(dcol + 1) * 512],
                                           in0=p.pb[dcol][:],
                                           in1=x0_t[:, dcol * 512:(dcol + 1) * 512],
                                           op=ALU.add)
                b.inc("dv", ins)
            tok_x1 = b.cnt["dv"]

            @block.gpsimd
            def _(gp, i=i):
                b.wait(gp, "dv", tok_x1)
                b.inc("GX", gp.dma_start(out=p.s_xA[i * 128:(i + 1) * 128, :],
                                         in_=x1_t), 16)
                if tap_x1 is not None:
                    b.inc("GX", gp.dma_start(out=tap_x1[i * 128:(i + 1) * 128, :],
                                             in_=x1_t), 16)

            @block.vector
            def _(dv):
                dv.drain()
                b.wait(dv, "GT")

            @block.scalar
            def _(ac):
                b.wait(ac, "GT")

            emit_ln_xhatT(p, block, x1_t, xhat_b, xhatT_b, scr, (4, 5), expln=True)
            tok_lnT = b.cnt["ac"]

            @block.gpsimd
            def _(gp, i=i):
                b.wait(gp, "ac", tok_lnT)
                b.inc("GT", gp.dma_start(
                    out=p.s_xhatT[:, :, i * 128:(i + 1) * 128].rearrange("c p t -> p c t"),
                    in_=xhatT_b.rearrange("p (c t) -> p c t", c=DC)), 16)

    ar.pop()



def emit_p1(p, block):
    nc, b, ar = p.nc, p.b, p.arena
    ar.push()
    NB = HID0 // 512
    WgT = [ar.alloc([128, DC * 512], BF16) for _ in range(2)]
    WuT = [ar.alloc([128, DC * 512], BF16) for _ in range(2)]
    WdT = [ar.alloc([128, 4 * D], BF16) for _ in range(2)]
    wtmp = [ar.alloc([128, D], I32) for _ in range(2)]
    wdq = [ar.alloc([128, D], BF16) for _ in range(2)]
    sgu = [ar.alloc([128, 8], F32) for _ in range(2)]
    sd_s = ar.alloc([128, DC], F32)
    xh_t = [ar.alloc([128, D], BF16) for _ in range(2)]
    x1_t = [ar.alloc([128, D], F32) for _ in range(2)]
    g_s = [ar.alloc([128, 512], F32) for _ in range(2)]
    h_b = [ar.alloc([128, 512], BF16) for _ in range(2)]
    hT_b = [ar.alloc([128, 512], BF16) for _ in range(2)]
    y_t = [ar.alloc([128, D], F32) for _ in range(2)]
    y_tok = [0, 0]
    BG = (0, 6)
    BU = (1, 7)

    @block.sync
    def _(sp):
        b.inc("LC", sp.dma_start(out=sd_s, in_=p.d0s), 16)
    tok_sd = b.cnt["LC"]

    wprep_dv = [0, 0]
    wprep_tok = [0, 0]

    def prep_band(hb, gate_pe_tok):
        sl = hb % 2

        @block.sync
        def _(sp, hb=hb, sl=sl, gate_pe_tok=gate_pe_tok):
            if gate_pe_tok:
                b.wait(sp, "pe", gate_pe_tok)
            b.inc("LB", sp.dma_start(
                out=sgu[sl][:, 0:4], in_=p.g0s[:, hb * 4:(hb + 1) * 4]), 16)
            b.inc("LB", sp.dma_start(
                out=sgu[sl][:, 4:8], in_=p.u0s[:, hb * 4:(hb + 1) * 4]), 16)

        steps = []
        for mi, (wsrc, scol) in enumerate(((p.g0w, 0), (p.u0w, 4))):
            for q in range(4):
                steps.append(("gu", mi, q, wsrc, scol))
        for dk in range(DC):
            steps.append(("d", dk, None, None, None))

        for si, st in enumerate(steps):
            par = si % 2

            @block.sync
            def _(sp, st=st, hb=hb, par=par):
                b.wait(sp, "dv")
                if st[0] == "gu":
                    _, mi, q, wsrc, scol = st
                    b.inc("LB", sp.dma_start(
                        out=wtmp[par],
                        in_=wsrc[hb * 512 + q * 128: hb * 512 + (q + 1) * 128, :]), 16)
                else:
                    dk = st[1]
                    b.inc("LB", sp.dma_start(
                        out=wtmp[par][:, 0:512],
                        in_=p.d0w[dk * 128:(dk + 1) * 128, hb * 512:(hb + 1) * 512]), 16)
            tok_wt = b.cnt["LB"]

            @block.vector
            def _(dv, st=st, sl=sl, par=par):
                b.wait(dv, "LB", tok_wt)
                b.wait(dv, "LC", tok_sd)
                b.wait(dv, "LBT")
                if st[0] == "gu":
                    _, mi, q, wsrc, scol = st
                    ins = dv.tensor_scalar(
                        out=wdq[par], in0=wtmp[par], scalar1=-128.0,
                        scalar2=sgu[sl][:, scol + q: scol + q + 1],
                        op0=ALU.add, op1=ALU.mult)
                else:
                    dk = st[1]
                    ins = dv.tensor_scalar(
                        out=wdq[par][:, 0:512], in0=wtmp[par][:, 0:512],
                        scalar1=-128.0, scalar2=sd_s[:, dk:dk + 1],
                        op0=ALU.add, op1=ALU.mult)
                b.inc("dv", ins)
            tok_dq = b.cnt["dv"]

            @block.scalar
            def _(ac, st=st, sl=sl, par=par):
                b.wait(ac, "dv", tok_dq)
                if st[0] == "gu":
                    _, mi, q, _, _ = st
                    dstT = (WgT, WuT)[mi][sl]
                    for dk in range(DC):
                        b.inc("LBT", ac.dma_start(
                            out=dstT[:, dk * 512 + q * 128: dk * 512 + (q + 1) * 128],
                            in_=wdq[par][:, dk * 128:(dk + 1) * 128],
                            transpose=True), 16)
                else:
                    dk = st[1]
                    for a in range(4):
                        b.inc("LBT", ac.dma_start(
                            out=WdT[sl][:, a * D + dk * 128: a * D + (dk + 1) * 128],
                            in_=wdq[par][:, a * 128:(a + 1) * 128],
                            transpose=True), 16)
        wprep_tok[sl] = b.cnt["LBT"]

    prep_band(0, 0)

    t_silu = [0, 0]
    t_h = [0, 0]
    t_tr = [0, 0]
    t_hT = [0, 0]
    t_dn = [0, 0]
    t_gu = [0, 0]
    t_ye = 0
    t_lh = [0, 0]
    t_lx = [0, 0]
    last_u_tok = 0

    for hb in range(NB):
        sl = hb % 2
        for i in range(NT):
            cur = i % 2
            bg, bu = BG[cur], BU[cur]

            @block.sync
            def _(sp, hb=hb, i=i, cur=cur):
                b.wait(sp, "pe", t_gu[cur])
                b.inc(f"LH{cur}", sp.dma_start(
                    out=xh_t[cur].rearrange("p (c t) -> p c t", c=DC),
                    in_=p.s_xhatT[:, :, i * 128:(i + 1) * 128].rearrange("c p t -> p c t")), 16)
                if hb == 0:
                    b.wait(sp, "ac", t_ye)
                    b.inc(f"LX{cur}", sp.dma_start(
                        out=x1_t[cur], in_=p.s_xA[i * 128:(i + 1) * 128, :]), 16)
            t_lh[cur] = b.cnt[f"LH{cur}"]
            if hb == 0:
                t_lx[cur] = b.cnt[f"LX{cur}"]

            @block.tensor
            def _(pe, cur=cur, sl=sl, bg=bg, bu=bu):
                b.wait(pe, "LBT", wprep_tok[sl])
                b.wait(pe, f"LH{cur}", t_lh[cur])
                b.wait(pe, "ac", t_hT[cur])
                b.wait(pe, "dv", t_h[cur])
                for dk in range(DC):
                    ins = pe.matmul(out=p.pb[bg][:],
                                    lhsT=xh_t[cur][:, dk * 128:(dk + 1) * 128],
                                    rhs=WgT[sl][:, dk * 512:(dk + 1) * 512],
                                    start=(dk == 0), stop=(dk == DC - 1))
                b.inc("pe", ins)
                for dk in range(DC):
                    ins = pe.matmul(out=p.pb[bu][:],
                                    lhsT=xh_t[cur][:, dk * 128:(dk + 1) * 128],
                                    rhs=WuT[sl][:, dk * 512:(dk + 1) * 512],
                                    start=(dk == 0), stop=(dk == DC - 1))
                b.inc("pe", ins)
            tok_g = b.cnt["pe"] - 1
            t_gu[cur] = last_u_tok = b.cnt["pe"]

            @block.scalar
            def _(ac, cur=cur, bg=bg):
                b.wait(ac, "pe", tok_g)
                b.wait(ac, "dv", t_h[cur])
                b.inc("ac", ac.activation(out=g_s[cur], in_=p.pb[bg][:],
                                          func=ACTF.Silu))
            t_silu[cur] = b.cnt["ac"]

            @block.vector
            def _(dv, cur=cur, bu=bu):
                b.wait(dv, "pe", t_gu[cur])
                b.wait(dv, "ac", t_silu[cur])
                b.wait(dv, "pe", t_tr[cur])
                b.inc("dv", dv.tensor_tensor(out=h_b[cur], in0=g_s[cur],
                                             in1=p.pb[bu][:], op=ALU.mult))
            t_h[cur] = b.cnt["dv"]

            @block.tensor
            def _(pe, cur=cur, bg=bg):
                b.wait(pe, "dv", t_h[cur])
                b.wait(pe, "ac", t_silu[cur])
                b.wait(pe, "ac", t_hT[cur])
                for jj in range(4):
                    ins = pe.transpose(out=p.pbb[bg][:, jj * 128:(jj + 1) * 128],
                                       in_=h_b[cur][:, jj * 128:(jj + 1) * 128],
                                       identity=p.id_b)
                b.inc("pe", ins)
            t_tr[cur] = b.cnt["pe"]

            @block.scalar
            def _(ac, cur=cur, bg=bg):
                b.wait(ac, "pe", t_tr[cur])
                b.wait(ac, "pe", t_dn[cur])
                b.inc("ac", ac.activation(out=hT_b[cur], in_=p.pbb[bg][:, 0:512],
                                          func=ACTF.Copy))
            t_hT[cur] = b.cnt["ac"]

            @block.tensor
            def _(pe, cur=cur, sl=sl):
                b.wait(pe, "ac", t_hT[cur])
                b.wait(pe, "ac", t_ye)
                for dcol in range(4):
                    for hc in range(4):
                        ins = pe.matmul(
                            out=p.pb[2 + dcol][:],
                            lhsT=hT_b[cur][:, hc * 128:(hc + 1) * 128],
                            rhs=WdT[sl][:, hc * D + dcol * 512: hc * D + (dcol + 1) * 512],
                            start=(hc == 0), stop=(hc == 3))
                    b.inc("pe", ins)
            t_dn[cur] = b.cnt["pe"]

            @block.scalar
            def _(ac, hb=hb, cur=cur):
                b.wait(ac, "pe", t_dn[cur])
                b.wait(ac, f"SY{cur}", y_tok[cur])
                if hb == 0:
                    b.wait(ac, f"LX{cur}", t_lx[cur])
                for dcol in range(4):
                    ins = ac.activation(out=y_t[cur][:, dcol * 512:(dcol + 1) * 512],
                                        in_=p.pb[2 + dcol][:], func=ACTF.Copy)
                b.inc("ac", ins)
            t_ye = b.cnt["ac"]

            if hb == 0:
                @block.vector
                def _(dv, cur=cur):
                    b.wait(dv, "ac", t_ye)
                    for dcol in range(4):
                        ins = dv.tensor_tensor(
                            out=y_t[cur][:, dcol * 512:(dcol + 1) * 512],
                            in0=y_t[cur][:, dcol * 512:(dcol + 1) * 512],
                            in1=x1_t[cur][:, dcol * 512:(dcol + 1) * 512],
                            op=ALU.add)
                    b.inc("dv", ins)
                tok_resid = b.cnt["dv"]

            @block.gpsimd
            def _(gp, hb=hb, i=i, cur=cur):
                if hb == 0:
                    b.wait(gp, "dv", tok_resid)
                else:
                    b.wait(gp, "ac", t_ye)
                ch = f"SY{cur}"
                if b.cnt.get(ch, 0) >= 48:
                    b.wait(gp, ch, b.cnt[ch] - 32)
                if hb == 0:
                    ins = gp.dma_start(out=p.s_xB[i * 128:(i + 1) * 128, :],
                                       in_=y_t[cur])
                else:
                    ins = gp.dma_start(out=p.s_xB[i * 128:(i + 1) * 128, :],
                                       in_=y_t[cur], accum_op=ALU.add)
                b.inc(ch, ins, 16)
                y_tok[cur] = b.cnt[ch]

            if i == 4 and hb + 1 < NB:
                prep_band(hb + 1, last_u_tok)

    tap_x2 = p.tap("x2", [T, D], F32)
    if tap_x2 is not None:
        @block.sync
        def _(sp):
            for ch in ("SY0", "SY1"):
                b.wait(sp, ch)
            b.inc("ST", sp.dma_start(out=tap_x2, in_=p.s_xB), 16)

    ar.pop()



def emit_p2p(p, block):
    nc, b, ar = p.nc, p.b, p.arena
    ar.push()
    x2_t = [ar.alloc([128, D], F32) for _ in range(2)]
    xh2 = [ar.alloc([128, 2 * D], BF16) for _ in range(2)]
    scrs = []
    for par in range(2):
        scr = {"big": xh2[par].bitcast(F32)[:, 0:D]}
        for k in ("sum", "ssq", "negmean", "m2", "var", "std", "rstd", "lnb"):
            scr[k] = ar.alloc([128, 1], F32)
        scrs.append(scr)
    t_xhat = [0, 0]
    t_st = [0, 0]

    for i in range(NT):
        cur = i % 2

        @block.sync
        def _(sp, i=i, cur=cur):
            b.wait(sp, "ac", t_xhat[cur])
            b.inc(f"LP{cur}", sp.dma_start(out=x2_t[cur],
                                           in_=p.s_xB[i * 128:(i + 1) * 128, :]), 16)
        tok_x2 = b.cnt[f"LP{cur}"]

        @block.vector
        def _(dv, cur=cur):
            b.wait(dv, f"LP{cur}", tok_x2)
            b.wait(dv, f"ST{cur}", t_st[cur])

        @block.scalar
        def _(ac, cur=cur):
            b.wait(ac, f"LP{cur}", tok_x2)
            b.wait(ac, f"ST{cur}", t_st[cur])

        emit_ln_xhatT(p, block, x2_t[cur], xh2[cur][:, 0:D], xh2[cur][:, D:2 * D],
                      scrs[cur], (2 * cur, 2 * cur + 1))
        t_xhat[cur] = b.cnt["ac"]

        @block.scalar
        def _(ac, i=i, cur=cur):
            ac.drain()
            b.inc(f"ST{cur}", ac.dma_start(
                out=p.s_xhatT[:, :, i * 128:(i + 1) * 128].rearrange("c p t -> p c t"),
                in_=xh2[cur][:, D:2 * D].rearrange("p (c t) -> p c t", c=DC)), 16)
        t_st[cur] = b.cnt[f"ST{cur}"]
    ar.pop()


def emit_p2a(p, block):
    nc, b, ar = p.nc, p.b, p.arena
    ar.push()
    w_s = ar.alloc([128, DC * D], BF16)
    xhT_blk = [ar.alloc([128, DC * 512], BF16) for _ in range(2)]
    ev = [ar.alloc([128, 512], F32) for _ in range(2)]

    for wi, (wsrc, dstT) in enumerate(((p.w_q1, p.s_qT1), (p.w_k1, p.s_kT1))):
        @block.gpsimd
        def _(gp, wsrc=wsrc):
            b.wait(gp, "pe")
            for dk in range(DC):
                b.inc("LW", gp.dma_start(out=w_s[:, dk * D:(dk + 1) * D],
                                         in_=wsrc[dk * 128:(dk + 1) * 128, :]), 16)
        tok_w = b.cnt["LW"]

        for tb in range(NSEG):
            cur = tb % 2

            @block.sync
            def _(sp, tb=tb, cur=cur):
                b.wait(sp, "pe")
                b.inc(f"LG{cur}", sp.dma_start(
                    out=xhT_blk[cur].rearrange("p (c t) -> p c t", c=DC),
                    in_=p.s_xhatT[:, :, tb * 512:(tb + 1) * 512]
                        .rearrange("c p t -> p c t")), 16)
            tok_xh = b.cnt[f"LG{cur}"]

            for fc in range(DC):
                bank = fc % 2

                @block.tensor
                def _(pe, fc=fc, cur=cur, bank=bank):
                    b.wait(pe, "LW", tok_w)
                    b.wait(pe, f"LG{cur}", tok_xh)
                    b.wait(pe, f"SP{fc % 2}")
                    for dk in range(DC):
                        ins = pe.matmul(
                            out=p.pb[bank][:],
                            lhsT=w_s[:, dk * D + fc * 128: dk * D + (fc + 1) * 128],
                            rhs=xhT_blk[cur][:, dk * 512:(dk + 1) * 512],
                            start=(dk == 0), stop=(dk == DC - 1))
                    b.inc("pe", ins)
                tok_mm = b.cnt["pe"]

                @block.scalar
                def _(ac, fc=fc, tb=tb, bank=bank, dstT=dstT):
                    b.wait(ac, "pe", tok_mm)
                    b.wait(ac, f"SP{fc % 2}")
                    ac.activation(out=ev[bank], in_=p.pb[bank][:], func=ACTF.Copy)
                    ac.drain()
                    b.inc(f"SP{fc % 2}", ac.dma_start(
                        out=dstT[fc, :, tb * 512:(tb + 1) * 512], in_=ev[bank]), 16)

    @block.gpsimd
    def _(gp):
        b.wait(gp, "pe")
        for dk in range(DC):
            b.inc("LW", gp.dma_start(out=w_s[:, dk * D:(dk + 1) * D],
                                     in_=p.w_v1[dk * 128:(dk + 1) * 128, :]), 16)
    tok_w = b.cnt["LW"]
    vout = [ar.alloc([128, D], F32) for _ in range(2)]

    for i in range(NT):
        cur = i % 2

        @block.sync
        def _(sp, i=i, cur=cur):
            b.wait(sp, "pe")
            b.inc(f"LG{cur}", sp.dma_start(
                out=xhT_blk[cur][:, 0:DC * 128].rearrange("p (c t) -> p c t", c=DC),
                in_=p.s_xhatT[:, :, i * 128:(i + 1) * 128].rearrange("c p t -> p c t")), 16)
        tok_xh = b.cnt[f"LG{cur}"]

        @block.tensor
        def _(pe, cur=cur):
            b.wait(pe, "LW", tok_w)
            b.wait(pe, f"LG{cur}", tok_xh)
            b.wait(pe, "SP0")
            b.wait(pe, "SP1")
            for vc in range(4):
                for dk in range(DC):
                    ins = pe.matmul(
                        out=p.pb[2 + vc][:],
                        lhsT=xhT_blk[cur][:, dk * 128:(dk + 1) * 128],
                        rhs=w_s[:, dk * D + vc * 512: dk * D + (vc + 1) * 512],
                        start=(dk == 0), stop=(dk == DC - 1))
                b.inc("pe", ins)
        tok_mm = b.cnt["pe"]

        @block.scalar
        def _(ac, i=i, cur=cur):
            b.wait(ac, "pe", tok_mm)
            b.wait(ac, f"SP{cur}")
            for vc in range(4):
                ac.activation(out=vout[cur][:, vc * 512:(vc + 1) * 512],
                              in_=p.pb[2 + vc][:], func=ACTF.Copy)
            ac.drain()
            b.inc(f"SP{cur}", ac.dma_start(out=p.s_v1[i * 128:(i + 1) * 128, :],
                                           in_=vout[cur]), 16)
    ar.pop()


def emit_p2b(p, block):
    nc, b, ar = p.nc, p.b, p.arena
    ar.push()
    C1 = ATTN_SCALE
    wo_s = ar.alloc([128, DC * D], BF16)
    mask1 = ar.alloc([128, 4 * 512], BF16)
    memz_f = ar.alloc([128, NH1 * 129], F32)
    memz_b = ar.alloc([128, NH1 * 129], BF16)
    beta_s = ar.alloc([128, NH1], F32)
    btmp = ar.alloc([128, NH1], F32)
    qT_h = [ar.alloc([128, 512], F32) for _ in range(2)]
    kT_h = [ar.alloc([128, 512], F32) for _ in range(2)]
    v_h = [ar.alloc([128, 512], F32) for _ in range(2)]
    v_hb = [ar.alloc([128, 512], BF16) for _ in range(2)]
    min0 = [ar.alloc([128, 512], F32) for _ in range(2)]
    e_s = [ar.alloc([128, 512], F32) for _ in range(2)]
    sigq_b = [ar.alloc([128, 512], BF16) for _ in range(2)]
    sigk_b = [ar.alloc([128, 512], BF16) for _ in range(2)]
    sigk_tok = [ar.alloc([128, 512], BF16) for _ in range(2)]
    probs_seg = [ar.alloc([128, 4 * 512], BF16) for _ in range(2)]
    pslot = [ar.alloc([128, 512], BF16) for _ in range(2)]
    attT_s = ar.alloc([128, NH1 * 512], BF16)
    delta_ones = [ar.alloc([128, 4 * 132], BF16) for _ in range(2)]
    rz_col = ar.alloc([128, 1], F32)
    nrz_col = ar.alloc([128, 1], F32)
    rz_row = ar.alloc([1, 512], F32)
    rzb_row = ar.alloc([1, 512], BF16)
    ones_b = ar.alloc([1, 128], BF16)
    brcp_s = ar.alloc([128, 512], F32)
    t1_s = ar.alloc([128, 512], F32)
    t2_s = ar.alloc([128, 512], F32)
    x2_t = ar.alloc([128, D], F32)
    x3_t = ar.alloc([128, D], F32)
    xh2 = ar.alloc([128, 2 * D], BF16)
    xhat_b = xh2[:, 0:D]
    xhatT_b = xh2[:, D:2 * D]
    scr = {"big": xh2.bitcast(F32)[:, 0:D]}
    for k in ("sum", "ssq", "negmean", "m2", "var", "std", "rstd", "lnb"):
        scr[k] = ar.alloc([128, 1], F32)
    nm3 = ar.alloc([128, 4], F32)
    nm = ar.alloc([128, 1], F32)
    bias1 = ar.alloc([128, 1], F32)
    lb = ar.alloc([128, 4], F32)
    lcol = ar.alloc([128, 1], F32)
    lnl = ar.alloc([128, 1], F32)
    bias2 = ar.alloc([128, 1], F32)

    tap_x3 = p.tap("x3", [T, D], F32)

    @block.gpsimd
    def _(gp):
        for c in range(DC):
            b.inc("LW", gp.dma_start(out=wo_s[:, c * D:(c + 1) * D],
                                     in_=p.w_o1[c * 128:(c + 1) * 128, :]), 16)

    tok_w = b.cnt["LW"]
    tok_mask = b.cnt["LM"]

    @block.sync
    def _(sp):
        for mi in range(4):
            b.inc("LM", sp.dma_start(
                out=mask1[:, mi * 512:(mi + 1) * 512], in_=p.cmask[mi]), 16)
        b.inc("LC", sp.dma_start(out=memz_f, in_=p.cmemz0), 16)
        b.inc("LC", sp.dma_start(out=btmp, in_=p.betasT), 16)
    tok_mz = b.cnt["LC"]
    tok_mask = b.cnt["LM"]

    @block.scalar
    def _(ac):
        b.wait(ac, "LC", tok_mz)
        b.inc("ac", ac.activation(out=beta_s, in_=btmp, func=ACTF.Exp, scale=-1.0))

    @block.vector
    def _(dv):
        b.wait(dv, "LC", tok_mz)
        b.wait(dv, "ac")
        b.inc("dv", dv.tensor_copy(out=memz_b, in_=memz_f))
        dv.tensor_scalar(out=beta_s, in0=beta_s, scalar1=1.0, scalar2=None,
                         op0=ALU.add)
        dv.drain()
        dv.reciprocal(out=beta_s, in_=beta_s)
        dv.memset(ones_b[:], 1.0)
        for par in range(2):
            for sub in range(4):
                dv.memset(delta_ones[par][:, sub * 132 + 128: sub * 132 + 129], 1.0)
        dv.drain()
        b.inc("dv", dv.memset(rz_col, 0.0))

    qTb = [ar.alloc([128, 512], BF16) for _ in range(2)]
    kTb = [ar.alloc([128, 512], BF16) for _ in range(2)]
    relq_tok = [0, 0]

    for s in range(NSEG):
        for n in range(NH1):
            cur = n % 2
            hpar = cur
            hpar = cur

            @block.sync
            def _(sp, s=s, n=n, cur=cur):
                b.wait(sp, "pe")
                b.wait(sp, "dv")
                b.inc(f"LA{cur}", sp.dma_start(
                    out=qT_h[cur], in_=p.s_qT1[n, :, s * 512:(s + 1) * 512]), 16)
                b.inc(f"LA{cur}", sp.dma_start(
                    out=kT_h[cur], in_=p.s_kT1[n, :, s * 512:(s + 1) * 512]), 16)
                b.inc(f"LA{cur}", sp.dma_start(
                    out=v_h[cur].rearrange("p (a n) -> p a n", a=4),
                    in_=p.s_v1[s * 512:(s + 1) * 512, n * 128:(n + 1) * 128]
                        .rearrange("(a p) n -> p a n", p=128)), 16)
            tok_la = b.cnt[f"LA{cur}"]

            @block.gpsimd
            def _(gp, s=s, n=n, cur=cur):
                b.wait(gp, "pe")
                b.inc(f"LV{cur}", gp.dma_start(
                    out=v_hb[cur].rearrange("p (a n) -> p a n", a=4),
                    in_=p.s_v1[s * 512:(s + 1) * 512, n * 128:(n + 1) * 128]
                        .rearrange("(a p) n -> p a n", p=128)), 16)
                b.inc(f"LV{cur}", gp.dma_start(
                    out=qTb[cur], in_=p.s_qT1[n, :, s * 512:(s + 1) * 512]), 16)
                b.inc(f"LV{cur}", gp.dma_start(
                    out=kTb[cur], in_=p.s_kT1[n, :, s * 512:(s + 1) * 512]), 16)
            tok_lv = b.cnt[f"LV{cur}"]

            @block.vector
            def _(dv, cur=cur):
                b.wait(dv, f"LA{cur}", tok_la)
                b.wait(dv, "ac")
                b.wait(dv, "pe")
                b.inc("dv", dv.tensor_scalar(out=min0[hpar], in0=qT_h[cur], scalar1=0.0,
                                             scalar2=None, op0=ALU.min))

            @block.scalar
            def _(ac):
                b.wait(ac, "dv")
                b.inc("ac", ac.activation(out=e_s[hpar], in_=min0[hpar], func=ACTF.Exp))

            @block.vector
            def _(dv, cur=cur):
                b.wait(dv, "ac")
                b.inc("dv", dv.scalar_tensor_tensor(
                    out=sigq_b[hpar], in0=qT_h[cur], scalar=0.0, in1=e_s[hpar],
                    op0=ALU.max, op1=ALU.add))
                b.inc("dv", dv.tensor_scalar(out=min0[hpar], in0=kT_h[cur], scalar1=0.0,
                                             scalar2=None, op0=ALU.min))

            @block.scalar
            def _(ac):
                b.wait(ac, "dv")
                b.inc("ac", ac.activation(out=e_s[hpar], in_=min0[hpar], func=ACTF.Exp))

            @block.vector
            def _(dv, cur=cur):
                b.wait(dv, "ac")
                b.inc("dv", dv.scalar_tensor_tensor(
                    out=sigk_b[hpar], in0=kT_h[cur], scalar=0.0, in1=e_s[hpar],
                    op0=ALU.max, op1=ALU.add))
            tok_sig = b.cnt["dv"]

            @block.tensor
            def _(pe):
                b.wait(pe, "dv", tok_sig)
                b.wait(pe, "ac")
                for sub in range(4):
                    ins = pe.transpose(out=p.pbb[2][:, sub * 128:(sub + 1) * 128],
                                       in_=sigk_b[hpar][:, sub * 128:(sub + 1) * 128],
                                       identity=p.id_b)
                b.inc("pe", ins)
            tok_str = b.cnt["pe"]

            @block.scalar
            def _(ac):
                b.wait(ac, "pe", tok_str)
                b.wait(ac, "pe")
                b.inc("ac", ac.activation(out=sigk_tok[hpar], in_=p.pbb[2][:, 0:512],
                                          func=ACTF.Copy))
            tok_stok = b.cnt["ac"]

            tok_pass2 = [0, 0, 0, 0]
            for j in range(4):
                bank = j % 2

                @block.tensor
                def _(pe, j=j, cur=cur, bank=bank):
                    b.wait(pe, f"LV{cur}", tok_lv)
                    b.wait(pe, "ac", relq_tok[bank])
                    b.inc("pe", pe.matmul(out=p.pb[bank][:],
                                          lhsT=qTb[cur][:, j * 128:(j + 1) * 128],
                                          rhs=kTb[cur], start=True, stop=True))
                tok_sc = b.cnt["pe"]

                @block.vector
                def _(dv, j=j, bank=bank):
                    b.wait(dv, "pe", tok_sc)
                    b.wait(dv, "LM", tok_mask)
                    dv.tensor_tensor(out=p.pb[bank][:], in0=p.pb[bank][:],
                                     in1=mask1[:, j * 512:(j + 1) * 512],
                                     op=ALU.add)
                    dv.drain()
                    dv.reduce_max(out=nm, in_=p.pb[bank][:], axis=AX.X, negate=True)
                    dv.drain()
                    b.inc("dv", dv.tensor_scalar(out=bias1, in0=nm, scalar1=C1,
                                                 scalar2=None, op0=ALU.mult))
                tok_nm = b.cnt["dv"]

                @block.scalar
                def _(ac, j=j, bank=bank):
                    b.wait(ac, "dv", tok_nm)
                    b.inc("ac", ac.activation(
                        out=probs_seg[hpar][:, j * 512:(j + 1) * 512],
                        in_=p.pb[bank][:], func=ACTF.Exp, bias=bias1,
                        scale=C1, accum_out=lcol))
                tok_l = b.cnt["ac"]

                @block.scalar
                def _(ac):
                    b.wait(ac, "ac", tok_l)
                    ac.drain()
                    b.inc("ac", ac.activation(out=lnl, in_=lcol, func=ACTF.Ln))

                @block.vector
                def _(dv):
                    b.wait(dv, "ac")
                    b.inc("dv", dv.tensor_tensor(out=bias2, in0=bias1, in1=lnl,
                                                 op=ALU.subtract))

                @block.scalar
                def _(ac, j=j, bank=bank):
                    b.wait(ac, "dv")
                    b.inc("ac", ac.activation(
                        out=probs_seg[hpar][:, j * 512:(j + 1) * 512],
                        in_=p.pb[bank][:], func=ACTF.Exp, bias=bias2, scale=C1))
                tok_pass2[j] = b.cnt["ac"]
                relq_tok[bank] = tok_pass2[j]


            for sb in range(4):
                @block.tensor
                def _(pe, sb=sb):
                    b.wait(pe, "ac", max(tok_pass2))
                    b.wait(pe, "ac")
                    for j in range(4):
                        ins = pe.transpose(
                            out=p.pbb[2][:, j * 128:(j + 1) * 128],
                            in_=probs_seg[hpar][:, j * 512 + sb * 128: j * 512 + (sb + 1) * 128],
                            identity=p.id_b)
                    b.inc("pe", ins)
                tok_tr = b.cnt["pe"]

                @block.scalar
                def _(ac, sb=sb):
                    b.wait(ac, "pe", tok_tr)
                    b.inc("ac", ac.activation(out=pslot[sb % 2],
                                              in_=p.pbb[2][:, 0:512],
                                              func=ACTF.Copy))
                tok_ev = b.cnt["ac"]

                @block.tensor
                def _(pe, sb=sb, cur=cur):
                    b.wait(pe, "ac", tok_ev)
                    if sb == 0:
                        b.wait(pe, "dv")
                    b.inc("pe", pe.matmul(
                        out=p.pb[3][:],
                        lhsT=v_hb[cur][:, sb * 128:(sb + 1) * 128],
                        rhs=pslot[sb % 2], start=(sb == 0), stop=(sb == 3)))
            tok_pv = b.cnt["pe"]

            @block.tensor
            def _(pe, n=n):
                b.wait(pe, "ac")
                b.wait(pe, "dv", tok_sig)
                b.inc("pe", pe.matmul(out=p.pb[4][:],
                                      lhsT=memz_b[:, n * 129: n * 129 + 128],
                                      rhs=sigq_b[hpar], start=True, stop=True))
                b.inc("pe", pe.matmul(out=p.pb[5][0:1, :],
                                      lhsT=memz_b[:, n * 129 + 128: n * 129 + 129],
                                      rhs=sigq_b[hpar], start=True, stop=True))
            tok_ro = b.cnt["pe"]

            @block.vector
            def _(dv):
                b.wait(dv, "pe", tok_ro)
                dv.reciprocal(out=rz_row, in_=p.pb[5][0:1, :])
                dv.drain()
                b.inc("dv", dv.tensor_copy(out=rzb_row, in_=rz_row))
            tok_rz = b.cnt["dv"]

            @block.tensor
            def _(pe):
                b.wait(pe, "dv", tok_rz)
                b.inc("pe", pe.matmul(out=p.pb[5][:], lhsT=ones_b, rhs=rzb_row,
                                      start=True, stop=True))
            tok_bc = b.cnt["pe"]

            @block.scalar
            def _(ac):
                b.wait(ac, "pe", tok_bc)
                b.inc("ac", ac.activation(out=brcp_s, in_=p.pb[5][:],
                                          func=ACTF.Copy))
            tok_brcp = b.cnt["ac"]

            for sub in range(4):
                @block.tensor
                def _(pe, n=n, sub=sub):
                    b.wait(pe, "ac", tok_stok)
                    b.wait(pe, "dv")
                    b.inc("pe", pe.matmul(
                        out=p.pb[6][:, 0:129],
                        lhsT=sigk_b[hpar][:, sub * 128:(sub + 1) * 128],
                        rhs=memz_b[:, n * 129:(n + 1) * 129],
                        start=True, stop=True))
                tok_R = b.cnt["pe"]

                @block.vector
                def _(dv, sub=sub, cur=cur):
                    b.wait(dv, "pe", tok_R)
                    b.wait(dv, f"LA{cur}", tok_la)
                    dv.reciprocal(out=rz_col, in_=p.pb[6][:, 128:129])
                    dv.drain()
                    dv.tensor_scalar(out=nrz_col, in0=rz_col, scalar1=-1.0,
                                     scalar2=None, op0=ALU.mult)
                    dv.drain()
                    b.inc("dv", dv.scalar_tensor_tensor(
                        out=delta_ones[hpar][:, sub * 132: sub * 132 + 128],
                        in0=p.pb[6][:, 0:128], scalar=nrz_col,
                        in1=v_h[cur][:, sub * 128:(sub + 1) * 128],
                        op0=ALU.mult, op1=ALU.add))
                tok_delta = b.cnt["dv"]

                @block.tensor
                def _(pe, sub=sub):
                    b.wait(pe, "dv", tok_delta)
                    b.inc("pe", pe.matmul(
                        out=p.pb[7][:, 0:129],
                        lhsT=sigk_tok[hpar][:, sub * 128:(sub + 1) * 128],
                        rhs=delta_ones[hpar][:, sub * 132: sub * 132 + 129],
                        start=(sub == 0), stop=(sub == 3)))
            tok_upd = b.cnt["pe"]

            @block.vector
            def _(dv, n=n):
                b.wait(dv, "pe", tok_upd)
                b.inc("dv", dv.tensor_tensor(
                    out=memz_f[:, n * 129:(n + 1) * 129],
                    in0=memz_f[:, n * 129:(n + 1) * 129],
                    in1=p.pb[7][:, 0:129], op=ALU.add))
            tok_mzf = b.cnt["dv"]

            @block.scalar
            def _(ac, n=n):
                b.wait(ac, "dv", tok_mzf)
                b.inc("ac", ac.activation(out=memz_b[:, n * 129:(n + 1) * 129],
                                          in_=memz_f[:, n * 129:(n + 1) * 129],
                                          func=ACTF.Copy))

            att_scr = attT_s.rearrange("p (c j a bb) -> p c j a bb",
                                       c=16, j=4, a=4)

            @block.vector
            def _(dv, n=n):
                b.wait(dv, "ac", tok_brcp)
                b.wait(dv, "pe", tok_pv)
                dv.tensor_tensor(out=t1_s, in0=p.pb[4][:], in1=brcp_s, op=ALU.mult)
                dv.drain()
                dv.tensor_tensor(out=t2_s, in0=t1_s, in1=p.pb[3][:], op=ALU.subtract)
                dv.drain()
                out_ap = att_scr[:, :, n // 4, n % 4, :].rearrange("p c b -> p b c")
                b.inc("dv", dv.scalar_tensor_tensor(
                    out=out_ap,
                    in0=t2_s, scalar=beta_s[:, n:n + 1], in1=p.pb[3][:],
                    op0=ALU.mult, op1=ALU.add))
            tok_att = b.cnt["dv"]

        for j in range(4):
            i = 4 * s + j

            @block.sync
            def _(sp, i=i):
                b.wait(sp, "dv")
                b.inc("LX2", sp.dma_start(out=x2_t,
                                          in_=p.s_xB[i * 128:(i + 1) * 128, :]), 16)
            tok_x2 = b.cnt["LX2"]

            @block.tensor
            def _(pe, j=j):
                b.wait(pe, "LW", tok_w)
                b.wait(pe, "dv", tok_att)
                b.wait(pe, "dv")
                b.wait(pe, "ac")
                for dcol in range(4):
                    for c in range(DC):
                        ins = pe.matmul(
                            out=p.pb[dcol][:],
                            lhsT=attT_s[:, c * 512 + j * 128: c * 512 + j * 128 + 128],
                            rhs=wo_s[:, c * D + dcol * 512: c * D + (dcol + 1) * 512],
                            start=(c == 0), stop=(c == DC - 1))
                    b.inc("pe", ins)
            tok_op = b.cnt["pe"]

            @block.vector
            def _(dv, j=j):
                b.wait(dv, "pe", tok_op)
                b.wait(dv, "LX2", tok_x2)
                b.wait(dv, "GX")
                for dcol in range(4):
                    ins = dv.tensor_tensor(out=x3_t[:, dcol * 512:(dcol + 1) * 512],
                                           in0=p.pb[dcol][:],
                                           in1=x2_t[:, dcol * 512:(dcol + 1) * 512],
                                           op=ALU.add)
                b.inc("dv", ins)
            tok_x3 = b.cnt["dv"]

            @block.gpsimd
            def _(gp, i=i):
                b.wait(gp, "dv", tok_x3)
                b.inc("GX", gp.dma_start(out=p.s_xA[i * 128:(i + 1) * 128, :],
                                         in_=x3_t), 16)
                if tap_x3 is not None:
                    b.inc("GX", gp.dma_start(out=tap_x3[i * 128:(i + 1) * 128, :],
                                             in_=x3_t), 16)
            tok_sx = b.cnt["GX"]

            @block.vector
            def _(dv):
                dv.drain()
                b.wait(dv, "GT")

            @block.scalar
            def _(ac):
                b.wait(ac, "GT")
                b.wait(ac, "GX", tok_sx)

            emit_ln_xhatT(p, block, x3_t, xhat_b, xhatT_b, scr, (4, 5), expln=True)
            tok_lnT = b.cnt["ac"]

            @block.gpsimd
            def _(gp, i=i):
                b.wait(gp, "ac", tok_lnT)
                b.inc("GT", gp.dma_start(
                    out=p.s_xhatT[:, :, i * 128:(i + 1) * 128].rearrange("c p t -> p c t"),
                    in_=xhatT_b.rearrange("p (c t) -> p c t", c=DC)), 16)

    ar.pop()



def emit_p3(p, block):
    nc, b, ar = p.nc, p.b, p.arena
    ar.push()
    NB = HID1 // 512
    Wg_s = [ar.alloc([128, DC * 512], BF16) for _ in range(2)]
    Wu_s = [ar.alloc([128, DC * 512], BF16) for _ in range(2)]
    Wd_s = [ar.alloc([128, 4 * D], BF16) for _ in range(2)]
    xh_t = [ar.alloc([128, D], BF16) for _ in range(2)]
    x3_t = [ar.alloc([128, D], F32) for _ in range(2)]
    g_s = [ar.alloc([128, 512], F32) for _ in range(2)]
    h_f = [ar.alloc([128, 512], F32) for _ in range(2)]
    s_f = [ar.alloc([128, 512], F32) for _ in range(2)]
    pq = [ar.alloc([128, 512], F32) for _ in range(2)]
    m2 = [ar.alloc([128, 256], F32) for _ in range(2)]
    geb = [ar.alloc([128, 512], F32) for _ in range(2)]
    h_sp = [ar.alloc([128, 512], BF16) for _ in range(2)]
    hT_b = [ar.alloc([128, 512], BF16) for _ in range(2)]
    y_t = [ar.alloc([128, D], F32) for _ in range(2)]
    y_tok = [0, 0]
    BG = (0, 6)
    BU = (1, 7)

    def load_band(ib, gate_tok):
        @block.gpsimd
        def _(gp, ib=ib, gate_tok=gate_tok):
            if gate_tok:
                b.wait(gp, "pe", gate_tok)
            sl = ib % 2
            for dk in range(DC):
                b.inc("LW", gp.dma_start(
                    out=Wg_s[sl][:, dk * 512:(dk + 1) * 512],
                    in_=p.g1w[dk * 128:(dk + 1) * 128, ib * 512:(ib + 1) * 512]), 16)
                b.inc("LW", gp.dma_start(
                    out=Wu_s[sl][:, dk * 512:(dk + 1) * 512],
                    in_=p.u1w[dk * 128:(dk + 1) * 128, ib * 512:(ib + 1) * 512]), 16)
            for a in range(4):
                b.inc("LW", gp.dma_start(
                    out=Wd_s[sl][:, a * D:(a + 1) * D],
                    in_=p.d1w[ib * 512 + a * 128: ib * 512 + (a + 1) * 128, :]), 16)
        return b.cnt["LW"]

    tok_w = [0, 0]
    tok_w[0] = load_band(0, 0)
    t_silu = [0, 0]
    t_h = [0, 0]
    t_sq = [0, 0]
    t_sp = [0, 0]
    t_tr = [0, 0]
    t_hT = [0, 0]
    t_dn = [0, 0]
    t_gu = [0, 0]
    t_ye = 0
    t_lh = [0, 0]
    t_lx = [0, 0]
    last_u_tok = 0

    for ib in range(NB):
        sl = ib % 2
        for i in range(NT):
            cur = i % 2
            bg, bu = BG[cur], BU[cur]

            @block.sync
            def _(sp, ib=ib, i=i, cur=cur):
                b.wait(sp, "pe", t_gu[cur])
                b.inc(f"LH{cur}", sp.dma_start(
                    out=xh_t[cur].rearrange("p (c t) -> p c t", c=DC),
                    in_=p.s_xhatT[:, :, i * 128:(i + 1) * 128].rearrange("c p t -> p c t")), 16)
                if ib == 0:
                    b.wait(sp, "ac", t_ye)
                    b.inc(f"LX{cur}", sp.dma_start(
                        out=x3_t[cur], in_=p.s_xA[i * 128:(i + 1) * 128, :]), 16)
            t_lh[cur] = b.cnt[f"LH{cur}"]
            if ib == 0:
                t_lx[cur] = b.cnt[f"LX{cur}"]

            @block.tensor
            def _(pe, cur=cur, sl=sl, bg=bg, bu=bu):
                b.wait(pe, "LW", tok_w[sl])
                b.wait(pe, f"LH{cur}", t_lh[cur])
                b.wait(pe, "ac", t_hT[cur])
                b.wait(pe, "dv", t_h[cur])
                for dk in range(DC):
                    ins = pe.matmul(out=p.pb[bg][:],
                                    lhsT=xh_t[cur][:, dk * 128:(dk + 1) * 128],
                                    rhs=Wg_s[sl][:, dk * 512:(dk + 1) * 512],
                                    start=(dk == 0), stop=(dk == DC - 1))
                b.inc("pe", ins)
                for dk in range(DC):
                    ins = pe.matmul(out=p.pb[bu][:],
                                    lhsT=xh_t[cur][:, dk * 128:(dk + 1) * 128],
                                    rhs=Wu_s[sl][:, dk * 512:(dk + 1) * 512],
                                    start=(dk == 0), stop=(dk == DC - 1))
                b.inc("pe", ins)
            tok_g = b.cnt["pe"] - 1
            t_gu[cur] = last_u_tok = b.cnt["pe"]

            @block.scalar
            def _(ac, cur=cur, bg=bg):
                b.wait(ac, "pe", tok_g)
                b.wait(ac, "dv", t_h[cur])
                b.inc("ac", ac.activation(out=g_s[cur], in_=p.pb[bg][:],
                                          func=ACTF.Silu))
            t_silu[cur] = b.cnt["ac"]

            @block.vector
            def _(dv, cur=cur, bu=bu):
                b.wait(dv, "pe", t_gu[cur])
                b.wait(dv, "ac", t_silu[cur])
                b.wait(dv, "ac", t_sq[cur])
                b.inc("dv", dv.tensor_tensor(out=h_f[cur], in0=g_s[cur],
                                             in1=p.pb[bu][:], op=ALU.mult))
            t_h[cur] = b.cnt["dv"]

            @block.scalar
            def _(ac, cur=cur):
                b.wait(ac, "dv", t_h[cur])
                b.inc("ac", ac.activation(out=s_f[cur], in_=h_f[cur],
                                          func=ACTF.Square))
            t_sq[cur] = b.cnt["ac"]

            @block.vector
            def _(dv, cur=cur):
                b.wait(dv, "ac", t_sq[cur])
                b.wait(dv, "pe", t_tr[cur])
                sf, pqc, m2c, gec = s_f[cur], pq[cur], m2[cur], geb[cur]
                dv.tensor_tensor(out=pqc[:, 0:128], in0=sf[:, 0:512:4],
                                 in1=sf[:, 1:512:4], op=ALU.max)
                dv.tensor_tensor(out=pqc[:, 128:256], in0=sf[:, 0:512:4],
                                 in1=sf[:, 1:512:4], op=ALU.min)
                dv.tensor_tensor(out=pqc[:, 256:384], in0=sf[:, 2:512:4],
                                 in1=sf[:, 3:512:4], op=ALU.max)
                dv.tensor_tensor(out=pqc[:, 384:512], in0=sf[:, 2:512:4],
                                 in1=sf[:, 3:512:4], op=ALU.min)
                dv.drain()
                dv.tensor_tensor(out=m2c[:, 0:128], in0=pqc[:, 0:128],
                                 in1=pqc[:, 256:384], op=ALU.min)
                dv.tensor_tensor(out=m2c[:, 128:256], in0=pqc[:, 128:256],
                                 in1=pqc[:, 384:512], op=ALU.max)
                dv.drain()
                dv.tensor_tensor(out=m2c[:, 128:256], in0=m2c[:, 0:128],
                                 in1=m2c[:, 128:256], op=ALU.max)
                dv.drain()
                for ph in range(4):
                    dv.tensor_tensor(out=gec[:, ph * 128:(ph + 1) * 128],
                                     in0=sf[:, ph:512:4],
                                     in1=m2c[:, 128:256], op=ALU.is_ge)
                dv.drain()
                for ph in range(4):
                    ins = dv.tensor_tensor(out=h_sp[cur][:, ph:512:4],
                                           in0=gec[:, ph * 128:(ph + 1) * 128],
                                           in1=h_f[cur][:, ph:512:4], op=ALU.mult)
                b.inc("dv", ins)
            t_sp[cur] = b.cnt["dv"]

            @block.tensor
            def _(pe, cur=cur, bg=bg):
                b.wait(pe, "dv", t_sp[cur])
                b.wait(pe, "ac", t_silu[cur])
                b.wait(pe, "ac", t_hT[cur])
                for jj in range(4):
                    ins = pe.transpose(out=p.pbb[bg][:, jj * 128:(jj + 1) * 128],
                                       in_=h_sp[cur][:, jj * 128:(jj + 1) * 128],
                                       identity=p.id_b)
                b.inc("pe", ins)
            t_tr[cur] = b.cnt["pe"]

            @block.scalar
            def _(ac, cur=cur, bg=bg):
                b.wait(ac, "pe", t_tr[cur])
                b.wait(ac, "pe", t_dn[cur])
                b.inc("ac", ac.activation(out=hT_b[cur], in_=p.pbb[bg][:, 0:512],
                                          func=ACTF.Copy))
            t_hT[cur] = b.cnt["ac"]

            @block.tensor
            def _(pe, cur=cur, sl=sl):
                b.wait(pe, "ac", t_hT[cur])
                b.wait(pe, "ac", t_ye)
                for dcol in range(4):
                    for hc in range(4):
                        ins = pe.matmul(
                            out=p.pb[2 + dcol][:],
                            lhsT=hT_b[cur][:, hc * 128:(hc + 1) * 128],
                            rhs=Wd_s[sl][:, hc * D + dcol * 512: hc * D + (dcol + 1) * 512],
                            start=(hc == 0), stop=(hc == 3))
                    b.inc("pe", ins)
            t_dn[cur] = b.cnt["pe"]

            @block.scalar
            def _(ac, ib=ib, cur=cur):
                b.wait(ac, "pe", t_dn[cur])
                b.wait(ac, f"SY{cur}", y_tok[cur])
                if ib == 0:
                    b.wait(ac, f"LX{cur}", t_lx[cur])
                for dcol in range(4):
                    ins = ac.activation(out=y_t[cur][:, dcol * 512:(dcol + 1) * 512],
                                        in_=p.pb[2 + dcol][:], func=ACTF.Copy)
                b.inc("ac", ins)
            t_ye = b.cnt["ac"]

            if ib == 0:
                @block.vector
                def _(dv, cur=cur):
                    b.wait(dv, "ac", t_ye)
                    for dcol in range(4):
                        ins = dv.tensor_tensor(
                            out=y_t[cur][:, dcol * 512:(dcol + 1) * 512],
                            in0=y_t[cur][:, dcol * 512:(dcol + 1) * 512],
                            in1=x3_t[cur][:, dcol * 512:(dcol + 1) * 512],
                            op=ALU.add)
                    b.inc("dv", ins)
                tok_resid = b.cnt["dv"]

            @block.gpsimd
            def _(gp, ib=ib, i=i, cur=cur):
                if ib == 0:
                    b.wait(gp, "dv", tok_resid)
                else:
                    b.wait(gp, "ac", t_ye)
                ch = f"SY{cur}"
                if b.cnt.get(ch, 0) >= 48:
                    b.wait(gp, ch, b.cnt[ch] - 32)
                if ib == 0:
                    ins = gp.dma_start(out=p.y[i * 128:(i + 1) * 128, :],
                                       in_=y_t[cur])
                else:
                    ins = gp.dma_start(out=p.y[i * 128:(i + 1) * 128, :],
                                       in_=y_t[cur], accum_op=ALU.add)
                b.inc(ch, ins, 16)
                y_tok[cur] = b.cnt[ch]

            if i == 4 and ib + 1 < NB:
                tok_w[(ib + 1) % 2] = load_band(ib + 1, last_u_tok)

    ar.pop()


def preload_set6(ac):
    ac.add_instruction(mybir.InstLoadActFuncSet(
        name=ac.bass.get_next_instruction_name(),
        act_func_set_id=6, ins=[], outs=[]))


def instr_last(eng):
    return eng.nop()



def build(phases=("p0a",), taps=None):
    nc = bass.Bass("TRN2", target_bir_lowering=False)
    es = contextlib.ExitStack()
    p = Prog(nc, es, taps)
    with nc.Block() as block:
        emit_prologue(p, block)
        if "p0a" in phases:
            @block.scalar
            def _(ac):
                ac.preload_act_table(ACTF.Sqrt)
            emit_p0a(p, block)
            p.b.barrier(block)
        if "p0b" in phases:
            @block.scalar
            def _(ac):
                preload_set6(ac)
            emit_p0b(p, block)
            p.b.barrier(block)
        if "p1" in phases:
            @block.scalar
            def _(ac):
                ac.preload_act_table(ACTF.Silu)
            emit_p1(p, block)
            p.b.barrier(block)
        if "p2" in phases:
            @block.scalar
            def _(ac):
                ac.preload_act_table(ACTF.Sqrt)
            emit_p2p(p, block)
            p.b.barrier(block)
            emit_p2a(p, block)
            p.b.barrier(block)
            @block.scalar
            def _(ac):
                preload_set6(ac)
            emit_p2b(p, block)
            p.b.barrier(block)
        if "p3" in phases:
            @block.scalar
            def _(ac):
                ac.preload_act_table(ACTF.Silu)
            emit_p3(p, block)
            p.b.barrier(block)
        for sec in (block.tensor, block.vector, block.scalar, block.gpsimd, block.sync):
            @sec
            def _(eng):
                for ch in sorted(p.b.cnt):
                    p.b.wait(eng, ch)
    es.close()
    return nc, p


ALL_PHASES = ("p0a", "p0b", "p1", "p2", "p3")


def kernel(**inputs):
    consts = host_consts()
    nc, _ = build(phases=ALL_PHASES)
    in_maps = []
    for core in range(8):
        d = host_prepare(inputs, core)
        d.update(consts)
        in_maps.append(d)
    res = run_bass_kernel_spmd(nc, in_maps, core_ids=list(range(8)))
    out = np.stack([np.asarray(r["y"], dtype=np.float32) for r in res.results], 0)
    return out


# version 09
# speedup vs baseline: 1.0128x; 1.0128x over previous
import contextlib
import numpy as np
import ml_dtypes
import concourse.bass as bass
import concourse.mybir as mybir
from concourse.bass_utils import run_bass_kernel_spmd

F32 = mybir.dt.float32
BF16 = mybir.dt.bfloat16
I32 = mybir.dt.int32
U8 = mybir.dt.uint8
AX = mybir.AxisListType
ALU = mybir.AluOpType
ACTF = mybir.ActivationFunctionType

T = 4096
D = 2048
NT = T // 128
DC = D // 128
H = 128
NQH, NKH = 16, 8
ATTN_SCALE = 0.08838834764831845
WINDOW = 1024
HID0 = 5632
HID1 = 8192
SEG = 512
NSEG = T // SEG
NH1 = 16
EPS_LN = 1e-5
EPS_RMS = 1e-6
NEG = -1.0e30
SERIAL = False

ENG_NAME = {}


def _engname(eng):
    return ENG_NAME[id(eng)]


class Bld:

    def __init__(self, nc, es):
        self.nc = nc
        self.es = es
        self.sem = {}
        self.cnt = {}
        self.lastwait = {}
        for ch in ("pe", "dv", "ac", "gp"):
            self._mk(ch)
        ENG_NAME[id(nc.tensor)] = "pe"
        ENG_NAME[id(nc.vector)] = "dv"
        ENG_NAME[id(nc.scalar)] = "ac"
        ENG_NAME[id(nc.gpsimd)] = "gp"
        ENG_NAME[id(nc.sync)] = "sp"

    def _mk(self, ch):
        self.sem[ch] = self.es.enter_context(self.nc.semaphore(f"ch_{ch}"))
        self.cnt[ch] = 0

    def inc(self, ch, instr, k=1):
        if ch not in self.sem:
            self._mk(ch)
        instr.then_inc(self.sem[ch], k)
        self.cnt[ch] += k
        return self.cnt[ch]

    def wait(self, eng, ch, val=None):
        if ch not in self.cnt:
            return
        if val is None:
            val = self.cnt[ch]
        if val <= 0:
            return
        key = (_engname(eng), ch)
        if self.lastwait.get(key, -1) >= val:
            return
        eng.wait_ge(self.sem[ch], val)
        self.lastwait[key] = val

    def hard(self, eng):
        if SERIAL:
            for ch in ("pe", "dv", "ac"):
                self.wait(eng, ch)

    def barrier(self, block):
        for sec in (block.tensor, block.vector, block.scalar, block.gpsimd, block.sync):
            @sec
            def _(eng):
                for ch in sorted(self.cnt):
                    self.wait(eng, ch)


class Arena:

    def __init__(self, nc, nbytes):
        self.t_u8 = nc.alloc_sbuf_tensor("arena", [128, nbytes], U8)
        self.views = {
            F32: self.t_u8.bitcast(F32),
            BF16: self.t_u8.bitcast(BF16),
            I32: self.t_u8.bitcast(I32),
        }
        self.nbytes = nbytes
        self.off = 0
        self.marks = []

    def alloc(self, shape, dtype):
        parts, free = shape
        esz = {F32: 4, BF16: 2, I32: 4}[dtype]
        off = (self.off + 31) // 32 * 32
        self.off = off + free * esz
        assert self.off <= self.nbytes, f"SBUF arena overflow: {self.off} > {self.nbytes}"
        v = self.views[dtype]
        ap = v[0:parts, off // esz: off // esz + free]
        return ap

    def push(self):
        self.marks.append(self.off)

    def pop(self):
        self.off = self.marks.pop()

    def union(self, *shapes):
        base = (self.off + 31) // 32 * 32
        aps = []
        end = base
        for (parts, free), dtype in shapes:
            esz = {F32: 4, BF16: 2, I32: 4}[dtype]
            v = self.views[dtype]
            aps.append(v[0:parts, base // esz: base // esz + free])
            end = max(end, base + free * esz)
        self.off = end
        assert self.off <= self.nbytes, "SBUF arena overflow (union)"
        return aps



def host_consts():
    c = {}
    c["cid_f"] = np.eye(128, dtype=np.float32)
    c["cid_b"] = np.eye(128, dtype=ml_dtypes.bfloat16)
    pr = np.zeros((128, 128), np.float32)
    for k in range(128):
        pr[k, (k - 64) % 128] = 1.0
    c["cprot"] = pr
    masks = np.full((12, 128, 512), NEG, np.float32)
    tt = np.arange(128)[:, None]
    ss = np.arange(512)[None, :]
    for idx in range(12):
        dd = tt - ss + 128 * idx
        masks[idx][(dd >= 0) & (dd < WINDOW)] = 0.0
    c["cmask"] = masks.astype(ml_dtypes.bfloat16)
    frac = np.arange(0, H, 2, dtype=np.float64) / H
    timescale = 10000.0 ** frac
    pos = np.arange(T, dtype=np.float64)
    sinusoid = pos[None, :] / timescale[:, None]
    cosF = np.cos(sinusoid)
    sinF = np.sin(sinusoid)
    cos128 = np.concatenate([cosF, cosF], 0)
    sin128 = np.concatenate([-sinF, sinF], 0)
    cos4 = np.empty((128, NT * 512), np.float32)
    sin4 = np.empty((128, NT * 512), np.float32)
    for i in range(NT):
        ct = cos128[:, i * 128:(i + 1) * 128]
        st = sin128[:, i * 128:(i + 1) * 128]
        cos4[:, i * 512:(i + 1) * 512] = np.tile(ct, (1, 4))
        sin4[:, i * 512:(i + 1) * 512] = np.tile(st, (1, 4))
    c["ccos4"] = cos4
    c["csin4"] = sin4
    memz0 = np.zeros((128, NH1 * 129), np.float32)
    for n in range(NH1):
        memz0[:, n * 129 + 128] = 1.0 / 128.0
    c["cmemz0"] = memz0
    cpool = np.zeros((128, 8), np.float32)
    cpool[:, 0] = EPS_LN
    cpool[:, 1] = EPS_RMS
    cpool[:, 2] = 1.0
    c["ccpool"] = cpool
    return c


def host_prepare(inputs, core):
    f32 = np.float32
    d = {}
    d["x0"] = np.ascontiguousarray(inputs["L0_attn_x"][core]).astype(f32, copy=False)
    d["w_q0"] = np.ascontiguousarray(
        inputs["L0_attn_q_kernel"].transpose(1, 0, 2).reshape(D, NQH * H))
    d["w_k0"] = np.ascontiguousarray(
        inputs["L0_attn_kv_kernel"][0].transpose(1, 0, 2).reshape(D, NKH * H))
    d["w_v0"] = np.ascontiguousarray(
        inputs["L0_attn_kv_kernel"][1].transpose(1, 0, 2).reshape(D, NKH * H))
    d["w_o0"] = np.ascontiguousarray(inputs["L0_attn_out_kernel"].reshape(NQH * H, D))
    d["qns"] = np.ascontiguousarray(inputs["L0_attn_q_norm_scale"].reshape(H, 1))
    d["kns"] = np.ascontiguousarray(inputs["L0_attn_k_norm_scale"].reshape(H, 1))
    d["g0w"] = np.ascontiguousarray(inputs["L0_ffn_wq_gate"])
    d["g0s"] = np.ascontiguousarray(inputs["L0_ffn_s_gate"].reshape(HID0 // 128, 128).T)
    d["u0w"] = np.ascontiguousarray(inputs["L0_ffn_wq_up"])
    d["u0s"] = np.ascontiguousarray(inputs["L0_ffn_s_up"].reshape(HID0 // 128, 128).T)
    d["d0w"] = np.ascontiguousarray(inputs["L0_ffn_wq_down"])
    d["d0s"] = np.ascontiguousarray(inputs["L0_ffn_s_down"].reshape(DC, 128).T)
    d["w_q1"] = np.ascontiguousarray(inputs["L1_attn_Wq"])
    d["w_k1"] = np.ascontiguousarray(inputs["L1_attn_Wk"])
    d["w_v1"] = np.ascontiguousarray(inputs["L1_attn_Wv"])
    d["w_o1"] = np.ascontiguousarray(inputs["L1_attn_Wo"])
    d["betasT"] = np.ascontiguousarray(inputs["L1_attn_betas"][0, :, 0, :].T)
    d["g1w"] = np.ascontiguousarray(inputs["L1_ffn_w_gate"])
    d["u1w"] = np.ascontiguousarray(inputs["L1_ffn_w_up"])
    d["d1w"] = np.ascontiguousarray(inputs["L1_ffn_w_down"])
    return d



class Prog:
    def __init__(self, nc, es, taps):
        self.nc = nc
        self.es = es
        self.taps = taps or {}
        self.b = Bld(nc, es)
        dt = nc.dram_tensor
        self.x0 = dt("x0", [T, D], F32, kind="ExternalInput").ap()
        self.w_q0 = dt("w_q0", [D, NQH * H], F32, kind="ExternalInput").ap()
        self.w_k0 = dt("w_k0", [D, NKH * H], F32, kind="ExternalInput").ap()
        self.w_v0 = dt("w_v0", [D, NKH * H], F32, kind="ExternalInput").ap()
        self.w_o0 = dt("w_o0", [NQH * H, D], F32, kind="ExternalInput").ap()
        self.qns = dt("qns", [H, 1], F32, kind="ExternalInput").ap()
        self.kns = dt("kns", [H, 1], F32, kind="ExternalInput").ap()
        self.g0w = dt("g0w", [HID0, D], I32, kind="ExternalInput").ap()
        self.g0s = dt("g0s", [128, HID0 // 128], F32, kind="ExternalInput").ap()
        self.u0w = dt("u0w", [HID0, D], I32, kind="ExternalInput").ap()
        self.u0s = dt("u0s", [128, HID0 // 128], F32, kind="ExternalInput").ap()
        self.d0w = dt("d0w", [D, HID0], I32, kind="ExternalInput").ap()
        self.d0s = dt("d0s", [128, DC], F32, kind="ExternalInput").ap()
        self.w_q1 = dt("w_q1", [D, D], F32, kind="ExternalInput").ap()
        self.w_k1 = dt("w_k1", [D, D], F32, kind="ExternalInput").ap()
        self.w_v1 = dt("w_v1", [D, D], F32, kind="ExternalInput").ap()
        self.w_o1 = dt("w_o1", [D, D], F32, kind="ExternalInput").ap()
        self.betasT = dt("betasT", [H, NH1], F32, kind="ExternalInput").ap()
        self.g1w = dt("g1w", [D, HID1], F32, kind="ExternalInput").ap()
        self.u1w = dt("u1w", [D, HID1], F32, kind="ExternalInput").ap()
        self.d1w = dt("d1w", [HID1, D], F32, kind="ExternalInput").ap()
        self.cid_f = dt("cid_f", [128, 128], F32, kind="ExternalInput").ap()
        self.cid_b = dt("cid_b", [128, 128], BF16, kind="ExternalInput").ap()
        self.cprot = dt("cprot", [128, 128], F32, kind="ExternalInput").ap()
        self.cmask = dt("cmask", [12, 128, 512], BF16, kind="ExternalInput").ap()
        self.ccos4 = dt("ccos4", [128, NT * 512], F32, kind="ExternalInput").ap()
        self.csin4 = dt("csin4", [128, NT * 512], F32, kind="ExternalInput").ap()
        self.cmemz0 = dt("cmemz0", [128, NH1 * 129], F32, kind="ExternalInput").ap()
        self.ccpool = dt("ccpool", [128, 8], F32, kind="ExternalInput").ap()
        self.y = dt("y", [T, D], F32, kind="ExternalOutput").ap()
        self.s_xA = dt("s_xA", [T, D], F32, kind="Internal").ap()
        self.s_xB = dt("s_xB", [T, D], F32, kind="Internal").ap()
        self.s_xhatT = dt("s_xhatT", [DC, 128, T], BF16, kind="Internal").ap()
        self.s_qT0 = dt("s_qT0", [NQH, 128, T], BF16, kind="Internal").ap()
        self.s_kT0 = dt("s_kT0", [NKH, 128, T], BF16, kind="Internal").ap()
        self.s_v0 = dt("s_v0", [T, NKH * H], BF16, kind="Internal").ap()
        self.s_qT1 = dt("s_qT1", [NH1, 128, T], F32, kind="Internal").ap()
        self.s_kT1 = dt("s_kT1", [NH1, 128, T], F32, kind="Internal").ap()
        self.s_v1 = dt("s_v1", [T, D], F32, kind="Internal").ap()
        self.arena = Arena(nc, 188 * 1024)
        self.id_f = self.arena.alloc([128, 128], F32)
        self.id_b = self.arena.alloc([128, 128], BF16)
        self.prot = self.arena.alloc([128, 128], F32)
        self.cpool_s = self.arena.alloc([128, 8], F32)
        for ci, cv in enumerate([EPS_LN, EPS_RMS, 1.0]):
            nc.const_aps.register(F32, cv, self.cpool_s[:, ci:ci + 1])
        self.pb = [nc.alloc_psum_tensor(f"pb{i}", [128, 512], F32) for i in range(8)]
        self.pbb = [p.bitcast(BF16) for p in self.pb]

    def tap(self, name, shape, dtype):
        if name in self.taps:
            return self.nc.dram_tensor("tap_" + name, shape, dtype,
                                       kind="ExternalOutput").ap()
        return None


def emit_prologue(p, block):
    b = p.b

    @block.sync
    def _(sp):
        b.inc("LC", sp.dma_start(out=p.id_f, in_=p.cid_f), 16)
        b.inc("LC", sp.dma_start(out=p.id_b, in_=p.cid_b), 16)
        b.inc("LC", sp.dma_start(out=p.prot, in_=p.cprot), 16)
        b.inc("LC", sp.dma_start(out=p.cpool_s, in_=p.ccpool), 16)


def emit_ln_xhatT(p, block, x_ap, xhat_b, xhatT_b, scr, tr_banks, out_f32=None,
                  expln=False):
    nc, b = p.nc, p.b

    @block.vector
    def _(dv):
        dv.tensor_reduce(out=scr["sum"], in_=x_ap, axis=AX.X, op=ALU.add)
        dv.scalar_tensor_tensor(out=scr["big"], in0=x_ap, scalar=1.0, in1=x_ap,
                                op0=ALU.bypass, op1=ALU.mult, accum_out=scr["ssq"])
        dv.drain()
        dv.tensor_scalar(out=scr["negmean"], in0=scr["sum"], scalar1=-1.0 / D,
                         scalar2=None, op0=ALU.mult)
        dv.drain()
        dv.tensor_tensor(out=scr["m2"], in0=scr["negmean"], in1=scr["negmean"],
                         op=ALU.mult)
        dv.tensor_scalar(out=scr["var"], in0=scr["ssq"], scalar1=1.0 / D,
                         scalar2=None, op0=ALU.mult)
        dv.drain()
        b.inc("dv", dv.tensor_tensor(out=scr["var"], in0=scr["var"], in1=scr["m2"],
                                     op=ALU.subtract))

    if expln:
        @block.scalar
        def _(ac):
            b.wait(ac, "dv")
            ac.activation(out=scr["std"], in_=scr["var"], func=ACTF.Ln,
                          bias=EPS_LN, scale=1.0)
            ac.drain()
            b.inc("ac", ac.activation(out=scr["rstd"], in_=scr["std"],
                                      func=ACTF.Exp, scale=-0.5))

        @block.vector
        def _(dv):
            b.wait(dv, "ac")
            b.inc("dv", dv.tensor_tensor(out=scr["lnb"], in0=scr["negmean"],
                                         in1=scr["rstd"], op=ALU.mult))
    else:
        @block.scalar
        def _(ac):
            b.wait(ac, "dv")
            b.inc("ac", ac.activation(out=scr["std"], in_=scr["var"], func=ACTF.Sqrt,
                                      bias=EPS_LN, scale=1.0))

        @block.vector
        def _(dv):
            b.wait(dv, "ac")
            b.inc("dv", dv.reciprocal(out=scr["rstd"], in_=scr["std"]))
            dv.drain()
            b.inc("dv", dv.tensor_tensor(out=scr["lnb"], in0=scr["negmean"],
                                         in1=scr["rstd"], op=ALU.mult))

    @block.scalar
    def _(ac):
        b.wait(ac, "dv")
        b.inc("ac", ac.activation(out=xhat_b, in_=x_ap, func=ACTF.Identity,
                                  bias=scr["lnb"], scale=scr["rstd"]))
        if out_f32 is not None:
            b.inc("ac", ac.activation(out=out_f32, in_=x_ap, func=ACTF.Identity,
                                      bias=scr["lnb"], scale=scr["rstd"]))
    tok_xhat = b.cnt["ac"]

    for g in range(4):
        bank = tr_banks[g % 2]
        pbank = p.pbb[bank]

        @block.tensor
        def _(pe, g=g, bank=bank, pbank=pbank):
            b.wait(pe, "ac", tok_xhat)
            if g >= 2:
                b.wait(pe, "ac")
            for j in range(4):
                c = 4 * g + j
                instr = pe.transpose(out=pbank[:, j * 128:(j + 1) * 128],
                                     in_=xhat_b[:, c * 128:(c + 1) * 128],
                                     identity=p.id_b)
            b.inc("pe", instr)

        @block.scalar
        def _(ac, g=g, pbank=pbank):
            b.wait(ac, "pe")
            b.inc("ac", ac.activation(out=xhatT_b[:, g * 512:(g + 1) * 512],
                                      in_=pbank[:, 0:512], func=ACTF.Copy))


def ln_scratch(p):
    scr = {}
    scr["big"] = p.arena.alloc([128, D], F32)
    for k in ("sum", "ssq", "negmean", "m2", "var", "std", "rstd", "lnb"):
        scr[k] = p.arena.alloc([128, 1], F32)
    return scr



def emit_p0a(p, block):
    nc, b, ar = p.nc, p.b, p.arena
    ar.push()
    FQ = NQH * H
    FK = NKH * H
    wq_s = ar.alloc([128, DC * FQ], BF16)
    wk_s = ar.alloc([128, DC * FK], BF16)
    wv_s = ar.alloc([128, DC * FK], BF16)
    qns_s = ar.alloc([128, 1], F32)
    kns_s = ar.alloc([128, 1], F32)
    qsc_s = ar.alloc([128, 1], F32)
    xf = [ar.alloc([128, D], F32) for _ in range(2)]
    rc_s = [ar.alloc([128, 512], F32) for _ in range(2)]
    rs_s = [ar.alloc([128, 512], F32) for _ in range(2)]
    xhat_b = ar.alloc([128, D], BF16)
    xhatT_b = ar.alloc([128, D], BF16)
    scr = ln_scratch(p)
    qn_f = scr["big"]
    kn_f = ar.alloc([128, FK], F32)
    v_b = ar.alloc([128, FK], BF16)
    ssq_q = ar.alloc([128, NQH], F32)
    ssq_k = ar.alloc([128, NKH], F32)
    rq_s = ar.alloc([128, NQH], F32)
    rk_s = ar.alloc([128, NKH], F32)
    qTf = ar.alloc([128, 512], F32)
    qTrot = ar.alloc([128, 512], F32)
    r1 = ar.alloc([128, 512], F32)
    r2 = ar.alloc([128, 512], F32)
    qT_out = [ar.alloc([128, 512], BF16) for _ in range(2)]

    tap_qT = p.tap("qT0", [NQH, 128, T], BF16)
    tap_v = p.tap("v0", [T, NKH * H], BF16)
    tap_xhatT = p.tap("xhatT_p0", [DC, 128, T], BF16)

    @block.sync
    def _(sp):
        b.inc("LC", sp.dma_start(out=qns_s, in_=p.qns), 16)
        b.inc("LC", sp.dma_start(out=kns_s, in_=p.kns), 16)
    tok_ns = b.cnt["LC"]

    @block.gpsimd
    def _(gp):
        for dk in range(DC):
            b.inc("LW", gp.dma_start(out=wq_s[:, dk * FQ:(dk + 1) * FQ],
                                     in_=p.w_q0[dk * 128:(dk + 1) * 128, :]), 16)
            b.inc("LW", gp.dma_start(out=wk_s[:, dk * FK:(dk + 1) * FK],
                                     in_=p.w_k0[dk * 128:(dk + 1) * 128, :]), 16)
            b.inc("LW", gp.dma_start(out=wv_s[:, dk * FK:(dk + 1) * FK],
                                     in_=p.w_v0[dk * 128:(dk + 1) * 128, :]), 16)
    tok_w = b.cnt["LW"]

    @block.vector
    def _(dv):
        b.wait(dv, "LC", tok_ns)
        b.inc("dv", dv.tensor_scalar(out=qsc_s, in0=qns_s, scalar1=ATTN_SCALE,
                                     scalar2=None, op0=ALU.mult))

    @block.sync
    def _(sp):
        b.inc("LT0", sp.dma_start(out=xf[0], in_=p.x0[0:128, :]), 16)
        b.inc("LT0", sp.dma_start(out=rc_s[0], in_=p.ccos4[:, 0:512]), 16)
        b.inc("LT0", sp.dma_start(out=rs_s[0], in_=p.csin4[:, 0:512]), 16)

    lt_tok = [b.cnt["LT0"], 0]
    cons_tok = [0, 0]

    for i in range(NT):
        cur, nxt = i % 2, (i + 1) % 2
        if i + 1 < NT:
            @block.sync
            def _(sp, i=i, nxt=nxt):
                ch = f"LT{nxt}"
                if i >= 1:
                    b.wait(sp, "dv", cons_tok[nxt])
                b.inc(ch, sp.dma_start(out=xf[nxt],
                                       in_=p.x0[(i + 1) * 128:(i + 2) * 128, :]), 16)
                b.inc(ch, sp.dma_start(out=rc_s[nxt],
                                       in_=p.ccos4[:, (i + 1) * 512:(i + 2) * 512]), 16)
                b.inc(ch, sp.dma_start(out=rs_s[nxt],
                                       in_=p.csin4[:, (i + 1) * 512:(i + 2) * 512]), 16)
                lt_tok[nxt] = b.cnt[ch]

        @block.vector
        def _(dv, cur=cur):
            b.wait(dv, f"LT{cur}", lt_tok[cur])
            b.wait(dv, "pe")

        emit_ln_xhatT(p, block, xf[cur], xhat_b, xhatT_b, scr, (6, 7))
        tok_xhatT = b.cnt["ac"]

        @block.tensor
        def _(pe, i=i):
            b.wait(pe, "ac", tok_xhatT)
            if i == 0:
                b.wait(pe, "LW", tok_w)
            for j in range(4):
                for dk in range(DC):
                    instr = pe.matmul(out=p.pb[j][:],
                                      lhsT=xhatT_b[:, dk * 128:(dk + 1) * 128],
                                      rhs=wq_s[:, dk * FQ + j * 512: dk * FQ + (j + 1) * 512],
                                      start=(dk == 0), stop=(dk == DC - 1))
                b.inc("pe", instr)
            for j in range(2):
                for dk in range(DC):
                    instr = pe.matmul(out=p.pb[4 + j][:],
                                      lhsT=xhatT_b[:, dk * 128:(dk + 1) * 128],
                                      rhs=wk_s[:, dk * FK + j * 512: dk * FK + (j + 1) * 512],
                                      start=(dk == 0), stop=(dk == DC - 1))
                b.inc("pe", instr)
            for j in range(2):
                for dk in range(DC):
                    instr = pe.matmul(out=p.pb[6 + j][:],
                                      lhsT=xhatT_b[:, dk * 128:(dk + 1) * 128],
                                      rhs=wv_s[:, dk * FK + j * 512: dk * FK + (j + 1) * 512],
                                      start=(dk == 0), stop=(dk == DC - 1))
                b.inc("pe", instr)
        tok_projq = b.cnt["pe"] - 4
        tok_projk = b.cnt["pe"] - 2
        tok_projv = b.cnt["pe"]

        @block.scalar
        def _(ac):
            b.wait(ac, "pe", tok_projq)
            ins = None
            for hh in range(NQH):
                bank, col = hh // 4, (hh % 4) * 128
                ins = ac.activation(
                    out=scr["big"][:, hh * 128:(hh + 1) * 128],
                    in_=p.pb[bank][:, col:col + 128], func=ACTF.Square,
                    accum_out=ssq_q[:, hh:hh + 1])
            b.inc("ac", ins)
            b.wait(ac, "pe", tok_projk)
            for hh in range(NKH):
                bank, col = 4 + hh // 4, (hh % 4) * 128
                ins = ac.activation(
                    out=kn_f[:, hh * 128:(hh + 1) * 128],
                    in_=p.pb[bank][:, col:col + 128], func=ACTF.Square,
                    accum_out=ssq_k[:, hh:hh + 1])
            b.inc("ac", ins)
        tok_ssq = b.cnt["ac"]

        @block.scalar
        def _(ac):
            b.wait(ac, "pe", tok_projv)
            b.wait(ac, "SV")
            ac.activation(out=v_b[:, 0:512], in_=p.pb[6][:], func=ACTF.Copy)
            b.inc("ac", ac.activation(out=v_b[:, 512:1024], in_=p.pb[7][:],
                                      func=ACTF.Copy))
        tok_vevac = b.cnt["ac"]

        @block.gpsimd
        def _(gp, i=i):
            b.wait(gp, "ac", tok_vevac)
            b.inc("SV", gp.dma_start(out=p.s_v0[i * 128:(i + 1) * 128, :], in_=v_b), 16)
            if tap_v is not None:
                b.inc("SV", gp.dma_start(out=tap_v[i * 128:(i + 1) * 128, :], in_=v_b), 16)

        @block.scalar
        def _(ac):
            b.wait(ac, "ac", tok_ssq)
            ac.drain()
            ac.activation(out=rq_s, in_=ssq_q, func=ACTF.Sqrt,
                          bias=EPS_RMS, scale=1.0 / H)
            b.inc("ac", ac.activation(out=rk_s, in_=ssq_k, func=ACTF.Sqrt,
                                      bias=EPS_RMS, scale=1.0 / H))

        @block.vector
        def _(dv):
            b.wait(dv, "ac")
            dv.reciprocal(out=rq_s, in_=rq_s)
            b.inc("dv", dv.reciprocal(out=rk_s, in_=rk_s))
        tok_rms = b.cnt["dv"]

        @block.scalar
        def _(ac):
            b.wait(ac, "dv", tok_rms)
            for hh in range(NQH):
                bank, col = hh // 4, (hh % 4) * 128
                instr = ac.activation(out=qn_f[:, hh * 128:(hh + 1) * 128],
                                      in_=p.pb[bank][:, col:col + 128],
                                      func=ACTF.Identity, scale=rq_s[:, hh:hh + 1])
            for hh in range(NKH):
                bank, col = 4 + hh // 4, (hh % 4) * 128
                instr = ac.activation(out=kn_f[:, hh * 128:(hh + 1) * 128],
                                      in_=p.pb[bank][:, col:col + 128],
                                      func=ACTF.Identity, scale=rk_s[:, hh:hh + 1])
            b.inc("ac", instr)
        tok_rmsevac = b.cnt["ac"]

        for g in range(6):
            is_q = g < 4
            src = qn_f if is_q else kn_f
            goff = g * 512 if is_q else (g - 4) * 512
            sc = qsc_s if is_q else kns_s
            bank_t, bank_r = (0, 1) if g % 2 == 0 else (2, 3)

            @block.tensor
            def _(pe, src=src, goff=goff, bank_t=bank_t):
                b.wait(pe, "ac", tok_rmsevac)
                b.wait(pe, "dv")
                for j in range(4):
                    instr = pe.transpose(out=p.pb[bank_t][:, j * 128:(j + 1) * 128],
                                         in_=src[:, goff + j * 128: goff + (j + 1) * 128],
                                         identity=p.id_f)
                b.inc("pe", instr)
            tok_tr = b.cnt["pe"]

            @block.scalar
            def _(ac, bank_t=bank_t, sc=sc):
                b.wait(ac, "pe", tok_tr)
                b.inc("ac", ac.activation(out=qTf, in_=p.pb[bank_t][:],
                                          func=ACTF.Identity, scale=sc))
            tok_qTf = b.cnt["ac"]

            @block.tensor
            def _(pe, bank_r=bank_r):
                b.wait(pe, "ac", tok_qTf)
                b.inc("pe", pe.matmul(out=p.pb[bank_r][:], lhsT=p.prot, rhs=qTf,
                                      start=True, stop=True))
            tok_rot = b.cnt["pe"]

            @block.scalar
            def _(ac, bank_r=bank_r):
                b.wait(ac, "pe", tok_rot)
                b.inc("ac", ac.activation(out=qTrot, in_=p.pb[bank_r][:],
                                          func=ACTF.Copy))
            tok_rot_e = b.cnt["ac"]

            gpar = g % 2

            @block.vector
            def _(dv, cur=cur, gpar=gpar):
                b.wait(dv, "ac", tok_rot_e)
                b.wait(dv, f"LT{cur}", lt_tok[cur])
                b.wait(dv, f"GQ{gpar}")
                dv.tensor_tensor(out=r1, in0=qTf, in1=rc_s[cur], op=ALU.mult)
                dv.tensor_tensor(out=r2, in0=qTrot, in1=rs_s[cur], op=ALU.mult)
                dv.drain()
                b.inc("dv", dv.tensor_tensor(out=qT_out[gpar], in0=r1, in1=r2,
                                             op=ALU.add))
            tok_rope = b.cnt["dv"]

            @block.gpsimd
            def _(gp, i=i, g=g, is_q=is_q, gpar=gpar):
                b.wait(gp, "dv", tok_rope)
                dst = p.s_qT0 if is_q else p.s_kT0
                h0 = 4 * g if is_q else 4 * (g - 4)
                b.inc(f"GQ{gpar}", gp.dma_start(
                    out=dst[h0:h0 + 4, :, i * 128:(i + 1) * 128].rearrange("h p t -> p h t"),
                    in_=qT_out[gpar].rearrange("p (h t) -> p h t", h=4)), 16)
                if is_q and tap_qT is not None:
                    b.inc(f"GQ{gpar}", gp.dma_start(
                        out=tap_qT[h0:h0 + 4, :, i * 128:(i + 1) * 128].rearrange("h p t -> p h t"),
                        in_=qT_out[gpar].rearrange("p (h t) -> p h t", h=4)), 16)
            @block.scalar
            def _(ac):
                b.wait(ac, "dv", tok_rope)

        if tap_xhatT is not None:
            @block.sync
            def _(sp, i=i):
                b.wait(sp, "ac", tok_xhatT)
                b.inc("ST", sp.dma_start(
                    out=tap_xhatT[:, :, i * 128:(i + 1) * 128].rearrange("c p t -> p c t"),
                    in_=xhatT_b.rearrange("p (c t) -> p c t", c=DC)), 16)
            @block.scalar
            def _(ac):
                b.wait(ac, "ST")

        cons_tok[cur] = b.cnt["dv"]

    ar.pop()



def emit_p0b(p, block):
    nc, b, ar = p.nc, p.b, p.arena
    ar.push()
    wo_s = ar.alloc([128, DC * D], BF16)
    mask_s = ar.alloc([128, 12 * 512], BF16)
    kr = ar.alloc([128, 3 * NKH * 512], BF16)
    vr = ar.alloc([128, 3 * 4 * 1024], BF16)
    qT4 = [ar.alloc([128, NQH * 128], BF16) for _ in range(4)]
    x0_t = ar.alloc([128, D], F32)
    probs_all4, x1_t = ar.union(
        ([128, 4 * 1536], BF16),
        ([128, D], F32))
    probs_all = [probs_all4[:, j * 1536:(j + 1) * 1536] for j in range(4)]
    pslot = [ar.alloc([128, 512], BF16) for _ in range(2)]
    attT_s = ar.alloc([128, NQH * 512], BF16)
    xhat_b2, xhatT_b2 = ar.union(([128, 2 * D], BF16), ([128, 2 * D], BF16))
    xhat_b = xhat_b2[:, 0:D]
    xhatT_b = xhatT_b2[:, D:2 * D]
    scr = {}
    scr["big"] = xhat_b2.bitcast(F32)[:, 0:D]
    for k in ("sum", "ssq", "negmean", "m2", "var", "std", "rstd", "lnb"):
        scr[k] = ar.alloc([128, 1], F32)
    nm3 = ar.alloc([128, 4], F32)
    nm = ar.alloc([128, 1], F32)
    lb = ar.alloc([128, 4], F32)
    lcol = ar.alloc([128, 1], F32)
    lnl = ar.alloc([128, 1], F32)
    bias2 = ar.alloc([128, 1], F32)

    tap_x1 = p.tap("x1", [T, D], F32)
    tap_dbg = p.tap("dbg0", [NSEG, 4, 128, 8], F32)
    tap_pr = p.tap("probs2", [4, 128, 1536], BF16)

    @block.gpsimd
    def _(gp):
        for c in range(DC):
            b.inc("LW", gp.dma_start(out=wo_s[:, c * D:(c + 1) * D],
                                     in_=p.w_o0[c * 128:(c + 1) * 128, :]), 16)
    tok_w = b.cnt["LW"]

    @block.sync
    def _(sp):
        for mi in range(12):
            b.inc("LM", sp.dma_start(
                out=mask_s[:, mi * 512:(mi + 1) * 512], in_=p.cmask[mi]), 16)
    tok_mask = b.cnt["LM"]

    SB = ((0, 1, 2), (3, 4, 5))
    BT, BA = 6, 7
    relq_tok = [0, 0]

    for B in range(NSEG):
        chunks = [c for c in (B - 2, B - 1, B) if c >= 0]
        nch = len(chunks)

        @block.sync
        def _(sp, B=B):
            b.wait(sp, "pe")
            slot = B % 3
            b.inc(f"KR{slot}", sp.dma_start(
                out=kr[:, slot * NKH * 512:(slot + 1) * NKH * 512]
                    .rearrange("p (h t) -> p h t", h=NKH),
                in_=p.s_kT0[:, :, B * 512:(B + 1) * 512].rearrange("h p t -> p h t")), 16)
            b.inc(f"VR{slot}", sp.dma_start(
                out=vr[:, slot * 4096:(slot + 1) * 4096]
                    .rearrange("p (a n) -> p a n", a=4),
                in_=p.s_v0[B * 512:(B + 1) * 512, :]
                    .rearrange("(a p) n -> p a n", p=128)), 16)
            for j in range(4):
                b.inc("LQ", sp.dma_start(
                    out=qT4[j].rearrange("p (h t) -> p h t", h=NQH),
                    in_=p.s_qT0[:, :, (4 * B + j) * 128:(4 * B + j + 1) * 128]
                        .rearrange("h p t -> p h t")), 16)
        tok_ring = {c: b.cnt[f"KR{c % 3}"] for c in chunks}
        tok_vring = {c: b.cnt[f"VR{c % 3}"] for c in chunks}
        tok_lq = b.cnt["LQ"]

        for n in range(NQH):
            kvn = n // 2
            tok_pass2 = [0, 0, 0, 0]
            for j in range(4):
                S = SB[j % 2]

                @block.tensor
                def _(pe, n=n, j=j, S=S, kvn=kvn, B=B, chunks=chunks):
                    b.hard(pe)
                    if n == 0 and j < 2:
                        b.wait(pe, "ac")
                        b.wait(pe, "dv")
                        b.wait(pe, "LW", tok_w)
                    b.wait(pe, "ac", relq_tok[j % 2])
                    b.wait(pe, "LQ", tok_lq)
                    for ci, c in enumerate(chunks):
                        b.wait(pe, f"KR{c % 3}", tok_ring[c])
                        ins = pe.matmul(
                            out=p.pb[S[ci]][:],
                            lhsT=qT4[j][:, n * 128:(n + 1) * 128],
                            rhs=kr[:, (c % 3) * NKH * 512 + kvn * 512:
                                   (c % 3) * NKH * 512 + (kvn + 1) * 512],
                            start=True, stop=True)
                    b.inc("pe", ins)
                tok_sc = b.cnt["pe"]

                @block.vector
                def _(dv, j=j, S=S, B=B, chunks=chunks, nch=nch):
                    b.hard(dv)
                    b.wait(dv, "pe", tok_sc)
                    b.wait(dv, "LM", tok_mask)
                    ins = None
                    for ci, c in enumerate(chunks):
                        idx = j + 4 * (B - c)
                        if 4 <= idx <= 7:
                            continue
                        ins = dv.tensor_tensor(
                            out=p.pb[S[ci]][:], in0=p.pb[S[ci]][:],
                            in1=mask_s[:, idx * 512:(idx + 1) * 512], op=ALU.add)
                    if ins is not None:
                        dv.drain()
                    for ci in range(nch):
                        dv.reduce_max(out=nm3[:, ci:ci + 1], in_=p.pb[S[ci]][:],
                                      axis=AX.X, negate=True)
                    dv.drain()
                    b.inc("dv", dv.tensor_reduce(out=nm, in_=nm3[:, 0:nch],
                                                 axis=AX.X, op=ALU.min))
                tok_nm = b.cnt["dv"]

                @block.scalar
                def _(ac, j=j, S=S, nch=nch):
                    b.hard(ac)
                    b.wait(ac, "dv", tok_nm)
                    b.wait(ac, "GT")
                    b.wait(ac, "GX")
                    for ci in range(nch):
                        ins = ac.activation(
                            out=probs_all[j][:, ci * 512:(ci + 1) * 512],
                            in_=p.pb[S[ci]][:], func=ACTF.Exp, bias=nm,
                            scale=1.0, accum_out=lb[:, ci:ci + 1])
                    b.inc("ac", ins)
                tok_l = b.cnt["ac"]

                @block.vector
                def _(dv, nch=nch):
                    b.hard(dv)
                    b.wait(dv, "ac", tok_l)
                    b.inc("dv", dv.tensor_reduce(out=lcol, in_=lb[:, 0:nch],
                                                 axis=AX.X, op=ALU.add))

                @block.scalar
                def _(ac):
                    b.hard(ac)
                    b.wait(ac, "dv")
                    b.inc("ac", ac.activation(out=lnl, in_=lcol, func=ACTF.Ln))

                @block.vector
                def _(dv):
                    b.hard(dv)
                    b.wait(dv, "ac")
                    b.inc("dv", dv.tensor_tensor(out=bias2, in0=nm, in1=lnl,
                                                 op=ALU.subtract))

                @block.scalar
                def _(ac, j=j, S=S, nch=nch):
                    b.hard(ac)
                    b.wait(ac, "dv")
                    for ci in range(nch):
                        ins = ac.activation(
                            out=probs_all[j][:, ci * 512:(ci + 1) * 512],
                            in_=p.pb[S[ci]][:], func=ACTF.Exp, bias=bias2,
                            scale=1.0)
                    b.inc("ac", ins)
                tok_pass2[j] = b.cnt["ac"]
                relq_tok[j % 2] = tok_pass2[j]

                if tap_dbg is not None and n == 0:
                    @block.sync
                    def _(sp, B=B, j=j, nch=nch):
                        b.wait(sp, "ac", tok_pass2[j])
                        b.wait(sp, "dv")
                        for col, buf in ((0, nm), (1, lcol), (2, lnl), (3, bias2)):
                            b.inc("ST", sp.dma_start(
                                out=tap_dbg[B, j, :, col:col + 1], in_=buf,
                                allow_slow_non_contiguous=True), 16)
                        b.inc("ST", sp.dma_start(
                            out=tap_dbg[B, j, :, 4:4 + nch], in_=lb[:, 0:nch],
                            allow_slow_non_contiguous=True), 16)
                        if tap_pr is not None and B == 2:
                            b.inc("ST", sp.dma_start(
                                out=tap_pr[j], in_=probs_all[j]), 16)

                    @block.vector
                    def _(dv):
                        b.wait(dv, "ST")

            nsb = 4 * nch
            for sb in range(nsb):
                ci, sub = sb // 4, sb % 4
                c = chunks[ci]

                @block.tensor
                def _(pe, ci=ci, sub=sub):
                    b.hard(pe)
                    b.wait(pe, "ac", max(tok_pass2))
                    b.wait(pe, "ac")
                    for j in range(4):
                        ins = pe.transpose(
                            out=p.pbb[BT][:, j * 128:(j + 1) * 128],
                            in_=probs_all[j][:, ci * 512 + sub * 128:
                                             ci * 512 + (sub + 1) * 128],
                            identity=p.id_b)
                    b.inc("pe", ins)
                tok_tr = b.cnt["pe"]

                @block.scalar
                def _(ac, sb=sb):
                    b.hard(ac)
                    b.wait(ac, "pe", tok_tr)
                    b.inc("ac", ac.activation(out=pslot[sb % 2],
                                              in_=p.pbb[BT][:, 0:512],
                                              func=ACTF.Copy))
                tok_ev = b.cnt["ac"]

                @block.tensor
                def _(pe, sb=sb, ci=ci, sub=sub, c=c, kvn=kvn, nsb=nsb):
                    b.hard(pe)
                    b.wait(pe, "ac", tok_ev)
                    b.wait(pe, f"VR{c % 3}", tok_vring[c])
                    b.inc("pe", pe.matmul(
                        out=p.pb[BA][:],
                        lhsT=vr[:, (c % 3) * 4096 + sub * 1024 + kvn * 128:
                                (c % 3) * 4096 + sub * 1024 + (kvn + 1) * 128],
                        rhs=pslot[sb % 2], start=(sb == 0), stop=(sb == nsb - 1)))
            tok_pv = b.cnt["pe"]

            @block.scalar
            def _(ac, n=n):
                b.hard(ac)
                b.wait(ac, "pe", tok_pv)
                b.inc("ac", ac.activation(out=attT_s[:, n * 512:(n + 1) * 512],
                                          in_=p.pb[BA][:], func=ACTF.Copy))

        tok_attT = b.cnt["ac"]

        for j in range(4):
            i = 4 * B + j

            @block.sync
            def _(sp, i=i):
                b.wait(sp, "dv")
                b.inc("LX", sp.dma_start(out=x0_t, in_=p.x0[i * 128:(i + 1) * 128, :]), 16)
            tok_x0 = b.cnt["LX"]

            @block.tensor
            def _(pe, j=j):
                b.wait(pe, "ac", tok_attT)
                b.wait(pe, "dv")
                for dcol in range(4):
                    for c in range(DC):
                        ins = pe.matmul(
                            out=p.pb[dcol][:],
                            lhsT=attT_s[:, c * 512 + j * 128: c * 512 + j * 128 + 128],
                            rhs=wo_s[:, c * D + dcol * 512: c * D + (dcol + 1) * 512],
                            start=(c == 0), stop=(c == DC - 1))
                    b.inc("pe", ins)
            tok_op = b.cnt["pe"]

            @block.vector
            def _(dv, j=j):
                b.wait(dv, "pe", tok_op)
                b.wait(dv, "LX", tok_x0)
                b.wait(dv, "GX")
                b.wait(dv, "ac")
                for dcol in range(4):
                    ins = dv.tensor_tensor(out=x1_t[:, dcol * 512:(dcol + 1) * 512],
                                           in0=p.pb[dcol][:],
                                           in1=x0_t[:, dcol * 512:(dcol + 1) * 512],
                                           op=ALU.add)
                b.inc("dv", ins)
            tok_x1 = b.cnt["dv"]

            @block.gpsimd
            def _(gp, i=i):
                b.wait(gp, "dv", tok_x1)
                b.inc("GX", gp.dma_start(out=p.s_xA[i * 128:(i + 1) * 128, :],
                                         in_=x1_t), 16)
                if tap_x1 is not None:
                    b.inc("GX", gp.dma_start(out=tap_x1[i * 128:(i + 1) * 128, :],
                                             in_=x1_t), 16)

            @block.vector
            def _(dv):
                dv.drain()
                b.wait(dv, "GT")

            @block.scalar
            def _(ac):
                b.wait(ac, "GT")

            emit_ln_xhatT(p, block, x1_t, xhat_b, xhatT_b, scr, (4, 5), expln=True)
            tok_lnT = b.cnt["ac"]

            @block.gpsimd
            def _(gp, i=i):
                b.wait(gp, "ac", tok_lnT)
                b.inc("GT", gp.dma_start(
                    out=p.s_xhatT[:, :, i * 128:(i + 1) * 128].rearrange("c p t -> p c t"),
                    in_=xhatT_b.rearrange("p (c t) -> p c t", c=DC)), 16)

    ar.pop()



def emit_p1(p, block):
    nc, b, ar = p.nc, p.b, p.arena
    ar.push()
    NB = HID0 // 512
    WgT = [ar.alloc([128, DC * 512], BF16) for _ in range(2)]
    WuT = [ar.alloc([128, DC * 512], BF16) for _ in range(2)]
    WdT = [ar.alloc([128, 4 * D], BF16) for _ in range(2)]
    wtmp = [ar.alloc([128, D], I32) for _ in range(2)]
    wdq = [ar.alloc([128, D], BF16) for _ in range(2)]
    sgu = [ar.alloc([128, 8], F32) for _ in range(2)]
    sd_s = ar.alloc([128, DC], F32)
    xh_t = [ar.alloc([128, D], BF16) for _ in range(2)]
    x1_t = [ar.alloc([128, D], F32) for _ in range(2)]
    g_s = [ar.alloc([128, 512], F32) for _ in range(2)]
    h_b = [ar.alloc([128, 512], BF16) for _ in range(2)]
    hT_b = [ar.alloc([128, 512], BF16) for _ in range(2)]
    y_t = [ar.alloc([128, D], F32) for _ in range(2)]
    y_tok = [0, 0]
    BG = (0, 6)
    BU = (1, 7)

    @block.sync
    def _(sp):
        b.inc("LC", sp.dma_start(out=sd_s, in_=p.d0s), 16)
    tok_sd = b.cnt["LC"]

    wprep_dv = [0, 0]
    wprep_tok = [0, 0]

    def prep_band(hb, gate_pe_tok):
        sl = hb % 2

        @block.sync
        def _(sp, hb=hb, sl=sl, gate_pe_tok=gate_pe_tok):
            if gate_pe_tok:
                b.wait(sp, "pe", gate_pe_tok)
            b.inc("LB", sp.dma_start(
                out=sgu[sl][:, 0:4], in_=p.g0s[:, hb * 4:(hb + 1) * 4]), 16)
            b.inc("LB", sp.dma_start(
                out=sgu[sl][:, 4:8], in_=p.u0s[:, hb * 4:(hb + 1) * 4]), 16)

        steps = []
        for mi, (wsrc, scol) in enumerate(((p.g0w, 0), (p.u0w, 4))):
            for q in range(4):
                steps.append(("gu", mi, q, wsrc, scol))
        for dk in range(DC):
            steps.append(("d", dk, None, None, None))

        for si, st in enumerate(steps):
            par = si % 2

            @block.sync
            def _(sp, st=st, hb=hb, par=par):
                b.wait(sp, "dv")
                if st[0] == "gu":
                    _, mi, q, wsrc, scol = st
                    b.inc("LB", sp.dma_start(
                        out=wtmp[par],
                        in_=wsrc[hb * 512 + q * 128: hb * 512 + (q + 1) * 128, :]), 16)
                else:
                    dk = st[1]
                    b.inc("LB", sp.dma_start(
                        out=wtmp[par][:, 0:512],
                        in_=p.d0w[dk * 128:(dk + 1) * 128, hb * 512:(hb + 1) * 512]), 16)
            tok_wt = b.cnt["LB"]

            @block.vector
            def _(dv, st=st, sl=sl, par=par):
                b.wait(dv, "LB", tok_wt)
                b.wait(dv, "LC", tok_sd)
                b.wait(dv, "LBT")
                if st[0] == "gu":
                    _, mi, q, wsrc, scol = st
                    ins = dv.tensor_scalar(
                        out=wdq[par], in0=wtmp[par], scalar1=-128.0,
                        scalar2=sgu[sl][:, scol + q: scol + q + 1],
                        op0=ALU.add, op1=ALU.mult)
                else:
                    dk = st[1]
                    ins = dv.tensor_scalar(
                        out=wdq[par][:, 0:512], in0=wtmp[par][:, 0:512],
                        scalar1=-128.0, scalar2=sd_s[:, dk:dk + 1],
                        op0=ALU.add, op1=ALU.mult)
                b.inc("dv", ins)
            tok_dq = b.cnt["dv"]

            @block.scalar
            def _(ac, st=st, sl=sl, par=par):
                b.wait(ac, "dv", tok_dq)
                if st[0] == "gu":
                    _, mi, q, _, _ = st
                    dstT = (WgT, WuT)[mi][sl]
                    for dk in range(DC):
                        b.inc("LBT", ac.dma_start(
                            out=dstT[:, dk * 512 + q * 128: dk * 512 + (q + 1) * 128],
                            in_=wdq[par][:, dk * 128:(dk + 1) * 128],
                            transpose=True), 16)
                else:
                    dk = st[1]
                    for a in range(4):
                        b.inc("LBT", ac.dma_start(
                            out=WdT[sl][:, a * D + dk * 128: a * D + (dk + 1) * 128],
                            in_=wdq[par][:, a * 128:(a + 1) * 128],
                            transpose=True), 16)
        wprep_tok[sl] = b.cnt["LBT"]

    prep_band(0, 0)

    t_silu = [0, 0]
    t_h = [0, 0]
    t_tr = [0, 0]
    t_hT = [0, 0]
    t_dn = [0, 0]
    t_gu = [0, 0]
    t_ye = 0
    t_lh = [0, 0]
    t_lx = [0, 0]
    last_u_tok = 0

    for hb in range(NB):
        sl = hb % 2
        for i in range(NT):
            cur = i % 2
            bg, bu = BG[cur], BU[cur]

            @block.sync
            def _(sp, hb=hb, i=i, cur=cur):
                b.wait(sp, "pe", t_gu[cur])
                b.inc(f"LH{cur}", sp.dma_start(
                    out=xh_t[cur].rearrange("p (c t) -> p c t", c=DC),
                    in_=p.s_xhatT[:, :, i * 128:(i + 1) * 128].rearrange("c p t -> p c t")), 16)
                if hb == 0:
                    b.wait(sp, "ac", t_ye)
                    b.inc(f"LX{cur}", sp.dma_start(
                        out=x1_t[cur], in_=p.s_xA[i * 128:(i + 1) * 128, :]), 16)
            t_lh[cur] = b.cnt[f"LH{cur}"]
            if hb == 0:
                t_lx[cur] = b.cnt[f"LX{cur}"]

            @block.tensor
            def _(pe, cur=cur, sl=sl, bg=bg, bu=bu):
                b.wait(pe, "LBT", wprep_tok[sl])
                b.wait(pe, f"LH{cur}", t_lh[cur])
                b.wait(pe, "ac", t_hT[cur])
                b.wait(pe, "dv", t_h[cur])
                for dk in range(DC):
                    ins = pe.matmul(out=p.pb[bg][:],
                                    lhsT=xh_t[cur][:, dk * 128:(dk + 1) * 128],
                                    rhs=WgT[sl][:, dk * 512:(dk + 1) * 512],
                                    start=(dk == 0), stop=(dk == DC - 1))
                b.inc("pe", ins)
                for dk in range(DC):
                    ins = pe.matmul(out=p.pb[bu][:],
                                    lhsT=xh_t[cur][:, dk * 128:(dk + 1) * 128],
                                    rhs=WuT[sl][:, dk * 512:(dk + 1) * 512],
                                    start=(dk == 0), stop=(dk == DC - 1))
                b.inc("pe", ins)
            tok_g = b.cnt["pe"] - 1
            t_gu[cur] = last_u_tok = b.cnt["pe"]

            @block.scalar
            def _(ac, cur=cur, bg=bg):
                b.wait(ac, "pe", tok_g)
                b.wait(ac, "dv", t_h[cur])
                b.inc("ac", ac.activation(out=g_s[cur], in_=p.pb[bg][:],
                                          func=ACTF.Silu))
            t_silu[cur] = b.cnt["ac"]

            @block.vector
            def _(dv, cur=cur, bu=bu):
                b.wait(dv, "pe", t_gu[cur])
                b.wait(dv, "ac", t_silu[cur])
                b.wait(dv, "pe", t_tr[cur])
                b.inc("dv", dv.tensor_tensor(out=h_b[cur], in0=g_s[cur],
                                             in1=p.pb[bu][:], op=ALU.mult))
            t_h[cur] = b.cnt["dv"]

            @block.tensor
            def _(pe, cur=cur, bg=bg):
                b.wait(pe, "dv", t_h[cur])
                b.wait(pe, "ac", t_silu[cur])
                b.wait(pe, "ac", t_hT[cur])
                for jj in range(4):
                    ins = pe.transpose(out=p.pbb[bg][:, jj * 128:(jj + 1) * 128],
                                       in_=h_b[cur][:, jj * 128:(jj + 1) * 128],
                                       identity=p.id_b)
                b.inc("pe", ins)
            t_tr[cur] = b.cnt["pe"]

            @block.scalar
            def _(ac, cur=cur, bg=bg):
                b.wait(ac, "pe", t_tr[cur])
                b.wait(ac, "pe", t_dn[cur])
                b.inc("ac", ac.activation(out=hT_b[cur], in_=p.pbb[bg][:, 0:512],
                                          func=ACTF.Copy))
            t_hT[cur] = b.cnt["ac"]

            @block.tensor
            def _(pe, cur=cur, sl=sl):
                b.wait(pe, "ac", t_hT[cur])
                b.wait(pe, "ac", t_ye)
                for dcol in range(4):
                    for hc in range(4):
                        ins = pe.matmul(
                            out=p.pb[2 + dcol][:],
                            lhsT=hT_b[cur][:, hc * 128:(hc + 1) * 128],
                            rhs=WdT[sl][:, hc * D + dcol * 512: hc * D + (dcol + 1) * 512],
                            start=(hc == 0), stop=(hc == 3))
                    b.inc("pe", ins)
            t_dn[cur] = b.cnt["pe"]

            @block.scalar
            def _(ac, hb=hb, cur=cur):
                b.wait(ac, "pe", t_dn[cur])
                b.wait(ac, f"SY{cur}", y_tok[cur])
                if hb == 0:
                    b.wait(ac, f"LX{cur}", t_lx[cur])
                for dcol in range(4):
                    ins = ac.activation(out=y_t[cur][:, dcol * 512:(dcol + 1) * 512],
                                        in_=p.pb[2 + dcol][:], func=ACTF.Copy)
                b.inc("ac", ins)
            t_ye = b.cnt["ac"]

            if hb == 0:
                @block.vector
                def _(dv, cur=cur):
                    b.wait(dv, "ac", t_ye)
                    for dcol in range(4):
                        ins = dv.tensor_tensor(
                            out=y_t[cur][:, dcol * 512:(dcol + 1) * 512],
                            in0=y_t[cur][:, dcol * 512:(dcol + 1) * 512],
                            in1=x1_t[cur][:, dcol * 512:(dcol + 1) * 512],
                            op=ALU.add)
                    b.inc("dv", ins)
                tok_resid = b.cnt["dv"]

            @block.gpsimd
            def _(gp, hb=hb, i=i, cur=cur):
                if hb == 0:
                    b.wait(gp, "dv", tok_resid)
                else:
                    b.wait(gp, "ac", t_ye)
                ch = f"SY{cur}"
                if b.cnt.get(ch, 0) >= 96:
                    b.wait(gp, ch, b.cnt[ch] - 80)
                if hb == 0:
                    ins = gp.dma_start(out=p.s_xB[i * 128:(i + 1) * 128, :],
                                       in_=y_t[cur])
                else:
                    ins = gp.dma_start(out=p.s_xB[i * 128:(i + 1) * 128, :],
                                       in_=y_t[cur], accum_op=ALU.add)
                b.inc(ch, ins, 16)
                y_tok[cur] = b.cnt[ch]

            if i == 4 and hb + 1 < NB:
                prep_band(hb + 1, last_u_tok)

    tap_x2 = p.tap("x2", [T, D], F32)
    if tap_x2 is not None:
        @block.sync
        def _(sp):
            for ch in ("SY0", "SY1"):
                b.wait(sp, ch)
            b.inc("ST", sp.dma_start(out=tap_x2, in_=p.s_xB), 16)

    ar.pop()



def emit_p2p(p, block):
    nc, b, ar = p.nc, p.b, p.arena
    ar.push()
    x2_t = [ar.alloc([128, D], F32) for _ in range(2)]
    xh2 = [ar.alloc([128, 2 * D], BF16) for _ in range(2)]
    scrs = []
    for par in range(2):
        scr = {"big": xh2[par].bitcast(F32)[:, 0:D]}
        for k in ("sum", "ssq", "negmean", "m2", "var", "std", "rstd", "lnb"):
            scr[k] = ar.alloc([128, 1], F32)
        scrs.append(scr)
    t_xhat = [0, 0]
    t_st = [0, 0]

    for i in range(NT):
        cur = i % 2

        @block.sync
        def _(sp, i=i, cur=cur):
            b.wait(sp, "ac", t_xhat[cur])
            b.inc(f"LP{cur}", sp.dma_start(out=x2_t[cur],
                                           in_=p.s_xB[i * 128:(i + 1) * 128, :]), 16)
        tok_x2 = b.cnt[f"LP{cur}"]

        @block.vector
        def _(dv, cur=cur):
            b.wait(dv, f"LP{cur}", tok_x2)
            b.wait(dv, f"ST{cur}", t_st[cur])

        @block.scalar
        def _(ac, cur=cur):
            b.wait(ac, f"LP{cur}", tok_x2)
            b.wait(ac, f"ST{cur}", t_st[cur])

        emit_ln_xhatT(p, block, x2_t[cur], xh2[cur][:, 0:D], xh2[cur][:, D:2 * D],
                      scrs[cur], (2 * cur, 2 * cur + 1))
        t_xhat[cur] = b.cnt["ac"]

        @block.scalar
        def _(ac, i=i, cur=cur):
            ac.drain()
            b.inc(f"ST{cur}", ac.dma_start(
                out=p.s_xhatT[:, :, i * 128:(i + 1) * 128].rearrange("c p t -> p c t"),
                in_=xh2[cur][:, D:2 * D].rearrange("p (c t) -> p c t", c=DC)), 16)
        t_st[cur] = b.cnt[f"ST{cur}"]
    ar.pop()


def emit_p2a(p, block):
    nc, b, ar = p.nc, p.b, p.arena
    ar.push()
    w_s = ar.alloc([128, DC * D], BF16)
    xhT_blk = [ar.alloc([128, DC * 512], BF16) for _ in range(2)]
    ev = [ar.alloc([128, 512], F32) for _ in range(4)]

    for wi, (wsrc, dstT) in enumerate(((p.w_q1, p.s_qT1), (p.w_k1, p.s_kT1))):
        @block.gpsimd
        def _(gp, wsrc=wsrc):
            b.wait(gp, "pe")
            for dk in range(DC):
                b.inc("LW", gp.dma_start(out=w_s[:, dk * D:(dk + 1) * D],
                                         in_=wsrc[dk * 128:(dk + 1) * 128, :]), 16)
        tok_w = b.cnt["LW"]

        for tb in range(NSEG):
            cur = tb % 2

            @block.sync
            def _(sp, tb=tb, cur=cur):
                b.wait(sp, "pe")
                b.inc(f"LG{cur}", sp.dma_start(
                    out=xhT_blk[cur].rearrange("p (c t) -> p c t", c=DC),
                    in_=p.s_xhatT[:, :, tb * 512:(tb + 1) * 512]
                        .rearrange("c p t -> p c t")), 16)
            tok_xh = b.cnt[f"LG{cur}"]

            for fc in range(DC):
                bank = fc % 2

                @block.tensor
                def _(pe, fc=fc, cur=cur, bank=bank):
                    b.wait(pe, "LW", tok_w)
                    b.wait(pe, f"LG{cur}", tok_xh)
                    b.wait(pe, "ac")
                    for dk in range(DC):
                        ins = pe.matmul(
                            out=p.pb[bank][:],
                            lhsT=w_s[:, dk * D + fc * 128: dk * D + (fc + 1) * 128],
                            rhs=xhT_blk[cur][:, dk * 512:(dk + 1) * 512],
                            start=(dk == 0), stop=(dk == DC - 1))
                    b.inc("pe", ins)
                tok_mm = b.cnt["pe"]

                @block.scalar
                def _(ac, fc=fc, tb=tb, bank=bank, dstT=dstT):
                    b.wait(ac, "pe", tok_mm)
                    b.wait(ac, f"SP{fc % 4}")
                    ins = ac.activation(out=ev[fc % 4], in_=p.pb[bank][:],
                                        func=ACTF.Copy)
                    b.inc("ac", ins)
                    ac.drain()
                    b.inc(f"SP{fc % 4}", ac.dma_start(
                        out=dstT[fc, :, tb * 512:(tb + 1) * 512], in_=ev[fc % 4]), 16)

    @block.gpsimd
    def _(gp):
        b.wait(gp, "pe")
        for dk in range(DC):
            b.inc("LW", gp.dma_start(out=w_s[:, dk * D:(dk + 1) * D],
                                     in_=p.w_v1[dk * 128:(dk + 1) * 128, :]), 16)
    tok_w = b.cnt["LW"]
    vout = [ar.alloc([128, D], F32) for _ in range(2)]

    for i in range(NT):
        cur = i % 2

        @block.sync
        def _(sp, i=i, cur=cur):
            b.wait(sp, "pe")
            b.inc(f"LG{cur}", sp.dma_start(
                out=xhT_blk[cur][:, 0:DC * 128].rearrange("p (c t) -> p c t", c=DC),
                in_=p.s_xhatT[:, :, i * 128:(i + 1) * 128].rearrange("c p t -> p c t")), 16)
        tok_xh = b.cnt[f"LG{cur}"]

        @block.tensor
        def _(pe, cur=cur):
            b.wait(pe, "LW", tok_w)
            b.wait(pe, f"LG{cur}", tok_xh)
            b.wait(pe, "SP0")
            b.wait(pe, "SP1")
            for vc in range(4):
                for dk in range(DC):
                    ins = pe.matmul(
                        out=p.pb[2 + vc][:],
                        lhsT=xhT_blk[cur][:, dk * 128:(dk + 1) * 128],
                        rhs=w_s[:, dk * D + vc * 512: dk * D + (vc + 1) * 512],
                        start=(dk == 0), stop=(dk == DC - 1))
                b.inc("pe", ins)
        tok_mm = b.cnt["pe"]

        @block.scalar
        def _(ac, i=i, cur=cur):
            b.wait(ac, "pe", tok_mm)
            b.wait(ac, f"SP{cur}")
            for vc in range(4):
                ac.activation(out=vout[cur][:, vc * 512:(vc + 1) * 512],
                              in_=p.pb[2 + vc][:], func=ACTF.Copy)
            ac.drain()
            b.inc(f"SP{cur}", ac.dma_start(out=p.s_v1[i * 128:(i + 1) * 128, :],
                                           in_=vout[cur]), 16)
    ar.pop()


def emit_p2b(p, block):
    nc, b, ar = p.nc, p.b, p.arena
    ar.push()
    C1 = ATTN_SCALE
    wo_s = ar.alloc([128, DC * D], BF16)
    mask1 = ar.alloc([128, 4 * 512], BF16)
    memz_f = ar.alloc([128, NH1 * 129], F32)
    memz_b = ar.alloc([128, NH1 * 129], BF16)
    beta_s = ar.alloc([128, NH1], F32)
    btmp = ar.alloc([128, NH1], F32)
    qT_h = [ar.alloc([128, 512], F32) for _ in range(2)]
    kT_h = [ar.alloc([128, 512], F32) for _ in range(2)]
    v_h = [ar.alloc([128, 512], F32) for _ in range(2)]
    v_hb = [ar.alloc([128, 512], BF16) for _ in range(2)]
    min0 = ar.alloc([128, 512], F32)
    e_s = ar.alloc([128, 512], F32)
    sigq_b = ar.alloc([128, 512], BF16)
    sigk_b = ar.alloc([128, 512], BF16)
    sigk_tok = ar.alloc([128, 512], BF16)
    probs_seg = ar.alloc([128, 4 * 512], BF16)
    pslot = [ar.alloc([128, 512], BF16) for _ in range(2)]
    attT_s = ar.alloc([128, NH1 * 512], BF16)
    delta_ones = ar.alloc([128, 4 * 132], BF16)
    rz_col = ar.alloc([128, 1], F32)
    nrz_col = ar.alloc([128, 1], F32)
    rz_row = ar.alloc([1, 512], F32)
    rzb_row = ar.alloc([1, 512], BF16)
    ones_b = ar.alloc([1, 128], BF16)
    brcp_s = ar.alloc([128, 512], F32)
    t1_s = ar.alloc([128, 512], F32)
    t2_s = ar.alloc([128, 512], F32)
    x2_t = ar.alloc([128, D], F32)
    x3_t = ar.alloc([128, D], F32)
    xh2 = ar.alloc([128, 2 * D], BF16)
    xhat_b = xh2[:, 0:D]
    xhatT_b = xh2[:, D:2 * D]
    scr = {"big": xh2.bitcast(F32)[:, 0:D]}
    for k in ("sum", "ssq", "negmean", "m2", "var", "std", "rstd", "lnb"):
        scr[k] = ar.alloc([128, 1], F32)
    nm3 = ar.alloc([128, 4], F32)
    nm = ar.alloc([128, 1], F32)
    bias1 = ar.alloc([128, 1], F32)
    lb = ar.alloc([128, 4], F32)
    lcol = ar.alloc([128, 1], F32)
    lnl = ar.alloc([128, 1], F32)
    bias2 = ar.alloc([128, 1], F32)

    tap_x3 = p.tap("x3", [T, D], F32)

    @block.gpsimd
    def _(gp):
        for c in range(DC):
            b.inc("LW", gp.dma_start(out=wo_s[:, c * D:(c + 1) * D],
                                     in_=p.w_o1[c * 128:(c + 1) * 128, :]), 16)

    tok_w = b.cnt["LW"]
    tok_mask = b.cnt["LM"]

    @block.sync
    def _(sp):
        for mi in range(4):
            b.inc("LM", sp.dma_start(
                out=mask1[:, mi * 512:(mi + 1) * 512], in_=p.cmask[mi]), 16)
        b.inc("LC", sp.dma_start(out=memz_f, in_=p.cmemz0), 16)
        b.inc("LC", sp.dma_start(out=btmp, in_=p.betasT), 16)
    tok_mz = b.cnt["LC"]
    tok_mask = b.cnt["LM"]

    @block.scalar
    def _(ac):
        b.wait(ac, "LC", tok_mz)
        b.inc("ac", ac.activation(out=beta_s, in_=btmp, func=ACTF.Exp, scale=-1.0))

    @block.vector
    def _(dv):
        b.wait(dv, "LC", tok_mz)
        b.wait(dv, "ac")
        b.inc("dv", dv.tensor_copy(out=memz_b, in_=memz_f))
        dv.tensor_scalar(out=beta_s, in0=beta_s, scalar1=1.0, scalar2=None,
                         op0=ALU.add)
        dv.drain()
        dv.reciprocal(out=beta_s, in_=beta_s)
        dv.memset(ones_b[:], 1.0)
        for sub in range(4):
            dv.memset(delta_ones[:, sub * 132 + 128: sub * 132 + 129], 1.0)
        dv.drain()
        b.inc("dv", dv.memset(delta_ones[:, 0:1], 0.0))

    qTb = [ar.alloc([128, 512], BF16) for _ in range(2)]
    kTb = [ar.alloc([128, 512], BF16) for _ in range(2)]
    relq_tok = [0, 0]

    for s in range(NSEG):
        for n in range(NH1):
            cur = n % 2

            @block.sync
            def _(sp, s=s, n=n, cur=cur):
                b.wait(sp, "pe")
                b.wait(sp, "dv")
                b.inc(f"LA{cur}", sp.dma_start(
                    out=qT_h[cur], in_=p.s_qT1[n, :, s * 512:(s + 1) * 512]), 16)
                b.inc(f"LA{cur}", sp.dma_start(
                    out=kT_h[cur], in_=p.s_kT1[n, :, s * 512:(s + 1) * 512]), 16)
                b.inc(f"LA{cur}", sp.dma_start(
                    out=v_h[cur].rearrange("p (a n) -> p a n", a=4),
                    in_=p.s_v1[s * 512:(s + 1) * 512, n * 128:(n + 1) * 128]
                        .rearrange("(a p) n -> p a n", p=128)), 16)
            tok_la = b.cnt[f"LA{cur}"]

            @block.gpsimd
            def _(gp, s=s, n=n, cur=cur):
                b.wait(gp, "pe")
                b.inc(f"LV{cur}", gp.dma_start(
                    out=v_hb[cur].rearrange("p (a n) -> p a n", a=4),
                    in_=p.s_v1[s * 512:(s + 1) * 512, n * 128:(n + 1) * 128]
                        .rearrange("(a p) n -> p a n", p=128)), 16)
                b.inc(f"LV{cur}", gp.dma_start(
                    out=qTb[cur], in_=p.s_qT1[n, :, s * 512:(s + 1) * 512]), 16)
                b.inc(f"LV{cur}", gp.dma_start(
                    out=kTb[cur], in_=p.s_kT1[n, :, s * 512:(s + 1) * 512]), 16)
            tok_lv = b.cnt[f"LV{cur}"]

            @block.vector
            def _(dv, cur=cur):
                b.wait(dv, f"LA{cur}", tok_la)
                b.wait(dv, "ac")
                b.wait(dv, "pe")
                b.inc("dv", dv.tensor_scalar(out=min0, in0=qT_h[cur], scalar1=0.0,
                                             scalar2=None, op0=ALU.min))

            @block.scalar
            def _(ac):
                b.wait(ac, "dv")
                b.inc("ac", ac.activation(out=e_s, in_=min0, func=ACTF.Exp))

            @block.vector
            def _(dv, cur=cur):
                b.wait(dv, "ac")
                b.inc("dv", dv.scalar_tensor_tensor(
                    out=sigq_b, in0=qT_h[cur], scalar=0.0, in1=e_s,
                    op0=ALU.max, op1=ALU.add))
                b.inc("dv", dv.tensor_scalar(out=min0, in0=kT_h[cur], scalar1=0.0,
                                             scalar2=None, op0=ALU.min))

            @block.scalar
            def _(ac):
                b.wait(ac, "dv")
                b.inc("ac", ac.activation(out=e_s, in_=min0, func=ACTF.Exp))

            @block.vector
            def _(dv, cur=cur):
                b.wait(dv, "ac")
                b.inc("dv", dv.scalar_tensor_tensor(
                    out=sigk_b, in0=kT_h[cur], scalar=0.0, in1=e_s,
                    op0=ALU.max, op1=ALU.add))
            tok_sig = b.cnt["dv"]

            @block.tensor
            def _(pe):
                b.wait(pe, "dv", tok_sig)
                b.wait(pe, "ac")
                for sub in range(4):
                    ins = pe.transpose(out=p.pbb[2][:, sub * 128:(sub + 1) * 128],
                                       in_=sigk_b[:, sub * 128:(sub + 1) * 128],
                                       identity=p.id_b)
                b.inc("pe", ins)
            tok_str = b.cnt["pe"]

            @block.scalar
            def _(ac):
                b.wait(ac, "pe", tok_str)
                b.wait(ac, "pe")
                b.inc("ac", ac.activation(out=sigk_tok, in_=p.pbb[2][:, 0:512],
                                          func=ACTF.Copy))
            tok_stok = b.cnt["ac"]

            tok_pass2 = [0, 0, 0, 0]
            for j in range(4):
                bank = j % 2

                @block.tensor
                def _(pe, j=j, cur=cur, bank=bank):
                    b.wait(pe, f"LV{cur}", tok_lv)
                    b.wait(pe, "ac", relq_tok[bank])
                    b.inc("pe", pe.matmul(out=p.pb[bank][:],
                                          lhsT=qTb[cur][:, j * 128:(j + 1) * 128],
                                          rhs=kTb[cur], start=True, stop=True))
                tok_sc = b.cnt["pe"]

                @block.vector
                def _(dv, j=j, bank=bank):
                    b.wait(dv, "pe", tok_sc)
                    b.wait(dv, "LM", tok_mask)
                    dv.tensor_tensor(out=p.pb[bank][:], in0=p.pb[bank][:],
                                     in1=mask1[:, j * 512:(j + 1) * 512],
                                     op=ALU.add)
                    dv.drain()
                    dv.reduce_max(out=nm, in_=p.pb[bank][:], axis=AX.X, negate=True)
                    dv.drain()
                    b.inc("dv", dv.tensor_scalar(out=bias1, in0=nm, scalar1=C1,
                                                 scalar2=None, op0=ALU.mult))
                tok_nm = b.cnt["dv"]

                @block.scalar
                def _(ac, j=j, bank=bank):
                    b.wait(ac, "dv", tok_nm)
                    b.inc("ac", ac.activation(
                        out=probs_seg[:, j * 512:(j + 1) * 512],
                        in_=p.pb[bank][:], func=ACTF.Exp, bias=bias1,
                        scale=C1, accum_out=lcol))
                tok_l = b.cnt["ac"]

                @block.scalar
                def _(ac):
                    b.wait(ac, "ac", tok_l)
                    ac.drain()
                    b.inc("ac", ac.activation(out=lnl, in_=lcol, func=ACTF.Ln))

                @block.vector
                def _(dv):
                    b.wait(dv, "ac")
                    b.inc("dv", dv.tensor_tensor(out=bias2, in0=bias1, in1=lnl,
                                                 op=ALU.subtract))

                @block.scalar
                def _(ac, j=j, bank=bank):
                    b.wait(ac, "dv")
                    b.inc("ac", ac.activation(
                        out=probs_seg[:, j * 512:(j + 1) * 512],
                        in_=p.pb[bank][:], func=ACTF.Exp, bias=bias2, scale=C1))
                tok_pass2[j] = b.cnt["ac"]
                relq_tok[bank] = tok_pass2[j]


            for sb in range(4):
                @block.tensor
                def _(pe, sb=sb):
                    b.wait(pe, "ac", max(tok_pass2))
                    b.wait(pe, "ac")
                    for j in range(4):
                        ins = pe.transpose(
                            out=p.pbb[2][:, j * 128:(j + 1) * 128],
                            in_=probs_seg[:, j * 512 + sb * 128: j * 512 + (sb + 1) * 128],
                            identity=p.id_b)
                    b.inc("pe", ins)
                tok_tr = b.cnt["pe"]

                @block.scalar
                def _(ac, sb=sb):
                    b.wait(ac, "pe", tok_tr)
                    b.inc("ac", ac.activation(out=pslot[sb % 2],
                                              in_=p.pbb[2][:, 0:512],
                                              func=ACTF.Copy))
                tok_ev = b.cnt["ac"]

                @block.tensor
                def _(pe, sb=sb, cur=cur):
                    b.wait(pe, "ac", tok_ev)
                    if sb == 0:
                        b.wait(pe, "dv")
                    b.inc("pe", pe.matmul(
                        out=p.pb[3][:],
                        lhsT=v_hb[cur][:, sb * 128:(sb + 1) * 128],
                        rhs=pslot[sb % 2], start=(sb == 0), stop=(sb == 3)))
            tok_pv = b.cnt["pe"]

            @block.tensor
            def _(pe, n=n):
                b.wait(pe, "ac")
                b.wait(pe, "dv", tok_sig)
                b.inc("pe", pe.matmul(out=p.pb[4][:],
                                      lhsT=memz_b[:, n * 129: n * 129 + 128],
                                      rhs=sigq_b, start=True, stop=True))
                b.inc("pe", pe.matmul(out=p.pb[5][0:1, :],
                                      lhsT=memz_b[:, n * 129 + 128: n * 129 + 129],
                                      rhs=sigq_b, start=True, stop=True))
            tok_ro = b.cnt["pe"]

            @block.vector
            def _(dv):
                b.wait(dv, "pe", tok_ro)
                dv.reciprocal(out=rz_row, in_=p.pb[5][0:1, :])
                dv.drain()
                b.inc("dv", dv.tensor_copy(out=rzb_row, in_=rz_row))
            tok_rz = b.cnt["dv"]

            @block.tensor
            def _(pe):
                b.wait(pe, "dv", tok_rz)
                b.inc("pe", pe.matmul(out=p.pb[5][:], lhsT=ones_b, rhs=rzb_row,
                                      start=True, stop=True))
            tok_bc = b.cnt["pe"]

            @block.scalar
            def _(ac):
                b.wait(ac, "pe", tok_bc)
                b.inc("ac", ac.activation(out=brcp_s, in_=p.pb[5][:],
                                          func=ACTF.Copy))
            tok_brcp = b.cnt["ac"]

            for sub in range(4):
                @block.tensor
                def _(pe, n=n, sub=sub):
                    b.wait(pe, "ac", tok_stok)
                    b.wait(pe, "dv")
                    b.inc("pe", pe.matmul(
                        out=p.pb[6][:, 0:129],
                        lhsT=sigk_b[:, sub * 128:(sub + 1) * 128],
                        rhs=memz_b[:, n * 129:(n + 1) * 129],
                        start=True, stop=True))
                tok_R = b.cnt["pe"]

                @block.vector
                def _(dv, sub=sub, cur=cur):
                    b.wait(dv, "pe", tok_R)
                    b.wait(dv, f"LA{cur}", tok_la)
                    dv.reciprocal(out=rz_col, in_=p.pb[6][:, 128:129])
                    dv.drain()
                    dv.tensor_scalar(out=nrz_col, in0=rz_col, scalar1=-1.0,
                                     scalar2=None, op0=ALU.mult)
                    dv.drain()
                    b.inc("dv", dv.scalar_tensor_tensor(
                        out=delta_ones[:, sub * 132: sub * 132 + 128],
                        in0=p.pb[6][:, 0:128], scalar=nrz_col,
                        in1=v_h[cur][:, sub * 128:(sub + 1) * 128],
                        op0=ALU.mult, op1=ALU.add))
                tok_delta = b.cnt["dv"]

                @block.tensor
                def _(pe, sub=sub):
                    b.wait(pe, "dv", tok_delta)
                    b.inc("pe", pe.matmul(
                        out=p.pb[7][:, 0:129],
                        lhsT=sigk_tok[:, sub * 128:(sub + 1) * 128],
                        rhs=delta_ones[:, sub * 132: sub * 132 + 129],
                        start=(sub == 0), stop=(sub == 3)))
            tok_upd = b.cnt["pe"]

            @block.vector
            def _(dv, n=n):
                b.wait(dv, "pe", tok_upd)
                b.inc("dv", dv.tensor_tensor(
                    out=memz_f[:, n * 129:(n + 1) * 129],
                    in0=memz_f[:, n * 129:(n + 1) * 129],
                    in1=p.pb[7][:, 0:129], op=ALU.add))
            tok_mzf = b.cnt["dv"]

            @block.scalar
            def _(ac, n=n):
                b.wait(ac, "dv", tok_mzf)
                b.inc("ac", ac.activation(out=memz_b[:, n * 129:(n + 1) * 129],
                                          in_=memz_f[:, n * 129:(n + 1) * 129],
                                          func=ACTF.Copy))

            att_scr = attT_s.rearrange("p (c j a bb) -> p c j a bb",
                                       c=16, j=4, a=4)

            @block.vector
            def _(dv, n=n):
                b.wait(dv, "ac", tok_brcp)
                b.wait(dv, "pe", tok_pv)
                dv.tensor_tensor(out=t1_s, in0=p.pb[4][:], in1=brcp_s, op=ALU.mult)
                dv.drain()
                dv.tensor_tensor(out=t2_s, in0=t1_s, in1=p.pb[3][:], op=ALU.subtract)
                dv.drain()
                out_ap = att_scr[:, :, n // 4, n % 4, :].rearrange("p c b -> p b c")
                b.inc("dv", dv.scalar_tensor_tensor(
                    out=out_ap,
                    in0=t2_s, scalar=beta_s[:, n:n + 1], in1=p.pb[3][:],
                    op0=ALU.mult, op1=ALU.add))
            tok_att = b.cnt["dv"]

        for j in range(4):
            i = 4 * s + j

            @block.sync
            def _(sp, i=i):
                b.wait(sp, "dv")
                b.inc("LX2", sp.dma_start(out=x2_t,
                                          in_=p.s_xB[i * 128:(i + 1) * 128, :]), 16)
            tok_x2 = b.cnt["LX2"]

            @block.tensor
            def _(pe, j=j):
                b.wait(pe, "LW", tok_w)
                b.wait(pe, "dv", tok_att)
                b.wait(pe, "dv")
                b.wait(pe, "ac")
                for dcol in range(4):
                    for c in range(DC):
                        ins = pe.matmul(
                            out=p.pb[dcol][:],
                            lhsT=attT_s[:, c * 512 + j * 128: c * 512 + j * 128 + 128],
                            rhs=wo_s[:, c * D + dcol * 512: c * D + (dcol + 1) * 512],
                            start=(c == 0), stop=(c == DC - 1))
                    b.inc("pe", ins)
            tok_op = b.cnt["pe"]

            @block.vector
            def _(dv, j=j):
                b.wait(dv, "pe", tok_op)
                b.wait(dv, "LX2", tok_x2)
                b.wait(dv, "GX")
                for dcol in range(4):
                    ins = dv.tensor_tensor(out=x3_t[:, dcol * 512:(dcol + 1) * 512],
                                           in0=p.pb[dcol][:],
                                           in1=x2_t[:, dcol * 512:(dcol + 1) * 512],
                                           op=ALU.add)
                b.inc("dv", ins)
            tok_x3 = b.cnt["dv"]

            @block.gpsimd
            def _(gp, i=i):
                b.wait(gp, "dv", tok_x3)
                b.inc("GX", gp.dma_start(out=p.s_xA[i * 128:(i + 1) * 128, :],
                                         in_=x3_t), 16)
                if tap_x3 is not None:
                    b.inc("GX", gp.dma_start(out=tap_x3[i * 128:(i + 1) * 128, :],
                                             in_=x3_t), 16)
            tok_sx = b.cnt["GX"]

            @block.vector
            def _(dv):
                dv.drain()
                b.wait(dv, "GT")

            @block.scalar
            def _(ac):
                b.wait(ac, "GT")
                b.wait(ac, "GX", tok_sx)

            emit_ln_xhatT(p, block, x3_t, xhat_b, xhatT_b, scr, (4, 5), expln=True)
            tok_lnT = b.cnt["ac"]

            @block.gpsimd
            def _(gp, i=i):
                b.wait(gp, "ac", tok_lnT)
                b.inc("GT", gp.dma_start(
                    out=p.s_xhatT[:, :, i * 128:(i + 1) * 128].rearrange("c p t -> p c t"),
                    in_=xhatT_b.rearrange("p (c t) -> p c t", c=DC)), 16)

    ar.pop()



def emit_p3(p, block):
    nc, b, ar = p.nc, p.b, p.arena
    ar.push()
    NB = HID1 // 512
    Wg_s = [ar.alloc([128, DC * 512], BF16) for _ in range(2)]
    Wu_s = [ar.alloc([128, DC * 512], BF16) for _ in range(2)]
    Wd_s = [ar.alloc([128, 4 * D], BF16) for _ in range(2)]
    xh_t = [ar.alloc([128, D], BF16) for _ in range(2)]
    x3_t = [ar.alloc([128, D], F32) for _ in range(2)]
    g_s = [ar.alloc([128, 512], F32) for _ in range(2)]
    h_f = [ar.alloc([128, 512], F32) for _ in range(2)]
    s_f = [ar.alloc([128, 512], F32) for _ in range(2)]
    pq = [ar.alloc([128, 512], F32) for _ in range(2)]
    m2 = [ar.alloc([128, 256], F32) for _ in range(2)]
    geb = [ar.alloc([128, 512], F32) for _ in range(2)]
    h_sp = [ar.alloc([128, 512], BF16) for _ in range(2)]
    hT_b = [ar.alloc([128, 512], BF16) for _ in range(2)]
    y_t = [ar.alloc([128, D], F32) for _ in range(2)]
    y_tok = [0, 0]
    BG = (0, 6)
    BU = (1, 7)

    def load_band(ib, gate_tok):
        @block.gpsimd
        def _(gp, ib=ib, gate_tok=gate_tok):
            if gate_tok:
                b.wait(gp, "pe", gate_tok)
            sl = ib % 2
            for dk in range(DC):
                b.inc("LW", gp.dma_start(
                    out=Wg_s[sl][:, dk * 512:(dk + 1) * 512],
                    in_=p.g1w[dk * 128:(dk + 1) * 128, ib * 512:(ib + 1) * 512]), 16)
                b.inc("LW", gp.dma_start(
                    out=Wu_s[sl][:, dk * 512:(dk + 1) * 512],
                    in_=p.u1w[dk * 128:(dk + 1) * 128, ib * 512:(ib + 1) * 512]), 16)
            for a in range(4):
                b.inc("LW", gp.dma_start(
                    out=Wd_s[sl][:, a * D:(a + 1) * D],
                    in_=p.d1w[ib * 512 + a * 128: ib * 512 + (a + 1) * 128, :]), 16)
        return b.cnt["LW"]

    tok_w = [0, 0]
    tok_w[0] = load_band(0, 0)
    t_silu = [0, 0]
    t_h = [0, 0]
    t_sq = [0, 0]
    t_sp = [0, 0]
    t_tr = [0, 0]
    t_hT = [0, 0]
    t_dn = [0, 0]
    t_gu = [0, 0]
    t_ye = 0
    t_lh = [0, 0]
    t_lx = [0, 0]
    last_u_tok = 0

    for ib in range(NB):
        sl = ib % 2
        for i in range(NT):
            cur = i % 2
            bg, bu = BG[cur], BU[cur]

            @block.sync
            def _(sp, ib=ib, i=i, cur=cur):
                b.wait(sp, "pe", t_gu[cur])
                b.inc(f"LH{cur}", sp.dma_start(
                    out=xh_t[cur].rearrange("p (c t) -> p c t", c=DC),
                    in_=p.s_xhatT[:, :, i * 128:(i + 1) * 128].rearrange("c p t -> p c t")), 16)
                if ib == 0:
                    b.wait(sp, "ac", t_ye)
                    b.inc(f"LX{cur}", sp.dma_start(
                        out=x3_t[cur], in_=p.s_xA[i * 128:(i + 1) * 128, :]), 16)
            t_lh[cur] = b.cnt[f"LH{cur}"]
            if ib == 0:
                t_lx[cur] = b.cnt[f"LX{cur}"]

            @block.tensor
            def _(pe, cur=cur, sl=sl, bg=bg, bu=bu):
                b.wait(pe, "LW", tok_w[sl])
                b.wait(pe, f"LH{cur}", t_lh[cur])
                b.wait(pe, "ac", t_hT[cur])
                b.wait(pe, "dv", t_h[cur])
                for dk in range(DC):
                    ins = pe.matmul(out=p.pb[bg][:],
                                    lhsT=xh_t[cur][:, dk * 128:(dk + 1) * 128],
                                    rhs=Wg_s[sl][:, dk * 512:(dk + 1) * 512],
                                    start=(dk == 0), stop=(dk == DC - 1))
                b.inc("pe", ins)
                for dk in range(DC):
                    ins = pe.matmul(out=p.pb[bu][:],
                                    lhsT=xh_t[cur][:, dk * 128:(dk + 1) * 128],
                                    rhs=Wu_s[sl][:, dk * 512:(dk + 1) * 512],
                                    start=(dk == 0), stop=(dk == DC - 1))
                b.inc("pe", ins)
            tok_g = b.cnt["pe"] - 1
            t_gu[cur] = last_u_tok = b.cnt["pe"]

            @block.scalar
            def _(ac, cur=cur, bg=bg):
                b.wait(ac, "pe", tok_g)
                b.wait(ac, "dv", t_h[cur])
                b.inc("ac", ac.activation(out=g_s[cur], in_=p.pb[bg][:],
                                          func=ACTF.Silu))
            t_silu[cur] = b.cnt["ac"]

            @block.vector
            def _(dv, cur=cur, bu=bu):
                b.wait(dv, "pe", t_gu[cur])
                b.wait(dv, "ac", t_silu[cur])
                b.wait(dv, "ac", t_sq[cur])
                b.inc("dv", dv.tensor_tensor(out=h_f[cur], in0=g_s[cur],
                                             in1=p.pb[bu][:], op=ALU.mult))
            t_h[cur] = b.cnt["dv"]

            @block.scalar
            def _(ac, cur=cur):
                b.wait(ac, "dv", t_h[cur])
                b.inc("ac", ac.activation(out=s_f[cur], in_=h_f[cur],
                                          func=ACTF.Square))
            t_sq[cur] = b.cnt["ac"]

            @block.vector
            def _(dv, cur=cur):
                b.wait(dv, "ac", t_sq[cur])
                b.wait(dv, "pe", t_tr[cur])
                sf, pqc, m2c, gec = s_f[cur], pq[cur], m2[cur], geb[cur]
                dv.tensor_tensor(out=pqc[:, 0:128], in0=sf[:, 0:512:4],
                                 in1=sf[:, 1:512:4], op=ALU.max)
                dv.tensor_tensor(out=pqc[:, 128:256], in0=sf[:, 0:512:4],
                                 in1=sf[:, 1:512:4], op=ALU.min)
                dv.tensor_tensor(out=pqc[:, 256:384], in0=sf[:, 2:512:4],
                                 in1=sf[:, 3:512:4], op=ALU.max)
                dv.tensor_tensor(out=pqc[:, 384:512], in0=sf[:, 2:512:4],
                                 in1=sf[:, 3:512:4], op=ALU.min)
                dv.drain()
                dv.tensor_tensor(out=m2c[:, 0:128], in0=pqc[:, 0:128],
                                 in1=pqc[:, 256:384], op=ALU.min)
                dv.tensor_tensor(out=m2c[:, 128:256], in0=pqc[:, 128:256],
                                 in1=pqc[:, 384:512], op=ALU.max)
                dv.drain()
                dv.tensor_tensor(out=m2c[:, 128:256], in0=m2c[:, 0:128],
                                 in1=m2c[:, 128:256], op=ALU.max)
                dv.drain()
                for ph in range(4):
                    dv.tensor_tensor(out=gec[:, ph * 128:(ph + 1) * 128],
                                     in0=sf[:, ph:512:4],
                                     in1=m2c[:, 128:256], op=ALU.is_ge)
                dv.drain()
                for ph in range(4):
                    ins = dv.tensor_tensor(out=h_sp[cur][:, ph:512:4],
                                           in0=gec[:, ph * 128:(ph + 1) * 128],
                                           in1=h_f[cur][:, ph:512:4], op=ALU.mult)
                b.inc("dv", ins)
            t_sp[cur] = b.cnt["dv"]

            @block.tensor
            def _(pe, cur=cur, bg=bg):
                b.wait(pe, "dv", t_sp[cur])
                b.wait(pe, "ac", t_silu[cur])
                b.wait(pe, "ac", t_hT[cur])
                for jj in range(4):
                    ins = pe.transpose(out=p.pbb[bg][:, jj * 128:(jj + 1) * 128],
                                       in_=h_sp[cur][:, jj * 128:(jj + 1) * 128],
                                       identity=p.id_b)
                b.inc("pe", ins)
            t_tr[cur] = b.cnt["pe"]

            @block.scalar
            def _(ac, cur=cur, bg=bg):
                b.wait(ac, "pe", t_tr[cur])
                b.wait(ac, "pe", t_dn[cur])
                b.inc("ac", ac.activation(out=hT_b[cur], in_=p.pbb[bg][:, 0:512],
                                          func=ACTF.Copy))
            t_hT[cur] = b.cnt["ac"]

            @block.tensor
            def _(pe, cur=cur, sl=sl):
                b.wait(pe, "ac", t_hT[cur])
                b.wait(pe, "ac", t_ye)
                for dcol in range(4):
                    for hc in range(4):
                        ins = pe.matmul(
                            out=p.pb[2 + dcol][:],
                            lhsT=hT_b[cur][:, hc * 128:(hc + 1) * 128],
                            rhs=Wd_s[sl][:, hc * D + dcol * 512: hc * D + (dcol + 1) * 512],
                            start=(hc == 0), stop=(hc == 3))
                    b.inc("pe", ins)
            t_dn[cur] = b.cnt["pe"]

            @block.scalar
            def _(ac, ib=ib, cur=cur):
                b.wait(ac, "pe", t_dn[cur])
                b.wait(ac, f"SY{cur}", y_tok[cur])
                if ib == 0:
                    b.wait(ac, f"LX{cur}", t_lx[cur])
                for dcol in range(4):
                    ins = ac.activation(out=y_t[cur][:, dcol * 512:(dcol + 1) * 512],
                                        in_=p.pb[2 + dcol][:], func=ACTF.Copy)
                b.inc("ac", ins)
            t_ye = b.cnt["ac"]

            if ib == 0:
                @block.vector
                def _(dv, cur=cur):
                    b.wait(dv, "ac", t_ye)
                    for dcol in range(4):
                        ins = dv.tensor_tensor(
                            out=y_t[cur][:, dcol * 512:(dcol + 1) * 512],
                            in0=y_t[cur][:, dcol * 512:(dcol + 1) * 512],
                            in1=x3_t[cur][:, dcol * 512:(dcol + 1) * 512],
                            op=ALU.add)
                    b.inc("dv", ins)
                tok_resid = b.cnt["dv"]

            @block.gpsimd
            def _(gp, ib=ib, i=i, cur=cur):
                if ib == 0:
                    b.wait(gp, "dv", tok_resid)
                else:
                    b.wait(gp, "ac", t_ye)
                ch = f"SY{cur}"
                if b.cnt.get(ch, 0) >= 96:
                    b.wait(gp, ch, b.cnt[ch] - 80)
                if ib == 0:
                    ins = gp.dma_start(out=p.y[i * 128:(i + 1) * 128, :],
                                       in_=y_t[cur])
                else:
                    ins = gp.dma_start(out=p.y[i * 128:(i + 1) * 128, :],
                                       in_=y_t[cur], accum_op=ALU.add)
                b.inc(ch, ins, 16)
                y_tok[cur] = b.cnt[ch]

            if i == 4 and ib + 1 < NB:
                tok_w[(ib + 1) % 2] = load_band(ib + 1, last_u_tok)

    ar.pop()


def preload_set6(ac):
    ac.add_instruction(mybir.InstLoadActFuncSet(
        name=ac.bass.get_next_instruction_name(),
        act_func_set_id=6, ins=[], outs=[]))


def instr_last(eng):
    return eng.nop()



def build(phases=("p0a",), taps=None):
    nc = bass.Bass("TRN2", target_bir_lowering=False)
    es = contextlib.ExitStack()
    p = Prog(nc, es, taps)
    with nc.Block() as block:
        emit_prologue(p, block)
        if "p0a" in phases:
            @block.scalar
            def _(ac):
                ac.preload_act_table(ACTF.Sqrt)
            emit_p0a(p, block)
            p.b.barrier(block)
        if "p0b" in phases:
            @block.scalar
            def _(ac):
                preload_set6(ac)
            emit_p0b(p, block)
            p.b.barrier(block)
        if "p1" in phases:
            @block.scalar
            def _(ac):
                ac.preload_act_table(ACTF.Silu)
            emit_p1(p, block)
            p.b.barrier(block)
        if "p2" in phases:
            @block.scalar
            def _(ac):
                ac.preload_act_table(ACTF.Sqrt)
            emit_p2p(p, block)
            p.b.barrier(block)
            emit_p2a(p, block)
            p.b.barrier(block)
            @block.scalar
            def _(ac):
                preload_set6(ac)
            emit_p2b(p, block)
            p.b.barrier(block)
        if "p3" in phases:
            @block.scalar
            def _(ac):
                ac.preload_act_table(ACTF.Silu)
            emit_p3(p, block)
            p.b.barrier(block)
        for sec in (block.tensor, block.vector, block.scalar, block.gpsimd, block.sync):
            @sec
            def _(eng):
                for ch in sorted(p.b.cnt):
                    p.b.wait(eng, ch)
    es.close()
    return nc, p


ALL_PHASES = ("p0a", "p0b", "p1", "p2", "p3")


def kernel(**inputs):
    consts = host_consts()
    nc, _ = build(phases=ALL_PHASES)
    in_maps = []
    for core in range(8):
        d = host_prepare(inputs, core)
        d.update(consts)
        in_maps.append(d)
    res = run_bass_kernel_spmd(nc, in_maps, core_ids=list(range(8)))
    out = np.stack([np.asarray(r["y"], dtype=np.float32) for r in res.results], 0)
    return out


# version 10
# speedup vs baseline: 1.0402x; 1.0270x over previous
import contextlib
import numpy as np
import ml_dtypes
import concourse.bass as bass
import concourse.mybir as mybir
from concourse.bass_utils import run_bass_kernel_spmd

F32 = mybir.dt.float32
BF16 = mybir.dt.bfloat16
I32 = mybir.dt.int32
U8 = mybir.dt.uint8
AX = mybir.AxisListType
ALU = mybir.AluOpType
ACTF = mybir.ActivationFunctionType

T = 4096
D = 2048
NT = T // 128
DC = D // 128
H = 128
NQH, NKH = 16, 8
ATTN_SCALE = 0.08838834764831845
WINDOW = 1024
HID0 = 5632
HID1 = 8192
SEG = 512
NSEG = T // SEG
NH1 = 16
EPS_LN = 1e-5
EPS_RMS = 1e-6
NEG = -1.0e30
SERIAL = False

ENG_NAME = {}


def _engname(eng):
    return ENG_NAME[id(eng)]


class Bld:

    def __init__(self, nc, es):
        self.nc = nc
        self.es = es
        self.sem = {}
        self.cnt = {}
        self.lastwait = {}
        for ch in ("pe", "dv", "ac", "gp"):
            self._mk(ch)
        ENG_NAME[id(nc.tensor)] = "pe"
        ENG_NAME[id(nc.vector)] = "dv"
        ENG_NAME[id(nc.scalar)] = "ac"
        ENG_NAME[id(nc.gpsimd)] = "gp"
        ENG_NAME[id(nc.sync)] = "sp"

    def _mk(self, ch):
        self.sem[ch] = self.es.enter_context(self.nc.semaphore(f"ch_{ch}"))
        self.cnt[ch] = 0

    def inc(self, ch, instr, k=1):
        if ch not in self.sem:
            self._mk(ch)
        instr.then_inc(self.sem[ch], k)
        self.cnt[ch] += k
        return self.cnt[ch]

    def wait(self, eng, ch, val=None):
        if ch not in self.cnt:
            return
        if val is None:
            val = self.cnt[ch]
        if val <= 0:
            return
        key = (_engname(eng), ch)
        if self.lastwait.get(key, -1) >= val:
            return
        eng.wait_ge(self.sem[ch], val)
        self.lastwait[key] = val

    def hard(self, eng):
        if SERIAL:
            for ch in ("pe", "dv", "ac"):
                self.wait(eng, ch)

    def barrier(self, block):
        for sec in (block.tensor, block.vector, block.scalar, block.gpsimd, block.sync):
            @sec
            def _(eng):
                for ch in sorted(self.cnt):
                    self.wait(eng, ch)


class Arena:

    def __init__(self, nc, nbytes):
        self.t_u8 = nc.alloc_sbuf_tensor("arena", [128, nbytes], U8)
        self.views = {
            F32: self.t_u8.bitcast(F32),
            BF16: self.t_u8.bitcast(BF16),
            I32: self.t_u8.bitcast(I32),
        }
        self.nbytes = nbytes
        self.off = 0
        self.marks = []

    def alloc(self, shape, dtype):
        parts, free = shape
        esz = {F32: 4, BF16: 2, I32: 4}[dtype]
        off = (self.off + 31) // 32 * 32
        self.off = off + free * esz
        assert self.off <= self.nbytes, f"SBUF arena overflow: {self.off} > {self.nbytes}"
        v = self.views[dtype]
        ap = v[0:parts, off // esz: off // esz + free]
        return ap

    def push(self):
        self.marks.append(self.off)

    def pop(self):
        self.off = self.marks.pop()

    def union(self, *shapes):
        base = (self.off + 31) // 32 * 32
        aps = []
        end = base
        for (parts, free), dtype in shapes:
            esz = {F32: 4, BF16: 2, I32: 4}[dtype]
            v = self.views[dtype]
            aps.append(v[0:parts, base // esz: base // esz + free])
            end = max(end, base + free * esz)
        self.off = end
        assert self.off <= self.nbytes, "SBUF arena overflow (union)"
        return aps



def host_consts():
    c = {}
    c["cid_f"] = np.eye(128, dtype=np.float32)
    c["cid_b"] = np.eye(128, dtype=ml_dtypes.bfloat16)
    pr = np.zeros((128, 128), np.float32)
    for k in range(128):
        pr[k, (k - 64) % 128] = 1.0
    c["cprot"] = pr
    masks = np.full((12, 128, 512), NEG, np.float32)
    tt = np.arange(128)[:, None]
    ss = np.arange(512)[None, :]
    for idx in range(12):
        dd = tt - ss + 128 * idx
        masks[idx][(dd >= 0) & (dd < WINDOW)] = 0.0
    c["cmask"] = masks.astype(ml_dtypes.bfloat16)
    frac = np.arange(0, H, 2, dtype=np.float64) / H
    timescale = 10000.0 ** frac
    pos = np.arange(T, dtype=np.float64)
    sinusoid = pos[None, :] / timescale[:, None]
    cosF = np.cos(sinusoid)
    sinF = np.sin(sinusoid)
    cos128 = np.concatenate([cosF, cosF], 0)
    sin128 = np.concatenate([-sinF, sinF], 0)
    cos4 = np.empty((128, NT * 512), np.float32)
    sin4 = np.empty((128, NT * 512), np.float32)
    for i in range(NT):
        ct = cos128[:, i * 128:(i + 1) * 128]
        st = sin128[:, i * 128:(i + 1) * 128]
        cos4[:, i * 512:(i + 1) * 512] = np.tile(ct, (1, 4))
        sin4[:, i * 512:(i + 1) * 512] = np.tile(st, (1, 4))
    c["ccos4"] = cos4
    c["csin4"] = sin4
    memz0 = np.zeros((128, NH1 * 129), np.float32)
    for n in range(NH1):
        memz0[:, n * 129 + 128] = 1.0 / 128.0
    c["cmemz0"] = memz0
    cpool = np.zeros((128, 8), np.float32)
    cpool[:, 0] = EPS_LN
    cpool[:, 1] = EPS_RMS
    cpool[:, 2] = 1.0
    c["ccpool"] = cpool
    return c


def host_prepare(inputs, core):
    f32 = np.float32
    d = {}
    d["x0"] = np.ascontiguousarray(inputs["L0_attn_x"][core]).astype(f32, copy=False)
    d["w_q0"] = np.ascontiguousarray(
        inputs["L0_attn_q_kernel"].transpose(1, 0, 2).reshape(D, NQH * H))
    d["w_k0"] = np.ascontiguousarray(
        inputs["L0_attn_kv_kernel"][0].transpose(1, 0, 2).reshape(D, NKH * H))
    d["w_v0"] = np.ascontiguousarray(
        inputs["L0_attn_kv_kernel"][1].transpose(1, 0, 2).reshape(D, NKH * H))
    d["w_o0"] = np.ascontiguousarray(inputs["L0_attn_out_kernel"].reshape(NQH * H, D))
    d["qns"] = np.ascontiguousarray(inputs["L0_attn_q_norm_scale"].reshape(H, 1))
    d["kns"] = np.ascontiguousarray(inputs["L0_attn_k_norm_scale"].reshape(H, 1))
    d["g0w"] = np.ascontiguousarray(inputs["L0_ffn_wq_gate"])
    d["g0s"] = np.ascontiguousarray(inputs["L0_ffn_s_gate"].reshape(HID0 // 128, 128).T)
    d["u0w"] = np.ascontiguousarray(inputs["L0_ffn_wq_up"])
    d["u0s"] = np.ascontiguousarray(inputs["L0_ffn_s_up"].reshape(HID0 // 128, 128).T)
    d["d0w"] = np.ascontiguousarray(inputs["L0_ffn_wq_down"])
    d["d0s"] = np.ascontiguousarray(inputs["L0_ffn_s_down"].reshape(DC, 128).T)
    d["w_q1"] = np.ascontiguousarray(inputs["L1_attn_Wq"])
    d["w_k1"] = np.ascontiguousarray(inputs["L1_attn_Wk"])
    d["w_v1"] = np.ascontiguousarray(inputs["L1_attn_Wv"])
    d["w_o1"] = np.ascontiguousarray(inputs["L1_attn_Wo"])
    d["betasT"] = np.ascontiguousarray(inputs["L1_attn_betas"][0, :, 0, :].T)
    d["g1w"] = np.ascontiguousarray(inputs["L1_ffn_w_gate"])
    d["u1w"] = np.ascontiguousarray(inputs["L1_ffn_w_up"])
    d["d1w"] = np.ascontiguousarray(inputs["L1_ffn_w_down"])
    return d



class Prog:
    def __init__(self, nc, es, taps):
        self.nc = nc
        self.es = es
        self.taps = taps or {}
        self.b = Bld(nc, es)
        dt = nc.dram_tensor
        self.x0 = dt("x0", [T, D], F32, kind="ExternalInput").ap()
        self.w_q0 = dt("w_q0", [D, NQH * H], F32, kind="ExternalInput").ap()
        self.w_k0 = dt("w_k0", [D, NKH * H], F32, kind="ExternalInput").ap()
        self.w_v0 = dt("w_v0", [D, NKH * H], F32, kind="ExternalInput").ap()
        self.w_o0 = dt("w_o0", [NQH * H, D], F32, kind="ExternalInput").ap()
        self.qns = dt("qns", [H, 1], F32, kind="ExternalInput").ap()
        self.kns = dt("kns", [H, 1], F32, kind="ExternalInput").ap()
        self.g0w = dt("g0w", [HID0, D], I32, kind="ExternalInput").ap()
        self.g0s = dt("g0s", [128, HID0 // 128], F32, kind="ExternalInput").ap()
        self.u0w = dt("u0w", [HID0, D], I32, kind="ExternalInput").ap()
        self.u0s = dt("u0s", [128, HID0 // 128], F32, kind="ExternalInput").ap()
        self.d0w = dt("d0w", [D, HID0], I32, kind="ExternalInput").ap()
        self.d0s = dt("d0s", [128, DC], F32, kind="ExternalInput").ap()
        self.w_q1 = dt("w_q1", [D, D], F32, kind="ExternalInput").ap()
        self.w_k1 = dt("w_k1", [D, D], F32, kind="ExternalInput").ap()
        self.w_v1 = dt("w_v1", [D, D], F32, kind="ExternalInput").ap()
        self.w_o1 = dt("w_o1", [D, D], F32, kind="ExternalInput").ap()
        self.betasT = dt("betasT", [H, NH1], F32, kind="ExternalInput").ap()
        self.g1w = dt("g1w", [D, HID1], F32, kind="ExternalInput").ap()
        self.u1w = dt("u1w", [D, HID1], F32, kind="ExternalInput").ap()
        self.d1w = dt("d1w", [HID1, D], F32, kind="ExternalInput").ap()
        self.cid_f = dt("cid_f", [128, 128], F32, kind="ExternalInput").ap()
        self.cid_b = dt("cid_b", [128, 128], BF16, kind="ExternalInput").ap()
        self.cprot = dt("cprot", [128, 128], F32, kind="ExternalInput").ap()
        self.cmask = dt("cmask", [12, 128, 512], BF16, kind="ExternalInput").ap()
        self.ccos4 = dt("ccos4", [128, NT * 512], F32, kind="ExternalInput").ap()
        self.csin4 = dt("csin4", [128, NT * 512], F32, kind="ExternalInput").ap()
        self.cmemz0 = dt("cmemz0", [128, NH1 * 129], F32, kind="ExternalInput").ap()
        self.ccpool = dt("ccpool", [128, 8], F32, kind="ExternalInput").ap()
        self.y = dt("y", [T, D], F32, kind="ExternalOutput").ap()
        self.s_xA = dt("s_xA", [T, D], F32, kind="Internal").ap()
        self.s_xB = dt("s_xB", [T, D], F32, kind="Internal").ap()
        self.s_xhatT = dt("s_xhatT", [DC, 128, T], BF16, kind="Internal").ap()
        self.s_qT0 = dt("s_qT0", [NQH, 128, T], BF16, kind="Internal").ap()
        self.s_kT0 = dt("s_kT0", [NKH, 128, T], BF16, kind="Internal").ap()
        self.s_v0 = dt("s_v0", [T, NKH * H], BF16, kind="Internal").ap()
        self.s_qT1 = dt("s_qT1", [NH1, 128, T], F32, kind="Internal").ap()
        self.s_kT1 = dt("s_kT1", [NH1, 128, T], F32, kind="Internal").ap()
        self.s_v1 = dt("s_v1", [T, D], F32, kind="Internal").ap()
        self.arena = Arena(nc, 190 * 1024)
        self.id_f = self.arena.alloc([128, 128], F32)
        self.id_b = self.arena.alloc([128, 128], BF16)
        self.prot = self.arena.alloc([128, 128], F32)
        self.cpool_s = self.arena.alloc([128, 8], F32)
        for ci, cv in enumerate([EPS_LN, EPS_RMS, 1.0]):
            nc.const_aps.register(F32, cv, self.cpool_s[:, ci:ci + 1])
        self.pb = [nc.alloc_psum_tensor(f"pb{i}", [128, 512], F32) for i in range(8)]
        self.pbb = [p.bitcast(BF16) for p in self.pb]

    def tap(self, name, shape, dtype):
        if name in self.taps:
            return self.nc.dram_tensor("tap_" + name, shape, dtype,
                                       kind="ExternalOutput").ap()
        return None


def emit_prologue(p, block):
    b = p.b

    @block.sync
    def _(sp):
        b.inc("LC", sp.dma_start(out=p.id_f, in_=p.cid_f), 16)
        b.inc("LC", sp.dma_start(out=p.id_b, in_=p.cid_b), 16)
        b.inc("LC", sp.dma_start(out=p.prot, in_=p.cprot), 16)
        b.inc("LC", sp.dma_start(out=p.cpool_s, in_=p.ccpool), 16)


def emit_ln_xhatT(p, block, x_ap, xhat_b, xhatT_b, scr, tr_banks, out_f32=None,
                  expln=False):
    nc, b = p.nc, p.b

    @block.vector
    def _(dv):
        dv.tensor_reduce(out=scr["sum"], in_=x_ap, axis=AX.X, op=ALU.add)
        dv.scalar_tensor_tensor(out=scr["big"], in0=x_ap, scalar=1.0, in1=x_ap,
                                op0=ALU.bypass, op1=ALU.mult, accum_out=scr["ssq"])
        dv.drain()
        dv.tensor_scalar(out=scr["negmean"], in0=scr["sum"], scalar1=-1.0 / D,
                         scalar2=None, op0=ALU.mult)
        dv.drain()
        dv.tensor_tensor(out=scr["m2"], in0=scr["negmean"], in1=scr["negmean"],
                         op=ALU.mult)
        dv.tensor_scalar(out=scr["var"], in0=scr["ssq"], scalar1=1.0 / D,
                         scalar2=None, op0=ALU.mult)
        dv.drain()
        b.inc("dv", dv.tensor_tensor(out=scr["var"], in0=scr["var"], in1=scr["m2"],
                                     op=ALU.subtract))

    if expln:
        @block.scalar
        def _(ac):
            b.wait(ac, "dv")
            ac.activation(out=scr["std"], in_=scr["var"], func=ACTF.Ln,
                          bias=EPS_LN, scale=1.0)
            ac.drain()
            b.inc("ac", ac.activation(out=scr["rstd"], in_=scr["std"],
                                      func=ACTF.Exp, scale=-0.5))

        @block.vector
        def _(dv):
            b.wait(dv, "ac")
            b.inc("dv", dv.tensor_tensor(out=scr["lnb"], in0=scr["negmean"],
                                         in1=scr["rstd"], op=ALU.mult))
    else:
        @block.scalar
        def _(ac):
            b.wait(ac, "dv")
            b.inc("ac", ac.activation(out=scr["std"], in_=scr["var"], func=ACTF.Sqrt,
                                      bias=EPS_LN, scale=1.0))

        @block.vector
        def _(dv):
            b.wait(dv, "ac")
            b.inc("dv", dv.reciprocal(out=scr["rstd"], in_=scr["std"]))
            dv.drain()
            b.inc("dv", dv.tensor_tensor(out=scr["lnb"], in0=scr["negmean"],
                                         in1=scr["rstd"], op=ALU.mult))

    @block.scalar
    def _(ac):
        b.wait(ac, "dv")
        b.inc("ac", ac.activation(out=xhat_b, in_=x_ap, func=ACTF.Identity,
                                  bias=scr["lnb"], scale=scr["rstd"]))
        if out_f32 is not None:
            b.inc("ac", ac.activation(out=out_f32, in_=x_ap, func=ACTF.Identity,
                                      bias=scr["lnb"], scale=scr["rstd"]))
    tok_xhat = b.cnt["ac"]

    for g in range(4):
        bank = tr_banks[g % 2]
        pbank = p.pbb[bank]

        @block.tensor
        def _(pe, g=g, bank=bank, pbank=pbank):
            b.wait(pe, "ac", tok_xhat)
            if g >= 2:
                b.wait(pe, "ac")
            for j in range(4):
                c = 4 * g + j
                instr = pe.transpose(out=pbank[:, j * 128:(j + 1) * 128],
                                     in_=xhat_b[:, c * 128:(c + 1) * 128],
                                     identity=p.id_b)
            b.inc("pe", instr)

        @block.scalar
        def _(ac, g=g, pbank=pbank):
            b.wait(ac, "pe")
            b.inc("ac", ac.activation(out=xhatT_b[:, g * 512:(g + 1) * 512],
                                      in_=pbank[:, 0:512], func=ACTF.Copy))


def ln_scratch(p):
    scr = {}
    scr["big"] = p.arena.alloc([128, D], F32)
    for k in ("sum", "ssq", "negmean", "m2", "var", "std", "rstd", "lnb"):
        scr[k] = p.arena.alloc([128, 1], F32)
    return scr



def emit_p0a(p, block):
    nc, b, ar = p.nc, p.b, p.arena
    ar.push()
    FQ = NQH * H
    FK = NKH * H
    wq_s = ar.alloc([128, DC * FQ], BF16)
    wk_s = ar.alloc([128, DC * FK], BF16)
    wv_s = ar.alloc([128, DC * FK], BF16)
    qns_s = ar.alloc([128, 1], F32)
    kns_s = ar.alloc([128, 1], F32)
    qsc_s = ar.alloc([128, 1], F32)
    xf = [ar.alloc([128, D], F32) for _ in range(2)]
    rc_s = [ar.alloc([128, 512], F32) for _ in range(2)]
    rs_s = [ar.alloc([128, 512], F32) for _ in range(2)]
    xhat_b = ar.alloc([128, D], BF16)
    xhatT_b = ar.alloc([128, D], BF16)
    scr = ln_scratch(p)
    qn_f = scr["big"]
    kn_f = ar.alloc([128, FK], F32)
    v_b = ar.alloc([128, FK], BF16)
    ssq_q = ar.alloc([128, NQH], F32)
    ssq_k = ar.alloc([128, NKH], F32)
    rq_s = ar.alloc([128, NQH], F32)
    rk_s = ar.alloc([128, NKH], F32)
    qTf = [ar.alloc([128, 512], F32) for _ in range(2)]
    qTrot = [ar.alloc([128, 512], F32) for _ in range(2)]
    r1 = ar.alloc([128, 512], F32)
    r2 = ar.alloc([128, 512], F32)
    qT_out = [ar.alloc([128, 512], BF16) for _ in range(2)]

    tap_qT = p.tap("qT0", [NQH, 128, T], BF16)
    tap_v = p.tap("v0", [T, NKH * H], BF16)
    tap_xhatT = p.tap("xhatT_p0", [DC, 128, T], BF16)

    @block.sync
    def _(sp):
        b.inc("LC", sp.dma_start(out=qns_s, in_=p.qns), 16)
        b.inc("LC", sp.dma_start(out=kns_s, in_=p.kns), 16)
    tok_ns = b.cnt["LC"]

    @block.gpsimd
    def _(gp):
        for dk in range(DC):
            b.inc("LW", gp.dma_start(out=wq_s[:, dk * FQ:(dk + 1) * FQ],
                                     in_=p.w_q0[dk * 128:(dk + 1) * 128, :]), 16)
            b.inc("LW", gp.dma_start(out=wk_s[:, dk * FK:(dk + 1) * FK],
                                     in_=p.w_k0[dk * 128:(dk + 1) * 128, :]), 16)
            b.inc("LW", gp.dma_start(out=wv_s[:, dk * FK:(dk + 1) * FK],
                                     in_=p.w_v0[dk * 128:(dk + 1) * 128, :]), 16)
    tok_w = b.cnt["LW"]

    @block.vector
    def _(dv):
        b.wait(dv, "LC", tok_ns)
        b.inc("dv", dv.tensor_scalar(out=qsc_s, in0=qns_s, scalar1=ATTN_SCALE,
                                     scalar2=None, op0=ALU.mult))

    @block.sync
    def _(sp):
        b.inc("LT0", sp.dma_start(out=xf[0], in_=p.x0[0:128, :]), 16)
        b.inc("LT0", sp.dma_start(out=rc_s[0], in_=p.ccos4[:, 0:512]), 16)
        b.inc("LT0", sp.dma_start(out=rs_s[0], in_=p.csin4[:, 0:512]), 16)

    lt_tok = [b.cnt["LT0"], 0]
    cons_tok = [0, 0]
    rope_par = [0, 0]
    qtf_tok = [0, 0]
    rot_tok = [0, 0]

    for i in range(NT):
        cur, nxt = i % 2, (i + 1) % 2
        if i + 1 < NT:
            @block.sync
            def _(sp, i=i, nxt=nxt):
                ch = f"LT{nxt}"
                if i >= 1:
                    b.wait(sp, "dv", cons_tok[nxt])
                b.inc(ch, sp.dma_start(out=xf[nxt],
                                       in_=p.x0[(i + 1) * 128:(i + 2) * 128, :]), 16)
                b.inc(ch, sp.dma_start(out=rc_s[nxt],
                                       in_=p.ccos4[:, (i + 1) * 512:(i + 2) * 512]), 16)
                b.inc(ch, sp.dma_start(out=rs_s[nxt],
                                       in_=p.csin4[:, (i + 1) * 512:(i + 2) * 512]), 16)
                lt_tok[nxt] = b.cnt[ch]

        @block.vector
        def _(dv, cur=cur):
            b.wait(dv, f"LT{cur}", lt_tok[cur])
            b.wait(dv, "pe")

        emit_ln_xhatT(p, block, xf[cur], xhat_b, xhatT_b, scr, (6, 7))
        tok_xhatT = b.cnt["ac"]

        @block.tensor
        def _(pe, i=i):
            b.wait(pe, "ac", tok_xhatT)
            if i == 0:
                b.wait(pe, "LW", tok_w)
            for j in range(4):
                for dk in range(DC):
                    instr = pe.matmul(out=p.pb[j][:],
                                      lhsT=xhatT_b[:, dk * 128:(dk + 1) * 128],
                                      rhs=wq_s[:, dk * FQ + j * 512: dk * FQ + (j + 1) * 512],
                                      start=(dk == 0), stop=(dk == DC - 1))
                b.inc("pe", instr)
            for j in range(2):
                for dk in range(DC):
                    instr = pe.matmul(out=p.pb[4 + j][:],
                                      lhsT=xhatT_b[:, dk * 128:(dk + 1) * 128],
                                      rhs=wk_s[:, dk * FK + j * 512: dk * FK + (j + 1) * 512],
                                      start=(dk == 0), stop=(dk == DC - 1))
                b.inc("pe", instr)
            for j in range(2):
                for dk in range(DC):
                    instr = pe.matmul(out=p.pb[6 + j][:],
                                      lhsT=xhatT_b[:, dk * 128:(dk + 1) * 128],
                                      rhs=wv_s[:, dk * FK + j * 512: dk * FK + (j + 1) * 512],
                                      start=(dk == 0), stop=(dk == DC - 1))
                b.inc("pe", instr)
        tok_projq = b.cnt["pe"] - 4
        tok_projk = b.cnt["pe"] - 2
        tok_projv = b.cnt["pe"]

        @block.scalar
        def _(ac):
            b.wait(ac, "pe", tok_projq)
            ins = None
            for hh in range(NQH):
                bank, col = hh // 4, (hh % 4) * 128
                ins = ac.activation(
                    out=scr["big"][:, hh * 128:(hh + 1) * 128],
                    in_=p.pb[bank][:, col:col + 128], func=ACTF.Square,
                    accum_out=ssq_q[:, hh:hh + 1])
            b.inc("ac", ins)
            b.wait(ac, "pe", tok_projk)
            for hh in range(NKH):
                bank, col = 4 + hh // 4, (hh % 4) * 128
                ins = ac.activation(
                    out=kn_f[:, hh * 128:(hh + 1) * 128],
                    in_=p.pb[bank][:, col:col + 128], func=ACTF.Square,
                    accum_out=ssq_k[:, hh:hh + 1])
            b.inc("ac", ins)
        tok_ssq = b.cnt["ac"]

        @block.scalar
        def _(ac):
            b.wait(ac, "pe", tok_projv)
            b.wait(ac, "SV")
            ac.activation(out=v_b[:, 0:512], in_=p.pb[6][:], func=ACTF.Copy)
            b.inc("ac", ac.activation(out=v_b[:, 512:1024], in_=p.pb[7][:],
                                      func=ACTF.Copy))
        tok_vevac = b.cnt["ac"]

        @block.gpsimd
        def _(gp, i=i):
            b.wait(gp, "ac", tok_vevac)
            b.inc("SV", gp.dma_start(out=p.s_v0[i * 128:(i + 1) * 128, :], in_=v_b), 16)
            if tap_v is not None:
                b.inc("SV", gp.dma_start(out=tap_v[i * 128:(i + 1) * 128, :], in_=v_b), 16)

        @block.scalar
        def _(ac):
            b.wait(ac, "ac", tok_ssq)
            ac.drain()
            ac.activation(out=rq_s, in_=ssq_q, func=ACTF.Sqrt,
                          bias=EPS_RMS, scale=1.0 / H)
            b.inc("ac", ac.activation(out=rk_s, in_=ssq_k, func=ACTF.Sqrt,
                                      bias=EPS_RMS, scale=1.0 / H))

        @block.vector
        def _(dv):
            b.wait(dv, "ac")
            dv.reciprocal(out=rq_s, in_=rq_s)
            b.inc("dv", dv.reciprocal(out=rk_s, in_=rk_s))
        tok_rms = b.cnt["dv"]

        @block.scalar
        def _(ac):
            b.wait(ac, "dv", tok_rms)
            for hh in range(NQH):
                bank, col = hh // 4, (hh % 4) * 128
                instr = ac.activation(out=qn_f[:, hh * 128:(hh + 1) * 128],
                                      in_=p.pb[bank][:, col:col + 128],
                                      func=ACTF.Identity, scale=rq_s[:, hh:hh + 1])
            for hh in range(NKH):
                bank, col = 4 + hh // 4, (hh % 4) * 128
                instr = ac.activation(out=kn_f[:, hh * 128:(hh + 1) * 128],
                                      in_=p.pb[bank][:, col:col + 128],
                                      func=ACTF.Identity, scale=rk_s[:, hh:hh + 1])
            b.inc("ac", instr)
        tok_rmsevac = b.cnt["ac"]

        for g in range(6):
            is_q = g < 4
            src = qn_f if is_q else kn_f
            goff = g * 512 if is_q else (g - 4) * 512
            sc = qsc_s if is_q else kns_s
            bank_t, bank_r = (0, 1) if g % 2 == 0 else (2, 3)
            gpar = g % 2

            @block.tensor
            def _(pe, src=src, goff=goff, bank_t=bank_t, gpar=gpar):
                b.wait(pe, "ac", tok_rmsevac)
                b.wait(pe, "ac", qtf_tok[gpar])
                for j in range(4):
                    instr = pe.transpose(out=p.pb[bank_t][:, j * 128:(j + 1) * 128],
                                         in_=src[:, goff + j * 128: goff + (j + 1) * 128],
                                         identity=p.id_f)
                b.inc("pe", instr)
            tok_tr = b.cnt["pe"]

            @block.scalar
            def _(ac, bank_t=bank_t, sc=sc, gpar=gpar):
                b.wait(ac, "pe", tok_tr)
                b.wait(ac, "dv", rope_par[gpar])
                b.inc("ac", ac.activation(out=qTf[gpar], in_=p.pb[bank_t][:],
                                          func=ACTF.Identity, scale=sc))
            tok_qTf = b.cnt["ac"]
            qtf_tok[gpar] = tok_qTf

            @block.tensor
            def _(pe, bank_r=bank_r, gpar=gpar):
                b.wait(pe, "ac", tok_qTf)
                b.wait(pe, "ac", rot_tok[gpar])
                b.inc("pe", pe.matmul(out=p.pb[bank_r][:], lhsT=p.prot, rhs=qTf[gpar],
                                      start=True, stop=True))
            tok_rot = b.cnt["pe"]

            @block.scalar
            def _(ac, bank_r=bank_r, gpar=gpar):
                b.wait(ac, "pe", tok_rot)
                b.inc("ac", ac.activation(out=qTrot[gpar], in_=p.pb[bank_r][:],
                                          func=ACTF.Copy))
            tok_rot_e = b.cnt["ac"]
            rot_tok[gpar] = tok_rot_e

            @block.vector
            def _(dv, cur=cur, gpar=gpar):
                b.wait(dv, "ac", tok_rot_e)
                b.wait(dv, f"LT{cur}", lt_tok[cur])
                b.wait(dv, f"GQ{gpar}")
                dv.drain()
                dv.tensor_tensor(out=r1, in0=qTf[gpar], in1=rc_s[cur], op=ALU.mult)
                dv.tensor_tensor(out=r2, in0=qTrot[gpar], in1=rs_s[cur], op=ALU.mult)
                dv.drain()
                b.inc("dv", dv.tensor_tensor(out=qT_out[gpar], in0=r1,
                                             in1=r2, op=ALU.add))
            tok_rope = b.cnt["dv"]
            rope_par[gpar] = tok_rope

            @block.gpsimd
            def _(gp, i=i, g=g, is_q=is_q, gpar=gpar):
                b.wait(gp, "dv", tok_rope)
                dst = p.s_qT0 if is_q else p.s_kT0
                h0 = 4 * g if is_q else 4 * (g - 4)
                b.inc(f"GQ{gpar}", gp.dma_start(
                    out=dst[h0:h0 + 4, :, i * 128:(i + 1) * 128].rearrange("h p t -> p h t"),
                    in_=qT_out[gpar].rearrange("p (h t) -> p h t", h=4)), 16)
                if is_q and tap_qT is not None:
                    b.inc(f"GQ{gpar}", gp.dma_start(
                        out=tap_qT[h0:h0 + 4, :, i * 128:(i + 1) * 128].rearrange("h p t -> p h t"),
                        in_=qT_out[gpar].rearrange("p (h t) -> p h t", h=4)), 16)

        if tap_xhatT is not None:
            @block.sync
            def _(sp, i=i):
                b.wait(sp, "ac", tok_xhatT)
                b.inc("ST", sp.dma_start(
                    out=tap_xhatT[:, :, i * 128:(i + 1) * 128].rearrange("c p t -> p c t"),
                    in_=xhatT_b.rearrange("p (c t) -> p c t", c=DC)), 16)
            @block.scalar
            def _(ac):
                b.wait(ac, "ST")

        cons_tok[cur] = b.cnt["dv"]

    ar.pop()



def emit_p0b(p, block):
    nc, b, ar = p.nc, p.b, p.arena
    ar.push()
    wo_s = ar.alloc([128, DC * D], BF16)
    mask_s = ar.alloc([128, 12 * 512], BF16)
    kr = ar.alloc([128, 3 * NKH * 512], BF16)
    vr = ar.alloc([128, 3 * 4 * 1024], BF16)
    qT4 = [ar.alloc([128, NQH * 128], BF16) for _ in range(4)]
    x0_t = ar.alloc([128, D], F32)
    probs_all4, x1_t = ar.union(
        ([128, 4 * 1536], BF16),
        ([128, D], F32))
    probs_all = [probs_all4[:, j * 1536:(j + 1) * 1536] for j in range(4)]
    pslot = [ar.alloc([128, 512], BF16) for _ in range(2)]
    attT_s = ar.alloc([128, NQH * 512], BF16)
    xhat_b2, xhatT_b2 = ar.union(([128, 2 * D], BF16), ([128, 2 * D], BF16))
    xhat_b = xhat_b2[:, 0:D]
    xhatT_b = xhatT_b2[:, D:2 * D]
    scr = {}
    scr["big"] = xhat_b2.bitcast(F32)[:, 0:D]
    for k in ("sum", "ssq", "negmean", "m2", "var", "std", "rstd", "lnb"):
        scr[k] = ar.alloc([128, 1], F32)
    nm3 = ar.alloc([128, 4], F32)
    nm = ar.alloc([128, 1], F32)
    lb = ar.alloc([128, 4], F32)
    lcol = ar.alloc([128, 1], F32)
    lnl = ar.alloc([128, 1], F32)
    bias2 = ar.alloc([128, 1], F32)

    tap_x1 = p.tap("x1", [T, D], F32)
    tap_dbg = p.tap("dbg0", [NSEG, 4, 128, 8], F32)
    tap_pr = p.tap("probs2", [4, 128, 1536], BF16)

    @block.gpsimd
    def _(gp):
        for c in range(DC):
            b.inc("LW", gp.dma_start(out=wo_s[:, c * D:(c + 1) * D],
                                     in_=p.w_o0[c * 128:(c + 1) * 128, :]), 16)
    tok_w = b.cnt["LW"]

    @block.sync
    def _(sp):
        for mi in range(12):
            b.inc("LM", sp.dma_start(
                out=mask_s[:, mi * 512:(mi + 1) * 512], in_=p.cmask[mi]), 16)
    tok_mask = b.cnt["LM"]

    SB = ((0, 1, 2), (3, 4, 5))
    BT, BA = 6, 7
    relq_tok = [0, 0]

    for B in range(NSEG):
        chunks = [c for c in (B - 2, B - 1, B) if c >= 0]
        nch = len(chunks)

        @block.sync
        def _(sp, B=B):
            b.wait(sp, "pe")
            slot = B % 3
            b.inc(f"KR{slot}", sp.dma_start(
                out=kr[:, slot * NKH * 512:(slot + 1) * NKH * 512]
                    .rearrange("p (h t) -> p h t", h=NKH),
                in_=p.s_kT0[:, :, B * 512:(B + 1) * 512].rearrange("h p t -> p h t")), 16)
            b.inc(f"VR{slot}", sp.dma_start(
                out=vr[:, slot * 4096:(slot + 1) * 4096]
                    .rearrange("p (a n) -> p a n", a=4),
                in_=p.s_v0[B * 512:(B + 1) * 512, :]
                    .rearrange("(a p) n -> p a n", p=128)), 16)
            for j in range(4):
                b.inc("LQ", sp.dma_start(
                    out=qT4[j].rearrange("p (h t) -> p h t", h=NQH),
                    in_=p.s_qT0[:, :, (4 * B + j) * 128:(4 * B + j + 1) * 128]
                        .rearrange("h p t -> p h t")), 16)
        tok_ring = {c: b.cnt[f"KR{c % 3}"] for c in chunks}
        tok_vring = {c: b.cnt[f"VR{c % 3}"] for c in chunks}
        tok_lq = b.cnt["LQ"]

        for n in range(NQH):
            kvn = n // 2
            tok_pass2 = [0, 0, 0, 0]
            for j in range(4):
                S = SB[j % 2]

                @block.tensor
                def _(pe, n=n, j=j, S=S, kvn=kvn, B=B, chunks=chunks):
                    b.hard(pe)
                    if n == 0 and j < 2:
                        b.wait(pe, "ac")
                        b.wait(pe, "dv")
                        b.wait(pe, "LW", tok_w)
                    b.wait(pe, "ac", relq_tok[j % 2])
                    b.wait(pe, "LQ", tok_lq)
                    for ci, c in enumerate(chunks):
                        b.wait(pe, f"KR{c % 3}", tok_ring[c])
                        ins = pe.matmul(
                            out=p.pb[S[ci]][:],
                            lhsT=qT4[j][:, n * 128:(n + 1) * 128],
                            rhs=kr[:, (c % 3) * NKH * 512 + kvn * 512:
                                   (c % 3) * NKH * 512 + (kvn + 1) * 512],
                            start=True, stop=True)
                    b.inc("pe", ins)
                tok_sc = b.cnt["pe"]

                @block.vector
                def _(dv, j=j, S=S, B=B, chunks=chunks, nch=nch):
                    b.hard(dv)
                    b.wait(dv, "pe", tok_sc)
                    b.wait(dv, "LM", tok_mask)
                    ins = None
                    for ci, c in enumerate(chunks):
                        idx = j + 4 * (B - c)
                        if 4 <= idx <= 7:
                            continue
                        ins = dv.tensor_tensor(
                            out=p.pb[S[ci]][:], in0=p.pb[S[ci]][:],
                            in1=mask_s[:, idx * 512:(idx + 1) * 512], op=ALU.add)
                    if ins is not None:
                        dv.drain()
                    for ci in range(nch):
                        dv.reduce_max(out=nm3[:, ci:ci + 1], in_=p.pb[S[ci]][:],
                                      axis=AX.X, negate=True)
                    dv.drain()
                    b.inc("dv", dv.tensor_reduce(out=nm, in_=nm3[:, 0:nch],
                                                 axis=AX.X, op=ALU.min))
                tok_nm = b.cnt["dv"]

                @block.scalar
                def _(ac, j=j, S=S, nch=nch):
                    b.hard(ac)
                    b.wait(ac, "dv", tok_nm)
                    b.wait(ac, "GT")
                    b.wait(ac, "GX")
                    for ci in range(nch):
                        ins = ac.activation(
                            out=probs_all[j][:, ci * 512:(ci + 1) * 512],
                            in_=p.pb[S[ci]][:], func=ACTF.Exp, bias=nm,
                            scale=1.0, accum_out=lb[:, ci:ci + 1])
                    b.inc("ac", ins)
                tok_l = b.cnt["ac"]

                @block.vector
                def _(dv, nch=nch):
                    b.hard(dv)
                    b.wait(dv, "ac", tok_l)
                    b.inc("dv", dv.tensor_reduce(out=lcol, in_=lb[:, 0:nch],
                                                 axis=AX.X, op=ALU.add))

                @block.scalar
                def _(ac):
                    b.hard(ac)
                    b.wait(ac, "dv")
                    b.inc("ac", ac.activation(out=lnl, in_=lcol, func=ACTF.Ln))

                @block.vector
                def _(dv):
                    b.hard(dv)
                    b.wait(dv, "ac")
                    b.inc("dv", dv.tensor_tensor(out=bias2, in0=nm, in1=lnl,
                                                 op=ALU.subtract))

                @block.scalar
                def _(ac, j=j, S=S, nch=nch):
                    b.hard(ac)
                    b.wait(ac, "dv")
                    for ci in range(nch):
                        ins = ac.activation(
                            out=probs_all[j][:, ci * 512:(ci + 1) * 512],
                            in_=p.pb[S[ci]][:], func=ACTF.Exp, bias=bias2,
                            scale=1.0)
                    b.inc("ac", ins)
                tok_pass2[j] = b.cnt["ac"]
                relq_tok[j % 2] = tok_pass2[j]

                if tap_dbg is not None and n == 0:
                    @block.sync
                    def _(sp, B=B, j=j, nch=nch):
                        b.wait(sp, "ac", tok_pass2[j])
                        b.wait(sp, "dv")
                        for col, buf in ((0, nm), (1, lcol), (2, lnl), (3, bias2)):
                            b.inc("ST", sp.dma_start(
                                out=tap_dbg[B, j, :, col:col + 1], in_=buf,
                                allow_slow_non_contiguous=True), 16)
                        b.inc("ST", sp.dma_start(
                            out=tap_dbg[B, j, :, 4:4 + nch], in_=lb[:, 0:nch],
                            allow_slow_non_contiguous=True), 16)
                        if tap_pr is not None and B == 2:
                            b.inc("ST", sp.dma_start(
                                out=tap_pr[j], in_=probs_all[j]), 16)

                    @block.vector
                    def _(dv):
                        b.wait(dv, "ST")

            nsb = 4 * nch
            for sb in range(nsb):
                ci, sub = sb // 4, sb % 4
                c = chunks[ci]

                @block.tensor
                def _(pe, ci=ci, sub=sub):
                    b.hard(pe)
                    b.wait(pe, "ac", max(tok_pass2))
                    b.wait(pe, "ac")
                    for j in range(4):
                        ins = pe.transpose(
                            out=p.pbb[BT][:, j * 128:(j + 1) * 128],
                            in_=probs_all[j][:, ci * 512 + sub * 128:
                                             ci * 512 + (sub + 1) * 128],
                            identity=p.id_b)
                    b.inc("pe", ins)
                tok_tr = b.cnt["pe"]

                @block.scalar
                def _(ac, sb=sb):
                    b.hard(ac)
                    b.wait(ac, "pe", tok_tr)
                    b.inc("ac", ac.activation(out=pslot[sb % 2],
                                              in_=p.pbb[BT][:, 0:512],
                                              func=ACTF.Copy))
                tok_ev = b.cnt["ac"]

                @block.tensor
                def _(pe, sb=sb, ci=ci, sub=sub, c=c, kvn=kvn, nsb=nsb):
                    b.hard(pe)
                    b.wait(pe, "ac", tok_ev)
                    b.wait(pe, f"VR{c % 3}", tok_vring[c])
                    b.inc("pe", pe.matmul(
                        out=p.pb[BA][:],
                        lhsT=vr[:, (c % 3) * 4096 + sub * 1024 + kvn * 128:
                                (c % 3) * 4096 + sub * 1024 + (kvn + 1) * 128],
                        rhs=pslot[sb % 2], start=(sb == 0), stop=(sb == nsb - 1)))
            tok_pv = b.cnt["pe"]

            @block.scalar
            def _(ac, n=n):
                b.hard(ac)
                b.wait(ac, "pe", tok_pv)
                b.inc("ac", ac.activation(out=attT_s[:, n * 512:(n + 1) * 512],
                                          in_=p.pb[BA][:], func=ACTF.Copy))

        tok_attT = b.cnt["ac"]

        for j in range(4):
            i = 4 * B + j

            @block.sync
            def _(sp, i=i):
                b.wait(sp, "dv")
                b.inc("LX", sp.dma_start(out=x0_t, in_=p.x0[i * 128:(i + 1) * 128, :]), 16)
            tok_x0 = b.cnt["LX"]

            @block.tensor
            def _(pe, j=j):
                b.wait(pe, "ac", tok_attT)
                b.wait(pe, "dv")
                for dcol in range(4):
                    for c in range(DC):
                        ins = pe.matmul(
                            out=p.pb[dcol][:],
                            lhsT=attT_s[:, c * 512 + j * 128: c * 512 + j * 128 + 128],
                            rhs=wo_s[:, c * D + dcol * 512: c * D + (dcol + 1) * 512],
                            start=(c == 0), stop=(c == DC - 1))
                    b.inc("pe", ins)
            tok_op = b.cnt["pe"]

            @block.vector
            def _(dv, j=j):
                b.wait(dv, "pe", tok_op)
                b.wait(dv, "LX", tok_x0)
                b.wait(dv, "GX")
                b.wait(dv, "ac")
                for dcol in range(4):
                    ins = dv.tensor_tensor(out=x1_t[:, dcol * 512:(dcol + 1) * 512],
                                           in0=p.pb[dcol][:],
                                           in1=x0_t[:, dcol * 512:(dcol + 1) * 512],
                                           op=ALU.add)
                b.inc("dv", ins)
            tok_x1 = b.cnt["dv"]

            @block.gpsimd
            def _(gp, i=i):
                b.wait(gp, "dv", tok_x1)
                b.inc("GX", gp.dma_start(out=p.s_xA[i * 128:(i + 1) * 128, :],
                                         in_=x1_t), 16)
                if tap_x1 is not None:
                    b.inc("GX", gp.dma_start(out=tap_x1[i * 128:(i + 1) * 128, :],
                                             in_=x1_t), 16)

            @block.vector
            def _(dv):
                dv.drain()
                b.wait(dv, "GT")

            @block.scalar
            def _(ac):
                b.wait(ac, "GT")

            emit_ln_xhatT(p, block, x1_t, xhat_b, xhatT_b, scr, (4, 5), expln=True)
            tok_lnT = b.cnt["ac"]

            @block.gpsimd
            def _(gp, i=i):
                b.wait(gp, "ac", tok_lnT)
                b.inc("GT", gp.dma_start(
                    out=p.s_xhatT[:, :, i * 128:(i + 1) * 128].rearrange("c p t -> p c t"),
                    in_=xhatT_b.rearrange("p (c t) -> p c t", c=DC)), 16)

    ar.pop()



def emit_p1(p, block):
    nc, b, ar = p.nc, p.b, p.arena
    ar.push()
    NB = HID0 // 512
    WgT = [ar.alloc([128, DC * 512], BF16) for _ in range(2)]
    WuT = [ar.alloc([128, DC * 512], BF16) for _ in range(2)]
    WdT = [ar.alloc([128, 4 * D], BF16) for _ in range(2)]
    wtmp = [ar.alloc([128, D], I32) for _ in range(2)]
    wdq = [ar.alloc([128, D], BF16) for _ in range(2)]
    sgu = [ar.alloc([128, 8], F32) for _ in range(2)]
    sd_s = ar.alloc([128, DC], F32)
    xh_t = [ar.alloc([128, D], BF16) for _ in range(2)]
    x1_t = [ar.alloc([128, D], F32) for _ in range(2)]
    g_s = [ar.alloc([128, 512], F32) for _ in range(2)]
    h_b = [ar.alloc([128, 512], BF16) for _ in range(2)]
    hT_b = [ar.alloc([128, 512], BF16) for _ in range(2)]
    y_t = [ar.alloc([128, D], F32) for _ in range(2)]
    y_tok = [0, 0]
    BG = (0, 6)
    BU = (1, 7)

    @block.sync
    def _(sp):
        b.inc("LC", sp.dma_start(out=sd_s, in_=p.d0s), 16)
    tok_sd = b.cnt["LC"]

    wprep_dv = [0, 0]
    wprep_tok = [0, 0]

    def prep_band(hb, gate_pe_tok):
        sl = hb % 2

        @block.sync
        def _(sp, hb=hb, sl=sl, gate_pe_tok=gate_pe_tok):
            if gate_pe_tok:
                b.wait(sp, "pe", gate_pe_tok)
            b.inc("LB", sp.dma_start(
                out=sgu[sl][:, 0:4], in_=p.g0s[:, hb * 4:(hb + 1) * 4]), 16)
            b.inc("LB", sp.dma_start(
                out=sgu[sl][:, 4:8], in_=p.u0s[:, hb * 4:(hb + 1) * 4]), 16)

        steps = []
        for mi, (wsrc, scol) in enumerate(((p.g0w, 0), (p.u0w, 4))):
            for q in range(4):
                steps.append(("gu", mi, q, wsrc, scol))
        for dk in range(DC):
            steps.append(("d", dk, None, None, None))

        for si, st in enumerate(steps):
            par = si % 2

            @block.sync
            def _(sp, st=st, hb=hb, par=par):
                b.wait(sp, "dv")
                if st[0] == "gu":
                    _, mi, q, wsrc, scol = st
                    b.inc("LB", sp.dma_start(
                        out=wtmp[par],
                        in_=wsrc[hb * 512 + q * 128: hb * 512 + (q + 1) * 128, :]), 16)
                else:
                    dk = st[1]
                    b.inc("LB", sp.dma_start(
                        out=wtmp[par][:, 0:512],
                        in_=p.d0w[dk * 128:(dk + 1) * 128, hb * 512:(hb + 1) * 512]), 16)
            tok_wt = b.cnt["LB"]

            @block.vector
            def _(dv, st=st, sl=sl, par=par):
                b.wait(dv, "LB", tok_wt)
                b.wait(dv, "LC", tok_sd)
                b.wait(dv, "LBT")
                if st[0] == "gu":
                    _, mi, q, wsrc, scol = st
                    ins = dv.tensor_scalar(
                        out=wdq[par], in0=wtmp[par], scalar1=-128.0,
                        scalar2=sgu[sl][:, scol + q: scol + q + 1],
                        op0=ALU.add, op1=ALU.mult)
                else:
                    dk = st[1]
                    ins = dv.tensor_scalar(
                        out=wdq[par][:, 0:512], in0=wtmp[par][:, 0:512],
                        scalar1=-128.0, scalar2=sd_s[:, dk:dk + 1],
                        op0=ALU.add, op1=ALU.mult)
                b.inc("dv", ins)
            tok_dq = b.cnt["dv"]

            @block.scalar
            def _(ac, st=st, sl=sl, par=par):
                b.wait(ac, "dv", tok_dq)
                if st[0] == "gu":
                    _, mi, q, _, _ = st
                    dstT = (WgT, WuT)[mi][sl]
                    for dk in range(DC):
                        b.inc("LBT", ac.dma_start(
                            out=dstT[:, dk * 512 + q * 128: dk * 512 + (q + 1) * 128],
                            in_=wdq[par][:, dk * 128:(dk + 1) * 128],
                            transpose=True), 16)
                else:
                    dk = st[1]
                    for a in range(4):
                        b.inc("LBT", ac.dma_start(
                            out=WdT[sl][:, a * D + dk * 128: a * D + (dk + 1) * 128],
                            in_=wdq[par][:, a * 128:(a + 1) * 128],
                            transpose=True), 16)
        wprep_tok[sl] = b.cnt["LBT"]

    prep_band(0, 0)

    t_silu = [0, 0]
    t_h = [0, 0]
    t_tr = [0, 0]
    t_hT = [0, 0]
    t_dn = [0, 0]
    t_gu = [0, 0]
    t_ye = 0
    t_lh = [0, 0]
    t_lx = [0, 0]
    last_u_tok = 0

    for hb in range(NB):
        sl = hb % 2
        for i in range(NT):
            cur = i % 2
            bg, bu = BG[cur], BU[cur]

            @block.sync
            def _(sp, hb=hb, i=i, cur=cur):
                b.wait(sp, "pe", t_gu[cur])
                b.inc(f"LH{cur}", sp.dma_start(
                    out=xh_t[cur].rearrange("p (c t) -> p c t", c=DC),
                    in_=p.s_xhatT[:, :, i * 128:(i + 1) * 128].rearrange("c p t -> p c t")), 16)
                if hb == 0:
                    b.wait(sp, "ac", t_ye)
                    b.inc(f"LX{cur}", sp.dma_start(
                        out=x1_t[cur], in_=p.s_xA[i * 128:(i + 1) * 128, :]), 16)
            t_lh[cur] = b.cnt[f"LH{cur}"]
            if hb == 0:
                t_lx[cur] = b.cnt[f"LX{cur}"]

            @block.tensor
            def _(pe, cur=cur, sl=sl, bg=bg, bu=bu):
                b.wait(pe, "LBT", wprep_tok[sl])
                b.wait(pe, f"LH{cur}", t_lh[cur])
                b.wait(pe, "ac", t_hT[cur])
                b.wait(pe, "dv", t_h[cur])
                for dk in range(DC):
                    ins = pe.matmul(out=p.pb[bg][:],
                                    lhsT=xh_t[cur][:, dk * 128:(dk + 1) * 128],
                                    rhs=WgT[sl][:, dk * 512:(dk + 1) * 512],
                                    start=(dk == 0), stop=(dk == DC - 1))
                b.inc("pe", ins)
                for dk in range(DC):
                    ins = pe.matmul(out=p.pb[bu][:],
                                    lhsT=xh_t[cur][:, dk * 128:(dk + 1) * 128],
                                    rhs=WuT[sl][:, dk * 512:(dk + 1) * 512],
                                    start=(dk == 0), stop=(dk == DC - 1))
                b.inc("pe", ins)
            tok_g = b.cnt["pe"] - 1
            t_gu[cur] = last_u_tok = b.cnt["pe"]

            @block.scalar
            def _(ac, cur=cur, bg=bg):
                b.wait(ac, "pe", tok_g)
                b.wait(ac, "dv", t_h[cur])
                b.inc("ac", ac.activation(out=g_s[cur], in_=p.pb[bg][:],
                                          func=ACTF.Silu))
            t_silu[cur] = b.cnt["ac"]

            @block.vector
            def _(dv, cur=cur, bu=bu):
                b.wait(dv, "pe", t_gu[cur])
                b.wait(dv, "ac", t_silu[cur])
                b.wait(dv, "pe", t_tr[cur])
                b.inc("dv", dv.tensor_tensor(out=h_b[cur], in0=g_s[cur],
                                             in1=p.pb[bu][:], op=ALU.mult))
            t_h[cur] = b.cnt["dv"]

            @block.tensor
            def _(pe, cur=cur, bg=bg):
                b.wait(pe, "dv", t_h[cur])
                b.wait(pe, "ac", t_silu[cur])
                b.wait(pe, "ac", t_hT[cur])
                for jj in range(4):
                    ins = pe.transpose(out=p.pbb[bg][:, jj * 128:(jj + 1) * 128],
                                       in_=h_b[cur][:, jj * 128:(jj + 1) * 128],
                                       identity=p.id_b)
                b.inc("pe", ins)
            t_tr[cur] = b.cnt["pe"]

            @block.scalar
            def _(ac, cur=cur, bg=bg):
                b.wait(ac, "pe", t_tr[cur])
                b.wait(ac, "pe", t_dn[cur])
                b.inc("ac", ac.activation(out=hT_b[cur], in_=p.pbb[bg][:, 0:512],
                                          func=ACTF.Copy))
            t_hT[cur] = b.cnt["ac"]

            @block.tensor
            def _(pe, cur=cur, sl=sl):
                b.wait(pe, "ac", t_hT[cur])
                b.wait(pe, "ac", t_ye)
                for dcol in range(4):
                    for hc in range(4):
                        ins = pe.matmul(
                            out=p.pb[2 + dcol][:],
                            lhsT=hT_b[cur][:, hc * 128:(hc + 1) * 128],
                            rhs=WdT[sl][:, hc * D + dcol * 512: hc * D + (dcol + 1) * 512],
                            start=(hc == 0), stop=(hc == 3))
                    b.inc("pe", ins)
            t_dn[cur] = b.cnt["pe"]

            @block.scalar
            def _(ac, hb=hb, cur=cur):
                b.wait(ac, "pe", t_dn[cur])
                b.wait(ac, f"SY{cur}", y_tok[cur])
                if hb == 0:
                    b.wait(ac, f"LX{cur}", t_lx[cur])
                for dcol in range(4):
                    ins = ac.activation(out=y_t[cur][:, dcol * 512:(dcol + 1) * 512],
                                        in_=p.pb[2 + dcol][:], func=ACTF.Copy)
                b.inc("ac", ins)
            t_ye = b.cnt["ac"]

            if hb == 0:
                @block.vector
                def _(dv, cur=cur):
                    b.wait(dv, "ac", t_ye)
                    for dcol in range(4):
                        ins = dv.tensor_tensor(
                            out=y_t[cur][:, dcol * 512:(dcol + 1) * 512],
                            in0=y_t[cur][:, dcol * 512:(dcol + 1) * 512],
                            in1=x1_t[cur][:, dcol * 512:(dcol + 1) * 512],
                            op=ALU.add)
                    b.inc("dv", ins)
                tok_resid = b.cnt["dv"]

            @block.gpsimd
            def _(gp, hb=hb, i=i, cur=cur):
                if hb == 0:
                    b.wait(gp, "dv", tok_resid)
                else:
                    b.wait(gp, "ac", t_ye)
                ch = f"SY{cur}"
                if b.cnt.get(ch, 0) >= 48:
                    b.wait(gp, ch, b.cnt[ch] - 32)
                if hb == 0:
                    ins = gp.dma_start(out=p.s_xB[i * 128:(i + 1) * 128, :],
                                       in_=y_t[cur])
                else:
                    ins = gp.dma_start(out=p.s_xB[i * 128:(i + 1) * 128, :],
                                       in_=y_t[cur], accum_op=ALU.add)
                b.inc(ch, ins, 16)
                y_tok[cur] = b.cnt[ch]

            if i == 4 and hb + 1 < NB:
                prep_band(hb + 1, last_u_tok)

    tap_x2 = p.tap("x2", [T, D], F32)
    if tap_x2 is not None:
        @block.sync
        def _(sp):
            for ch in ("SY0", "SY1"):
                b.wait(sp, ch)
            b.inc("ST", sp.dma_start(out=tap_x2, in_=p.s_xB), 16)

    ar.pop()



def emit_p2p(p, block):
    nc, b, ar = p.nc, p.b, p.arena
    ar.push()
    x2_t = [ar.alloc([128, D], F32) for _ in range(2)]
    xh2 = [ar.alloc([128, 2 * D], BF16) for _ in range(2)]
    scrs = []
    for par in range(2):
        scr = {"big": xh2[par].bitcast(F32)[:, 0:D]}
        for k in ("sum", "ssq", "negmean", "m2", "var", "std", "rstd", "lnb"):
            scr[k] = ar.alloc([128, 1], F32)
        scrs.append(scr)
    t_xhat = [0, 0]
    t_st = [0, 0]

    for i in range(NT):
        cur = i % 2

        @block.sync
        def _(sp, i=i, cur=cur):
            b.wait(sp, "ac", t_xhat[cur])
            b.inc(f"LP{cur}", sp.dma_start(out=x2_t[cur],
                                           in_=p.s_xB[i * 128:(i + 1) * 128, :]), 16)
        tok_x2 = b.cnt[f"LP{cur}"]

        @block.vector
        def _(dv, cur=cur):
            b.wait(dv, f"LP{cur}", tok_x2)
            b.wait(dv, f"ST{cur}", t_st[cur])

        @block.scalar
        def _(ac, cur=cur):
            b.wait(ac, f"LP{cur}", tok_x2)
            b.wait(ac, f"ST{cur}", t_st[cur])

        emit_ln_xhatT(p, block, x2_t[cur], xh2[cur][:, 0:D], xh2[cur][:, D:2 * D],
                      scrs[cur], (2 * cur, 2 * cur + 1))
        t_xhat[cur] = b.cnt["ac"]

        @block.scalar
        def _(ac, i=i, cur=cur):
            ac.drain()
            b.inc(f"ST{cur}", ac.dma_start(
                out=p.s_xhatT[:, :, i * 128:(i + 1) * 128].rearrange("c p t -> p c t"),
                in_=xh2[cur][:, D:2 * D].rearrange("p (c t) -> p c t", c=DC)), 16)
        t_st[cur] = b.cnt[f"ST{cur}"]
    ar.pop()


def emit_p2a(p, block):
    nc, b, ar = p.nc, p.b, p.arena
    ar.push()
    w_s = ar.alloc([128, DC * D], BF16)
    xhT_blk = [ar.alloc([128, DC * 512], BF16) for _ in range(2)]
    ev = [ar.alloc([128, 512], F32) for _ in range(2)]

    for wi, (wsrc, dstT) in enumerate(((p.w_q1, p.s_qT1), (p.w_k1, p.s_kT1))):
        @block.gpsimd
        def _(gp, wsrc=wsrc):
            b.wait(gp, "pe")
            for dk in range(DC):
                b.inc("LW", gp.dma_start(out=w_s[:, dk * D:(dk + 1) * D],
                                         in_=wsrc[dk * 128:(dk + 1) * 128, :]), 16)
        tok_w = b.cnt["LW"]

        for tb in range(NSEG):
            cur = tb % 2

            @block.sync
            def _(sp, tb=tb, cur=cur):
                b.wait(sp, "pe")
                b.inc(f"LG{cur}", sp.dma_start(
                    out=xhT_blk[cur].rearrange("p (c t) -> p c t", c=DC),
                    in_=p.s_xhatT[:, :, tb * 512:(tb + 1) * 512]
                        .rearrange("c p t -> p c t")), 16)
            tok_xh = b.cnt[f"LG{cur}"]

            for fc in range(DC):
                bank = fc % 2

                @block.tensor
                def _(pe, fc=fc, cur=cur, bank=bank):
                    b.wait(pe, "LW", tok_w)
                    b.wait(pe, f"LG{cur}", tok_xh)
                    b.wait(pe, f"SP{fc % 2}")
                    for dk in range(DC):
                        ins = pe.matmul(
                            out=p.pb[bank][:],
                            lhsT=w_s[:, dk * D + fc * 128: dk * D + (fc + 1) * 128],
                            rhs=xhT_blk[cur][:, dk * 512:(dk + 1) * 512],
                            start=(dk == 0), stop=(dk == DC - 1))
                    b.inc("pe", ins)
                tok_mm = b.cnt["pe"]

                @block.scalar
                def _(ac, fc=fc, tb=tb, bank=bank, dstT=dstT):
                    b.wait(ac, "pe", tok_mm)
                    b.wait(ac, f"SP{fc % 2}")
                    ac.activation(out=ev[bank], in_=p.pb[bank][:], func=ACTF.Copy)
                    ac.drain()
                    b.inc(f"SP{fc % 2}", ac.dma_start(
                        out=dstT[fc, :, tb * 512:(tb + 1) * 512], in_=ev[bank]), 16)

    @block.gpsimd
    def _(gp):
        b.wait(gp, "pe")
        for dk in range(DC):
            b.inc("LW", gp.dma_start(out=w_s[:, dk * D:(dk + 1) * D],
                                     in_=p.w_v1[dk * 128:(dk + 1) * 128, :]), 16)
    tok_w = b.cnt["LW"]
    vout = [ar.alloc([128, D], F32) for _ in range(2)]

    for i in range(NT):
        cur = i % 2

        @block.sync
        def _(sp, i=i, cur=cur):
            b.wait(sp, "pe")
            b.inc(f"LG{cur}", sp.dma_start(
                out=xhT_blk[cur][:, 0:DC * 128].rearrange("p (c t) -> p c t", c=DC),
                in_=p.s_xhatT[:, :, i * 128:(i + 1) * 128].rearrange("c p t -> p c t")), 16)
        tok_xh = b.cnt[f"LG{cur}"]

        @block.tensor
        def _(pe, cur=cur):
            b.wait(pe, "LW", tok_w)
            b.wait(pe, f"LG{cur}", tok_xh)
            b.wait(pe, "SP0")
            b.wait(pe, "SP1")
            for vc in range(4):
                for dk in range(DC):
                    ins = pe.matmul(
                        out=p.pb[2 + vc][:],
                        lhsT=xhT_blk[cur][:, dk * 128:(dk + 1) * 128],
                        rhs=w_s[:, dk * D + vc * 512: dk * D + (vc + 1) * 512],
                        start=(dk == 0), stop=(dk == DC - 1))
                b.inc("pe", ins)
        tok_mm = b.cnt["pe"]

        @block.scalar
        def _(ac, i=i, cur=cur):
            b.wait(ac, "pe", tok_mm)
            b.wait(ac, f"SP{cur}")
            for vc in range(4):
                ac.activation(out=vout[cur][:, vc * 512:(vc + 1) * 512],
                              in_=p.pb[2 + vc][:], func=ACTF.Copy)
            ac.drain()
            b.inc(f"SP{cur}", ac.dma_start(out=p.s_v1[i * 128:(i + 1) * 128, :],
                                           in_=vout[cur]), 16)
    ar.pop()


def emit_p2b(p, block):
    nc, b, ar = p.nc, p.b, p.arena
    ar.push()
    C1 = ATTN_SCALE
    wo_s = ar.alloc([128, DC * D], BF16)
    mask1 = ar.alloc([128, 4 * 512], BF16)
    memz_f = ar.alloc([128, NH1 * 129], F32)
    memz_b = ar.alloc([128, NH1 * 129], BF16)
    beta_s = ar.alloc([128, NH1], F32)
    btmp = ar.alloc([128, NH1], F32)
    qT_h = [ar.alloc([128, 512], F32) for _ in range(2)]
    kT_h = [ar.alloc([128, 512], F32) for _ in range(2)]
    v_h = [ar.alloc([128, 512], F32) for _ in range(2)]
    v_hb = [ar.alloc([128, 512], BF16) for _ in range(2)]
    min0 = ar.alloc([128, 512], F32)
    e_s = ar.alloc([128, 512], F32)
    sigq_b = ar.alloc([128, 512], BF16)
    sigk_b = ar.alloc([128, 512], BF16)
    sigk_tok = ar.alloc([128, 512], BF16)
    probs_seg = ar.alloc([128, 4 * 512], BF16)
    pslot = [ar.alloc([128, 512], BF16) for _ in range(2)]
    attT_s = ar.alloc([128, NH1 * 512], BF16)
    delta_ones = ar.alloc([128, 4 * 132], BF16)
    rz_col = ar.alloc([128, 1], F32)
    nrz_col = ar.alloc([128, 1], F32)
    rz_row = ar.alloc([1, 512], F32)
    rzb_row = ar.alloc([1, 512], BF16)
    ones_b = ar.alloc([1, 128], BF16)
    brcp_s = ar.alloc([128, 512], F32)
    t1_s = ar.alloc([128, 512], F32)
    t2_s = ar.alloc([128, 512], F32)
    x2_t = ar.alloc([128, D], F32)
    x3_t = ar.alloc([128, D], F32)
    xh2 = ar.alloc([128, 2 * D], BF16)
    xhat_b = xh2[:, 0:D]
    xhatT_b = xh2[:, D:2 * D]
    scr = {"big": xh2.bitcast(F32)[:, 0:D]}
    for k in ("sum", "ssq", "negmean", "m2", "var", "std", "rstd", "lnb"):
        scr[k] = ar.alloc([128, 1], F32)
    nm3 = ar.alloc([128, 4], F32)
    nm = ar.alloc([128, 1], F32)
    bias1 = ar.alloc([128, 1], F32)
    lb = ar.alloc([128, 4], F32)
    lcol = ar.alloc([128, 1], F32)
    lnl = ar.alloc([128, 1], F32)
    bias2 = ar.alloc([128, 1], F32)

    tap_x3 = p.tap("x3", [T, D], F32)

    @block.gpsimd
    def _(gp):
        for c in range(DC):
            b.inc("LW", gp.dma_start(out=wo_s[:, c * D:(c + 1) * D],
                                     in_=p.w_o1[c * 128:(c + 1) * 128, :]), 16)

    tok_w = b.cnt["LW"]
    tok_mask = b.cnt["LM"]

    @block.sync
    def _(sp):
        for mi in range(4):
            b.inc("LM", sp.dma_start(
                out=mask1[:, mi * 512:(mi + 1) * 512], in_=p.cmask[mi]), 16)
        b.inc("LC", sp.dma_start(out=memz_f, in_=p.cmemz0), 16)
        b.inc("LC", sp.dma_start(out=btmp, in_=p.betasT), 16)
    tok_mz = b.cnt["LC"]
    tok_mask = b.cnt["LM"]

    @block.scalar
    def _(ac):
        b.wait(ac, "LC", tok_mz)
        b.inc("ac", ac.activation(out=beta_s, in_=btmp, func=ACTF.Exp, scale=-1.0))

    @block.vector
    def _(dv):
        b.wait(dv, "LC", tok_mz)
        b.wait(dv, "ac")
        b.inc("dv", dv.tensor_copy(out=memz_b, in_=memz_f))
        dv.tensor_scalar(out=beta_s, in0=beta_s, scalar1=1.0, scalar2=None,
                         op0=ALU.add)
        dv.drain()
        dv.reciprocal(out=beta_s, in_=beta_s)
        dv.memset(ones_b[:], 1.0)
        for sub in range(4):
            dv.memset(delta_ones[:, sub * 132 + 128: sub * 132 + 129], 1.0)
        dv.drain()
        b.inc("dv", dv.memset(delta_ones[:, 0:1], 0.0))

    qTb = [ar.alloc([128, 512], BF16) for _ in range(2)]
    kTb = [ar.alloc([128, 512], BF16) for _ in range(2)]
    relq_tok = [0, 0]

    for s in range(NSEG):
        for n in range(NH1):
            cur = n % 2

            @block.sync
            def _(sp, s=s, n=n, cur=cur):
                b.wait(sp, "pe")
                b.wait(sp, "dv")
                b.inc(f"LA{cur}", sp.dma_start(
                    out=qT_h[cur], in_=p.s_qT1[n, :, s * 512:(s + 1) * 512]), 16)
                b.inc(f"LA{cur}", sp.dma_start(
                    out=kT_h[cur], in_=p.s_kT1[n, :, s * 512:(s + 1) * 512]), 16)
                b.inc(f"LA{cur}", sp.dma_start(
                    out=v_h[cur].rearrange("p (a n) -> p a n", a=4),
                    in_=p.s_v1[s * 512:(s + 1) * 512, n * 128:(n + 1) * 128]
                        .rearrange("(a p) n -> p a n", p=128)), 16)
            tok_la = b.cnt[f"LA{cur}"]

            @block.gpsimd
            def _(gp, s=s, n=n, cur=cur):
                b.wait(gp, "pe")
                b.inc(f"LV{cur}", gp.dma_start(
                    out=v_hb[cur].rearrange("p (a n) -> p a n", a=4),
                    in_=p.s_v1[s * 512:(s + 1) * 512, n * 128:(n + 1) * 128]
                        .rearrange("(a p) n -> p a n", p=128)), 16)
                b.inc(f"LV{cur}", gp.dma_start(
                    out=qTb[cur], in_=p.s_qT1[n, :, s * 512:(s + 1) * 512]), 16)
                b.inc(f"LV{cur}", gp.dma_start(
                    out=kTb[cur], in_=p.s_kT1[n, :, s * 512:(s + 1) * 512]), 16)
            tok_lv = b.cnt[f"LV{cur}"]

            @block.vector
            def _(dv, cur=cur):
                b.wait(dv, f"LA{cur}", tok_la)
                b.wait(dv, "ac")
                b.wait(dv, "pe")
                b.inc("dv", dv.tensor_scalar(out=min0, in0=qT_h[cur], scalar1=0.0,
                                             scalar2=None, op0=ALU.min))

            @block.scalar
            def _(ac):
                b.wait(ac, "dv")
                b.inc("ac", ac.activation(out=e_s, in_=min0, func=ACTF.Exp))

            @block.vector
            def _(dv, cur=cur):
                b.wait(dv, "ac")
                b.inc("dv", dv.scalar_tensor_tensor(
                    out=sigq_b, in0=qT_h[cur], scalar=0.0, in1=e_s,
                    op0=ALU.max, op1=ALU.add))
                b.inc("dv", dv.tensor_scalar(out=min0, in0=kT_h[cur], scalar1=0.0,
                                             scalar2=None, op0=ALU.min))

            @block.scalar
            def _(ac):
                b.wait(ac, "dv")
                b.inc("ac", ac.activation(out=e_s, in_=min0, func=ACTF.Exp))

            @block.vector
            def _(dv, cur=cur):
                b.wait(dv, "ac")
                b.inc("dv", dv.scalar_tensor_tensor(
                    out=sigk_b, in0=kT_h[cur], scalar=0.0, in1=e_s,
                    op0=ALU.max, op1=ALU.add))
            tok_sig = b.cnt["dv"]

            @block.tensor
            def _(pe):
                b.wait(pe, "dv", tok_sig)
                b.wait(pe, "ac")
                for sub in range(4):
                    ins = pe.transpose(out=p.pbb[2][:, sub * 128:(sub + 1) * 128],
                                       in_=sigk_b[:, sub * 128:(sub + 1) * 128],
                                       identity=p.id_b)
                b.inc("pe", ins)
            tok_str = b.cnt["pe"]

            @block.scalar
            def _(ac):
                b.wait(ac, "pe", tok_str)
                b.wait(ac, "pe")
                b.inc("ac", ac.activation(out=sigk_tok, in_=p.pbb[2][:, 0:512],
                                          func=ACTF.Copy))
            tok_stok = b.cnt["ac"]

            tok_pass2 = [0, 0, 0, 0]
            for j in range(4):
                bank = j % 2

                @block.tensor
                def _(pe, j=j, cur=cur, bank=bank):
                    b.wait(pe, f"LV{cur}", tok_lv)
                    b.wait(pe, "ac", relq_tok[bank])
                    b.inc("pe", pe.matmul(out=p.pb[bank][:],
                                          lhsT=qTb[cur][:, j * 128:(j + 1) * 128],
                                          rhs=kTb[cur], start=True, stop=True))
                tok_sc = b.cnt["pe"]

                @block.vector
                def _(dv, j=j, bank=bank):
                    b.wait(dv, "pe", tok_sc)
                    b.wait(dv, "LM", tok_mask)
                    dv.tensor_tensor(out=p.pb[bank][:], in0=p.pb[bank][:],
                                     in1=mask1[:, j * 512:(j + 1) * 512],
                                     op=ALU.add)
                    dv.drain()
                    dv.reduce_max(out=nm, in_=p.pb[bank][:], axis=AX.X, negate=True)
                    dv.drain()
                    b.inc("dv", dv.tensor_scalar(out=bias1, in0=nm, scalar1=C1,
                                                 scalar2=None, op0=ALU.mult))
                tok_nm = b.cnt["dv"]

                @block.scalar
                def _(ac, j=j, bank=bank):
                    b.wait(ac, "dv", tok_nm)
                    b.inc("ac", ac.activation(
                        out=probs_seg[:, j * 512:(j + 1) * 512],
                        in_=p.pb[bank][:], func=ACTF.Exp, bias=bias1,
                        scale=C1, accum_out=lcol))
                tok_l = b.cnt["ac"]

                @block.scalar
                def _(ac):
                    b.wait(ac, "ac", tok_l)
                    ac.drain()
                    b.inc("ac", ac.activation(out=lnl, in_=lcol, func=ACTF.Ln))

                @block.vector
                def _(dv):
                    b.wait(dv, "ac")
                    b.inc("dv", dv.tensor_tensor(out=bias2, in0=bias1, in1=lnl,
                                                 op=ALU.subtract))

                @block.scalar
                def _(ac, j=j, bank=bank):
                    b.wait(ac, "dv")
                    b.inc("ac", ac.activation(
                        out=probs_seg[:, j * 512:(j + 1) * 512],
                        in_=p.pb[bank][:], func=ACTF.Exp, bias=bias2, scale=C1))
                tok_pass2[j] = b.cnt["ac"]
                relq_tok[bank] = tok_pass2[j]


            for sb in range(4):
                @block.tensor
                def _(pe, sb=sb):
                    b.wait(pe, "ac", max(tok_pass2))
                    b.wait(pe, "ac")
                    for j in range(4):
                        ins = pe.transpose(
                            out=p.pbb[2][:, j * 128:(j + 1) * 128],
                            in_=probs_seg[:, j * 512 + sb * 128: j * 512 + (sb + 1) * 128],
                            identity=p.id_b)
                    b.inc("pe", ins)
                tok_tr = b.cnt["pe"]

                @block.scalar
                def _(ac, sb=sb):
                    b.wait(ac, "pe", tok_tr)
                    b.inc("ac", ac.activation(out=pslot[sb % 2],
                                              in_=p.pbb[2][:, 0:512],
                                              func=ACTF.Copy))
                tok_ev = b.cnt["ac"]

                @block.tensor
                def _(pe, sb=sb, cur=cur):
                    b.wait(pe, "ac", tok_ev)
                    if sb == 0:
                        b.wait(pe, "dv")
                    b.inc("pe", pe.matmul(
                        out=p.pb[3][:],
                        lhsT=v_hb[cur][:, sb * 128:(sb + 1) * 128],
                        rhs=pslot[sb % 2], start=(sb == 0), stop=(sb == 3)))
            tok_pv = b.cnt["pe"]

            @block.tensor
            def _(pe, n=n):
                b.wait(pe, "ac")
                b.wait(pe, "dv", tok_sig)
                b.inc("pe", pe.matmul(out=p.pb[4][:],
                                      lhsT=memz_b[:, n * 129: n * 129 + 128],
                                      rhs=sigq_b, start=True, stop=True))
                b.inc("pe", pe.matmul(out=p.pb[5][0:1, :],
                                      lhsT=memz_b[:, n * 129 + 128: n * 129 + 129],
                                      rhs=sigq_b, start=True, stop=True))
            tok_ro = b.cnt["pe"]

            @block.vector
            def _(dv):
                b.wait(dv, "pe", tok_ro)
                dv.reciprocal(out=rz_row, in_=p.pb[5][0:1, :])
                dv.drain()
                b.inc("dv", dv.tensor_copy(out=rzb_row, in_=rz_row))
            tok_rz = b.cnt["dv"]

            @block.tensor
            def _(pe):
                b.wait(pe, "dv", tok_rz)
                b.inc("pe", pe.matmul(out=p.pb[5][:], lhsT=ones_b, rhs=rzb_row,
                                      start=True, stop=True))
            tok_bc = b.cnt["pe"]

            @block.scalar
            def _(ac):
                b.wait(ac, "pe", tok_bc)
                b.inc("ac", ac.activation(out=brcp_s, in_=p.pb[5][:],
                                          func=ACTF.Copy))
            tok_brcp = b.cnt["ac"]

            for sub in range(4):
                @block.tensor
                def _(pe, n=n, sub=sub):
                    b.wait(pe, "ac", tok_stok)
                    b.wait(pe, "dv")
                    b.inc("pe", pe.matmul(
                        out=p.pb[6][:, 0:129],
                        lhsT=sigk_b[:, sub * 128:(sub + 1) * 128],
                        rhs=memz_b[:, n * 129:(n + 1) * 129],
                        start=True, stop=True))
                tok_R = b.cnt["pe"]

                @block.vector
                def _(dv, sub=sub, cur=cur):
                    b.wait(dv, "pe", tok_R)
                    b.wait(dv, f"LA{cur}", tok_la)
                    dv.reciprocal(out=rz_col, in_=p.pb[6][:, 128:129])
                    dv.drain()
                    dv.tensor_scalar(out=nrz_col, in0=rz_col, scalar1=-1.0,
                                     scalar2=None, op0=ALU.mult)
                    dv.drain()
                    b.inc("dv", dv.scalar_tensor_tensor(
                        out=delta_ones[:, sub * 132: sub * 132 + 128],
                        in0=p.pb[6][:, 0:128], scalar=nrz_col,
                        in1=v_h[cur][:, sub * 128:(sub + 1) * 128],
                        op0=ALU.mult, op1=ALU.add))
                tok_delta = b.cnt["dv"]

                @block.tensor
                def _(pe, sub=sub):
                    b.wait(pe, "dv", tok_delta)
                    b.inc("pe", pe.matmul(
                        out=p.pb[7][:, 0:129],
                        lhsT=sigk_tok[:, sub * 128:(sub + 1) * 128],
                        rhs=delta_ones[:, sub * 132: sub * 132 + 129],
                        start=(sub == 0), stop=(sub == 3)))
            tok_upd = b.cnt["pe"]

            @block.vector
            def _(dv, n=n):
                b.wait(dv, "pe", tok_upd)
                b.inc("dv", dv.tensor_tensor(
                    out=memz_f[:, n * 129:(n + 1) * 129],
                    in0=memz_f[:, n * 129:(n + 1) * 129],
                    in1=p.pb[7][:, 0:129], op=ALU.add))
            tok_mzf = b.cnt["dv"]

            @block.scalar
            def _(ac, n=n):
                b.wait(ac, "dv", tok_mzf)
                b.inc("ac", ac.activation(out=memz_b[:, n * 129:(n + 1) * 129],
                                          in_=memz_f[:, n * 129:(n + 1) * 129],
                                          func=ACTF.Copy))

            att_scr = attT_s.rearrange("p (c j a bb) -> p c j a bb",
                                       c=16, j=4, a=4)

            @block.vector
            def _(dv, n=n):
                b.wait(dv, "ac", tok_brcp)
                b.wait(dv, "pe", tok_pv)
                dv.tensor_tensor(out=t1_s, in0=p.pb[4][:], in1=brcp_s, op=ALU.mult)
                dv.drain()
                dv.tensor_tensor(out=t2_s, in0=t1_s, in1=p.pb[3][:], op=ALU.subtract)
                dv.drain()
                out_ap = att_scr[:, :, n // 4, n % 4, :].rearrange("p c b -> p b c")
                b.inc("dv", dv.scalar_tensor_tensor(
                    out=out_ap,
                    in0=t2_s, scalar=beta_s[:, n:n + 1], in1=p.pb[3][:],
                    op0=ALU.mult, op1=ALU.add))
            tok_att = b.cnt["dv"]

        for j in range(4):
            i = 4 * s + j

            @block.sync
            def _(sp, i=i):
                b.wait(sp, "dv")
                b.inc("LX2", sp.dma_start(out=x2_t,
                                          in_=p.s_xB[i * 128:(i + 1) * 128, :]), 16)
            tok_x2 = b.cnt["LX2"]

            @block.tensor
            def _(pe, j=j):
                b.wait(pe, "LW", tok_w)
                b.wait(pe, "dv", tok_att)
                b.wait(pe, "dv")
                b.wait(pe, "ac")
                for dcol in range(4):
                    for c in range(DC):
                        ins = pe.matmul(
                            out=p.pb[dcol][:],
                            lhsT=attT_s[:, c * 512 + j * 128: c * 512 + j * 128 + 128],
                            rhs=wo_s[:, c * D + dcol * 512: c * D + (dcol + 1) * 512],
                            start=(c == 0), stop=(c == DC - 1))
                    b.inc("pe", ins)
            tok_op = b.cnt["pe"]

            @block.vector
            def _(dv, j=j):
                b.wait(dv, "pe", tok_op)
                b.wait(dv, "LX2", tok_x2)
                b.wait(dv, "GX")
                for dcol in range(4):
                    ins = dv.tensor_tensor(out=x3_t[:, dcol * 512:(dcol + 1) * 512],
                                           in0=p.pb[dcol][:],
                                           in1=x2_t[:, dcol * 512:(dcol + 1) * 512],
                                           op=ALU.add)
                b.inc("dv", ins)
            tok_x3 = b.cnt["dv"]

            @block.gpsimd
            def _(gp, i=i):
                b.wait(gp, "dv", tok_x3)
                b.inc("GX", gp.dma_start(out=p.s_xA[i * 128:(i + 1) * 128, :],
                                         in_=x3_t), 16)
                if tap_x3 is not None:
                    b.inc("GX", gp.dma_start(out=tap_x3[i * 128:(i + 1) * 128, :],
                                             in_=x3_t), 16)
            tok_sx = b.cnt["GX"]

            @block.vector
            def _(dv):
                dv.drain()
                b.wait(dv, "GT")

            @block.scalar
            def _(ac):
                b.wait(ac, "GT")
                b.wait(ac, "GX", tok_sx)

            emit_ln_xhatT(p, block, x3_t, xhat_b, xhatT_b, scr, (4, 5), expln=True)
            tok_lnT = b.cnt["ac"]

            @block.gpsimd
            def _(gp, i=i):
                b.wait(gp, "ac", tok_lnT)
                b.inc("GT", gp.dma_start(
                    out=p.s_xhatT[:, :, i * 128:(i + 1) * 128].rearrange("c p t -> p c t"),
                    in_=xhatT_b.rearrange("p (c t) -> p c t", c=DC)), 16)

    ar.pop()



def emit_p3(p, block):
    nc, b, ar = p.nc, p.b, p.arena
    ar.push()
    NB = HID1 // 512
    Wg_s = [ar.alloc([128, DC * 512], BF16) for _ in range(2)]
    Wu_s = [ar.alloc([128, DC * 512], BF16) for _ in range(2)]
    Wd_s = [ar.alloc([128, 4 * D], BF16) for _ in range(2)]
    xh_t = [ar.alloc([128, D], BF16) for _ in range(2)]
    x3_t = [ar.alloc([128, D], F32) for _ in range(2)]
    g_s = [ar.alloc([128, 512], F32) for _ in range(2)]
    h_f = [ar.alloc([128, 512], F32) for _ in range(2)]
    s_f = [ar.alloc([128, 512], F32) for _ in range(2)]
    pq = [ar.alloc([128, 512], F32) for _ in range(2)]
    m2 = [ar.alloc([128, 256], F32) for _ in range(2)]
    geb = [ar.alloc([128, 512], F32) for _ in range(2)]
    h_sp = [ar.alloc([128, 512], BF16) for _ in range(2)]
    hT_b = [ar.alloc([128, 512], BF16) for _ in range(2)]
    y_t = [ar.alloc([128, D], F32) for _ in range(2)]
    y_tok = [0, 0]
    BG = (0, 6)
    BU = (1, 7)

    def load_band(ib, gate_tok):
        @block.gpsimd
        def _(gp, ib=ib, gate_tok=gate_tok):
            if gate_tok:
                b.wait(gp, "pe", gate_tok)
            sl = ib % 2
            for dk in range(DC):
                b.inc("LW", gp.dma_start(
                    out=Wg_s[sl][:, dk * 512:(dk + 1) * 512],
                    in_=p.g1w[dk * 128:(dk + 1) * 128, ib * 512:(ib + 1) * 512]), 16)
                b.inc("LW", gp.dma_start(
                    out=Wu_s[sl][:, dk * 512:(dk + 1) * 512],
                    in_=p.u1w[dk * 128:(dk + 1) * 128, ib * 512:(ib + 1) * 512]), 16)
            for a in range(4):
                b.inc("LW", gp.dma_start(
                    out=Wd_s[sl][:, a * D:(a + 1) * D],
                    in_=p.d1w[ib * 512 + a * 128: ib * 512 + (a + 1) * 128, :]), 16)
        return b.cnt["LW"]

    tok_w = [0, 0]
    tok_w[0] = load_band(0, 0)
    t_silu = [0, 0]
    t_h = [0, 0]
    t_sq = [0, 0]
    t_sp = [0, 0]
    t_tr = [0, 0]
    t_hT = [0, 0]
    t_dn = [0, 0]
    t_gu = [0, 0]
    t_ye = 0
    t_lh = [0, 0]
    t_lx = [0, 0]
    last_u_tok = 0

    for ib in range(NB):
        sl = ib % 2
        for i in range(NT):
            cur = i % 2
            bg, bu = BG[cur], BU[cur]

            @block.sync
            def _(sp, ib=ib, i=i, cur=cur):
                b.wait(sp, "pe", t_gu[cur])
                b.inc(f"LH{cur}", sp.dma_start(
                    out=xh_t[cur].rearrange("p (c t) -> p c t", c=DC),
                    in_=p.s_xhatT[:, :, i * 128:(i + 1) * 128].rearrange("c p t -> p c t")), 16)
                if ib == 0:
                    b.wait(sp, "ac", t_ye)
                    b.inc(f"LX{cur}", sp.dma_start(
                        out=x3_t[cur], in_=p.s_xA[i * 128:(i + 1) * 128, :]), 16)
            t_lh[cur] = b.cnt[f"LH{cur}"]
            if ib == 0:
                t_lx[cur] = b.cnt[f"LX{cur}"]

            @block.tensor
            def _(pe, cur=cur, sl=sl, bg=bg, bu=bu):
                b.wait(pe, "LW", tok_w[sl])
                b.wait(pe, f"LH{cur}", t_lh[cur])
                b.wait(pe, "ac", t_hT[cur])
                b.wait(pe, "dv", t_h[cur])
                for dk in range(DC):
                    ins = pe.matmul(out=p.pb[bg][:],
                                    lhsT=xh_t[cur][:, dk * 128:(dk + 1) * 128],
                                    rhs=Wg_s[sl][:, dk * 512:(dk + 1) * 512],
                                    start=(dk == 0), stop=(dk == DC - 1))
                b.inc("pe", ins)
                for dk in range(DC):
                    ins = pe.matmul(out=p.pb[bu][:],
                                    lhsT=xh_t[cur][:, dk * 128:(dk + 1) * 128],
                                    rhs=Wu_s[sl][:, dk * 512:(dk + 1) * 512],
                                    start=(dk == 0), stop=(dk == DC - 1))
                b.inc("pe", ins)
            tok_g = b.cnt["pe"] - 1
            t_gu[cur] = last_u_tok = b.cnt["pe"]

            @block.scalar
            def _(ac, cur=cur, bg=bg):
                b.wait(ac, "pe", tok_g)
                b.wait(ac, "dv", t_h[cur])
                b.inc("ac", ac.activation(out=g_s[cur], in_=p.pb[bg][:],
                                          func=ACTF.Silu))
            t_silu[cur] = b.cnt["ac"]

            @block.vector
            def _(dv, cur=cur, bu=bu):
                b.wait(dv, "pe", t_gu[cur])
                b.wait(dv, "ac", t_silu[cur])
                b.wait(dv, "ac", t_sq[cur])
                b.inc("dv", dv.tensor_tensor(out=h_f[cur], in0=g_s[cur],
                                             in1=p.pb[bu][:], op=ALU.mult))
            t_h[cur] = b.cnt["dv"]

            @block.scalar
            def _(ac, cur=cur):
                b.wait(ac, "dv", t_h[cur])
                b.inc("ac", ac.activation(out=s_f[cur], in_=h_f[cur],
                                          func=ACTF.Square))
            t_sq[cur] = b.cnt["ac"]

            @block.vector
            def _(dv, cur=cur):
                b.wait(dv, "ac", t_sq[cur])
                b.wait(dv, "pe", t_tr[cur])
                sf, pqc, m2c, gec = s_f[cur], pq[cur], m2[cur], geb[cur]
                dv.tensor_tensor(out=pqc[:, 0:128], in0=sf[:, 0:512:4],
                                 in1=sf[:, 1:512:4], op=ALU.max)
                dv.tensor_tensor(out=pqc[:, 128:256], in0=sf[:, 0:512:4],
                                 in1=sf[:, 1:512:4], op=ALU.min)
                dv.tensor_tensor(out=pqc[:, 256:384], in0=sf[:, 2:512:4],
                                 in1=sf[:, 3:512:4], op=ALU.max)
                dv.tensor_tensor(out=pqc[:, 384:512], in0=sf[:, 2:512:4],
                                 in1=sf[:, 3:512:4], op=ALU.min)
                dv.drain()
                dv.tensor_tensor(out=m2c[:, 0:128], in0=pqc[:, 0:128],
                                 in1=pqc[:, 256:384], op=ALU.min)
                dv.tensor_tensor(out=m2c[:, 128:256], in0=pqc[:, 128:256],
                                 in1=pqc[:, 384:512], op=ALU.max)
                dv.drain()
                dv.tensor_tensor(out=m2c[:, 128:256], in0=m2c[:, 0:128],
                                 in1=m2c[:, 128:256], op=ALU.max)
                dv.drain()
                for ph in range(4):
                    dv.tensor_tensor(out=gec[:, ph * 128:(ph + 1) * 128],
                                     in0=sf[:, ph:512:4],
                                     in1=m2c[:, 128:256], op=ALU.is_ge)
                dv.drain()
                for ph in range(4):
                    ins = dv.tensor_tensor(out=h_sp[cur][:, ph:512:4],
                                           in0=gec[:, ph * 128:(ph + 1) * 128],
                                           in1=h_f[cur][:, ph:512:4], op=ALU.mult)
                b.inc("dv", ins)
            t_sp[cur] = b.cnt["dv"]

            @block.tensor
            def _(pe, cur=cur, bg=bg):
                b.wait(pe, "dv", t_sp[cur])
                b.wait(pe, "ac", t_silu[cur])
                b.wait(pe, "ac", t_hT[cur])
                for jj in range(4):
                    ins = pe.transpose(out=p.pbb[bg][:, jj * 128:(jj + 1) * 128],
                                       in_=h_sp[cur][:, jj * 128:(jj + 1) * 128],
                                       identity=p.id_b)
                b.inc("pe", ins)
            t_tr[cur] = b.cnt["pe"]

            @block.scalar
            def _(ac, cur=cur, bg=bg):
                b.wait(ac, "pe", t_tr[cur])
                b.wait(ac, "pe", t_dn[cur])
                b.inc("ac", ac.activation(out=hT_b[cur], in_=p.pbb[bg][:, 0:512],
                                          func=ACTF.Copy))
            t_hT[cur] = b.cnt["ac"]

            @block.tensor
            def _(pe, cur=cur, sl=sl):
                b.wait(pe, "ac", t_hT[cur])
                b.wait(pe, "ac", t_ye)
                for dcol in range(4):
                    for hc in range(4):
                        ins = pe.matmul(
                            out=p.pb[2 + dcol][:],
                            lhsT=hT_b[cur][:, hc * 128:(hc + 1) * 128],
                            rhs=Wd_s[sl][:, hc * D + dcol * 512: hc * D + (dcol + 1) * 512],
                            start=(hc == 0), stop=(hc == 3))
                    b.inc("pe", ins)
            t_dn[cur] = b.cnt["pe"]

            @block.scalar
            def _(ac, ib=ib, cur=cur):
                b.wait(ac, "pe", t_dn[cur])
                b.wait(ac, f"SY{cur}", y_tok[cur])
                if ib == 0:
                    b.wait(ac, f"LX{cur}", t_lx[cur])
                for dcol in range(4):
                    ins = ac.activation(out=y_t[cur][:, dcol * 512:(dcol + 1) * 512],
                                        in_=p.pb[2 + dcol][:], func=ACTF.Copy)
                b.inc("ac", ins)
            t_ye = b.cnt["ac"]

            if ib == 0:
                @block.vector
                def _(dv, cur=cur):
                    b.wait(dv, "ac", t_ye)
                    for dcol in range(4):
                        ins = dv.tensor_tensor(
                            out=y_t[cur][:, dcol * 512:(dcol + 1) * 512],
                            in0=y_t[cur][:, dcol * 512:(dcol + 1) * 512],
                            in1=x3_t[cur][:, dcol * 512:(dcol + 1) * 512],
                            op=ALU.add)
                    b.inc("dv", ins)
                tok_resid = b.cnt["dv"]

            @block.gpsimd
            def _(gp, ib=ib, i=i, cur=cur):
                if ib == 0:
                    b.wait(gp, "dv", tok_resid)
                else:
                    b.wait(gp, "ac", t_ye)
                ch = f"SY{cur}"
                if b.cnt.get(ch, 0) >= 48:
                    b.wait(gp, ch, b.cnt[ch] - 32)
                if ib == 0:
                    ins = gp.dma_start(out=p.y[i * 128:(i + 1) * 128, :],
                                       in_=y_t[cur])
                else:
                    ins = gp.dma_start(out=p.y[i * 128:(i + 1) * 128, :],
                                       in_=y_t[cur], accum_op=ALU.add)
                b.inc(ch, ins, 16)
                y_tok[cur] = b.cnt[ch]

            if i == 4 and ib + 1 < NB:
                tok_w[(ib + 1) % 2] = load_band(ib + 1, last_u_tok)

    ar.pop()


def preload_set6(ac):
    ac.add_instruction(mybir.InstLoadActFuncSet(
        name=ac.bass.get_next_instruction_name(),
        act_func_set_id=6, ins=[], outs=[]))


def instr_last(eng):
    return eng.nop()



def build(phases=("p0a",), taps=None):
    nc = bass.Bass("TRN2", target_bir_lowering=False)
    es = contextlib.ExitStack()
    p = Prog(nc, es, taps)
    with nc.Block() as block:
        emit_prologue(p, block)
        if "p0a" in phases:
            @block.scalar
            def _(ac):
                ac.preload_act_table(ACTF.Sqrt)
            emit_p0a(p, block)
            p.b.barrier(block)
        if "p0b" in phases:
            @block.scalar
            def _(ac):
                preload_set6(ac)
            emit_p0b(p, block)
            p.b.barrier(block)
        if "p1" in phases:
            @block.scalar
            def _(ac):
                ac.preload_act_table(ACTF.Silu)
            emit_p1(p, block)
            p.b.barrier(block)
        if "p2" in phases:
            @block.scalar
            def _(ac):
                ac.preload_act_table(ACTF.Sqrt)
            emit_p2p(p, block)
            p.b.barrier(block)
            emit_p2a(p, block)
            p.b.barrier(block)
            @block.scalar
            def _(ac):
                preload_set6(ac)
            emit_p2b(p, block)
            p.b.barrier(block)
        if "p3" in phases:
            @block.scalar
            def _(ac):
                ac.preload_act_table(ACTF.Silu)
            emit_p3(p, block)
            p.b.barrier(block)
        for sec in (block.tensor, block.vector, block.scalar, block.gpsimd, block.sync):
            @sec
            def _(eng):
                for ch in sorted(p.b.cnt):
                    p.b.wait(eng, ch)
    es.close()
    return nc, p


ALL_PHASES = ("p0a", "p0b", "p1", "p2", "p3")


def kernel(**inputs):
    consts = host_consts()
    nc, _ = build(phases=ALL_PHASES)
    in_maps = []
    for core in range(8):
        d = host_prepare(inputs, core)
        d.update(consts)
        in_maps.append(d)
    res = run_bass_kernel_spmd(nc, in_maps, core_ids=list(range(8)))
    out = np.stack([np.asarray(r["y"], dtype=np.float32) for r in res.results], 0)
    return out
